# version 11
# baseline (speedup 1.0000x reference)
import numpy as np
import concourse.bass as bass
import concourse.mybir as mybir
from concourse.bass_utils import run_bass_kernel_spmd

F32 = mybir.dt.float32
BF16 = mybir.dt.bfloat16
AF = mybir.ActivationFunctionType
ALU = mybir.AluOpType
AX = mybir.AxisListType

MAGIC = float(np.float32(1.5 * 2 ** 23))
INV_Q16 = float(np.float32(1.0 / 32767.0))
INV_QW = float(np.float32(1.0 / 7.0))
INV_2P16 = float(np.float32(1.0 / 65536.0))

T = 2048
D = 1024
TCH = 16
FCH = 8
II = 3072
ICH = 24
NPAIR = 8
N_KEEP = 18

_DBG = set()
_STOP_AFTER = None


class _Sem:
    def __init__(self, nc, name):
        self.h = nc.semaphore(name).__enter__()
        self.v = 0

    def inc(self, n=1):
        self.v += n
        return self.v


def build_program():
    nc = bass.Bass("TRN2", target_bir_lowering=False, debug=False,
                   detect_race_conditions=False)

    x_d = nc.dram_tensor("x", [T, D], F32, kind="ExternalInput").ap()
    ident_d = nc.dram_tensor("ident", [128, 128], F32, kind="ExternalInput").ap()
    w_d = {nm: nc.dram_tensor(nm, [D, D], F32, kind="ExternalInput").ap()
           for nm in ("wq", "wk", "wv", "wg", "wo")}
    b_d = {nm: nc.dram_tensor(nm, [1, D], F32, kind="ExternalInput").ap()
           for nm in ("bq", "bk", "bv", "bg", "bo")}
    wup_d = nc.dram_tensor("wup", [II, D], F32, kind="ExternalInput").ap()
    wgt_d = nc.dram_tensor("wgt", [II, D], F32, kind="ExternalInput").ap()
    wdn_d = nc.dram_tensor("wdn", [D, II], F32, kind="ExternalInput").ap()
    y_d = nc.dram_tensor("y", [T, D], F32, kind="ExternalOutput").ap()

    wupT_h = nc.dram_tensor("wupT_h", [FCH, 128, II], BF16).ap()
    wgtT_h = nc.dram_tensor("wgtT_h", [FCH, 128, II], BF16).ap()
    wdnT_h = nc.dram_tensor("wdnT_h", [ICH, 128, D], BF16).ap()
    up_h = nc.dram_tensor("up_h", [ICH, 128, T], F32).ap()
    gate_h = nc.dram_tensor("gate_h", [ICH, 128, T], F32).ap()
    g2_h = nc.dram_tensor("g2_h", [ICH, 128, T], F32).ap()
    op_h = nc.dram_tensor("op_h", [ICH, 128, T], F32).ap()
    xln_h = nc.dram_tensor("xln_h", [TCH, 128, D], F32).ap()
    cc_in = [nc.dram_tensor(f"cc_in{k}", [2, 8], F32).ap() for k in range(4)]
    cc_out = [nc.dram_tensor(f"cc_out{k}", [16, 8], F32).ap() for k in range(4)]

    dbg_d = {}

    def dbg_out(name, shape, dtype=F32):
        dbg_d[name] = nc.dram_tensor("dbg_" + name, shape, dtype,
                                     kind="ExternalOutput").ap()
        return dbg_d[name]

    live = {}

    def alloc(name, shape, dtype, side="left"):
        cm = nc.sbuf_tensor(name, shape, dtype, side=side)
        t = cm.__enter__()
        live[name] = cm
        return t

    def free(name):
        live.pop(name).__exit__(None, None, None)

    ident = alloc("identt", [128, 128], F32)
    ones_f = alloc("ones_f", [1, 128], F32)
    zbuf = alloc("zbuf", [128, 16], F32)
    zs32 = alloc("zs32", [128, 32], F32)
    ccol = alloc("ccol", [128, 3], F32)
    ident_bf = alloc("ident_bf", [128, 128], BF16)
    zrow = alloc("zrow", [2, 8], F32)
    zsum = alloc("zsum", [128, 16], F32)
    bn6 = alloc("bn6", [128, TCH, 12], F32)
    mvall = alloc("mvall", [128, TCH, 2], F32)
    sdall = alloc("sdall", [128, TCH], F32)
    rstd = alloc("rstd", [128, TCH], F32)
    nmr = alloc("nmr", [128, TCH], F32)
    am24 = alloc("am24", [128, ICH], F32)
    am24b = alloc("am24b", [128, ICH], F32)
    am1 = alloc("am1", [128, 2], F32)
    lmax = alloc("lmax", [2, 1], F32)
    gth = alloc("gth", [1, 16], F32)
    smax = alloc("smax", [1, 12], F32)
    bc = alloc("bc", [128, 8], F32)
    nmcol = alloc("nmcol", [128, 4], F32)
    mvw = alloc("mvw", [128, 4, 2], F32)
    sdw = alloc("sdw", [128, 4], F32)
    rstdw = alloc("rstdw", [128, 4], F32)
    nmrw = alloc("nmrw", [128, 4], F32)
    PW = 512
    DQNB = PW // 32
    dq_in = alloc("dq_in", [128, 4, PW], F32, side="right")
    dq_bm = alloc("dq_bm", [128, 4, DQNB], F32, side="right")
    dq_s = alloc("dq_s", [128, 4, DQNB], F32, side="right")
    dq_si = alloc("dq_si", [128, 4, DQNB], F32, side="right")
    dq_bf = alloc("dq_bf", [128, 4, PW], BF16, side="right")
    dq_tr = alloc("dq_tr", [128, 4, PW // 128, 128], BF16, side="right")
    wslotA = alloc("wslotA", [128, FCH, D], BF16, side="right")
    ones_bf = alloc("ones_bf", [1, T], BF16, side="right")
    ones_col = alloc("ones_col", [128, 1], BF16, side="right")
    bias_sb = alloc("bias_sb", [1, 5, D], BF16, side="right")
    kvaug = alloc("kvaug", [128, NPAIR, 128], BF16, side="right")
    zk = alloc("zk", [128, 32], BF16, side="right")
    fs_a = alloc("fs_a", [128, 1024], F32, side="right")
    fs_b = alloc("fs_b", [128, 1024], F32, side="right")
    wslotB = alloc("wslotB", [128, FCH, D], BF16, side="right")

    ps = [nc.psum_tensor(f"ps{i}", [128, 512], F32).__enter__() for i in range(8)]

    s_ld = _Sem(nc, "s_ld")
    s_tr = _Sem(nc, "s_tr")
    s_gd = _Sem(nc, "s_gd")
    s_v = _Sem(nc, "s_v")
    s_s = _Sem(nc, "s_s")
    s_t = _Sem(nc, "s_t")
    s_g = _Sem(nc, "s_g")
    s_cc = _Sem(nc, "s_cc")
    s_out = _Sem(nc, "s_out")
    s_id = _Sem(nc, "s_id")
    s_gx = _Sem(nc, "s_gx")
    s_gb = _Sem(nc, "s_gb")
    s_gwA = _Sem(nc, "s_gwA")
    s_gwB = _Sem(nc, "s_gwB")
    s_dqld = [_Sem(nc, f"s_dqld{i}") for i in range(4)]
    s_dqtr = _Sem(nc, "s_dqtr")
    s_dqtr2 = [_Sem(nc, "s_dqtrA"), _Sem(nc, "s_dqtrB")]
    s_dqst = {"up": _Sem(nc, "s_dqstU"), "gt": _Sem(nc, "s_dqstG"),
              "dn": _Sem(nc, "s_dqstD")}
    s_x2 = _Sem(nc, "s_x2")
    s_wup = _Sem(nc, "s_wup")
    s_wgt = _Sem(nc, "s_wgt")
    s_wdn = _Sem(nc, "s_wdn")
    s_f1sp = {("u", 0): _Sem(nc, "s_f1spU0"), ("u", 1): _Sem(nc, "s_f1spU1"),
              ("g", 0): _Sem(nc, "s_f1spG0"), ("g", 1): _Sem(nc, "s_f1spG1")}
    s_gr = [_Sem(nc, "s_gr0"), _Sem(nc, "s_gr1"), _Sem(nc, "s_gr2")]
    s_g2st = _Sem(nc, "s_g2st")
    s_ur = [_Sem(nc, "s_ur0"), _Sem(nc, "s_ur1"), _Sem(nc, "s_ur2")]
    s_g2r = [_Sem(nc, "s_g2r0"), _Sem(nc, "s_g2r1")]
    s_opst = _Sem(nc, "s_opst")
    s_opr = [_Sem(nc, "s_opr0"), _Sem(nc, "s_opr1")]
    s_xl = [_Sem(nc, "s_xl0"), _Sem(nc, "s_xl1")]
    s_wd = [_Sem(nc, "s_wd0"), _Sem(nc, "s_wd1")]
    s_dv = _Sem(nc, "s_dv")
    s_xsp = _Sem(nc, "s_xsp")
    all_dma_sems = ([s_ld, s_tr, s_gd, s_id, s_gx, s_gb, s_gwA, s_gwB, s_dqtr,
                     s_x2, s_wup, s_wgt, s_wdn, s_g2st, s_opst, s_xsp]
                    + s_dqld + list(s_dqst.values()) + list(s_f1sp.values())
                    + s_gr + s_ur + s_g2r + s_opr + s_xl + s_wd + s_dqtr2)

    semmap = {"v": s_v, "s": s_s, "g": s_g, "t": s_t, "ld": s_ld, "tr": s_tr}
    bank_free = [("v", 0)] * 8

    def bank_wait(eng, i):
        nm, val = bank_free[i]
        if val > 0:
            eng.wait_ge(semmap[nm].h, val)

    def fence(eng, sem, inst):
        inst.then_inc(sem.h, 1)
        eng.wait_ge(sem.h, sem.inc(1))

    def pe_transpose(dst_fm, src_tm, banks, pre_twait=None, idx_wait=None):
        base_t = s_t.v
        base_v = s_v.v
        nblk = TCH * FCH

        @block.tensor
        def _(t):
            if pre_twait is not None:
                pre_twait(t)
            for pb in banks:
                bank_wait(t, pb)
            for idx in range(nblk):
                c, f = divmod(idx, FCH)
                pb = banks[idx % 2]
                if idx_wait is not None:
                    idx_wait(t, c, f)
                if idx >= 2:
                    t.wait_ge(s_v.h, base_v + idx - 1)
                t.transpose(ps[pb][:, 0:64].bitcast(BF16),
                            src_tm[:, c, f * 128:(f + 1) * 128],
                            ident_bf[:, :]).then_inc(s_t.h, 1)
                s_t.inc(1)

        @block.vector
        def _(v):
            for idx in range(nblk):
                c, f = divmod(idx, FCH)
                pb = banks[idx % 2]
                v.wait_ge(s_t.h, base_t + idx + 1)
                v.tensor_copy(dst_fm[:, f, c * 128:(c + 1) * 128],
                              ps[pb][:, 0:64].bitcast(BF16)).then_inc(s_v.h, 1)
                s_v.inc(1)
        done = s_v.v
        for pb in banks:
            bank_free[pb] = ("v", done)
        return done

    blk_cm = nc.Block()
    block = blk_cm.__enter__()

    def finish(dump_emitter=None):
        if dump_emitter is not None:
            dump_emitter()
        blk_cm.__exit__(None, None, None)
        return nc, dbg_d

    def dump_tm(tag, buf_ap3, nch, w, dtype=F32):
        o = dbg_out(tag, [nch * 128, w], dtype)
        waits = [(s_v, s_v.v), (s_s, s_s.v), (s_t, s_t.v), (s_g, s_g.v)]
        waits += [(sm, sm.v) for sm in all_dma_sems]

        @block.gpsimd
        def _(g):
            for sm, val in waits:
                if val:
                    g.wait_ge(sm.h, val)
            g.dma_start(out=o.rearrange("(c p) w -> p c w", p=128),
                        in_=buf_ap3).then_inc(s_out.h, 16)
            g.wait_ge(s_out.h, s_out.inc(16))

    xbf_fm = alloc("xbf_fm", [128, FCH, T], BF16)
    xbf_tm = alloc("xbf_tm", [128, TCH, D], BF16, side="right")

    @block.sync
    def _(sync):
        sync.dma_start(out=ident[:], in_=ident_d[:, :]).then_inc(s_id.h, 16)
    ld_ident = s_id.inc(16)

    @block.gpsimd
    def _(g):
        g.dma_start(out=xbf_tm[:, :, :],
                    in_=x_d.rearrange("(t p) d -> p t d", p=128)).then_inc(s_gx.h, 16)
        for i, nm in enumerate(("bq", "bk", "bv", "bg", "bo")):
            g.dma_start(out=bias_sb[0:1, i, :], in_=b_d[nm][:, :]).then_inc(s_gb.h, 16)
    gd_x = s_gx.inc(16)
    gd_bias = s_gb.inc(16 * 5)

    wslot = [wslotA, wslotB]
    wname2slot = {"wk": 0, "wv": 1, "wg": 0, "wq": 1, "wo": 0}
    gd_w = {}
    w_guard = {}

    wsem = {0: s_gwA, 1: s_gwB}

    def emit_wload(nm):
        slot_sem = wsem[wname2slot[nm]]

        @block.gpsimd
        def _(g):
            gd = w_guard.get(nm)
            if gd is not None:
                g.wait_ge(semmap[gd[0]].h, gd[1])
            g.dma_start(out=wslot[wname2slot[nm]][:, :, :],
                        in_=w_d[nm].rearrange("(k p) d -> p k d", p=128)).then_inc(slot_sem.h, 16)
        gd_w[nm] = (slot_sem, slot_sem.inc(16))

    emit_wload("wk")
    emit_wload("wv")

    @block.vector
    def _(v):
        v.memset(ones_bf[:], 1.0)
        v.memset(ones_f[:], 1.0)
        v.memset(ones_col[:], 1.0)
        v.memset(zk[:], 0.0)
        v.memset(kvaug[:, :, :], 0.0)
        v.memset(ccol[:, 0:1], MAGIC)
        v.memset(ccol[:, 1:2], 1e-5)
        v.memset(ccol[:, 2:3], 65536.0)
        v.memset(zrow[:, :], 0.0)
        v.wait_ge(s_id.h, ld_ident)
        v.tensor_copy(ident_bf[:, :], ident[:, :])
        v.engine_nop().then_inc(s_v.h, 1)
    v_const = s_v.inc(1)

    @block.sync
    def _(sync):
        sync.wait_ge(s_v.h, v_const)
        for k in range(4):
            sync.dma_start(out=cc_in[k][:, :], in_=zrow[:, :]).then_inc(s_id.h, 16)
    ld_cc0 = s_id.inc(16 * 4)

    tr_xbf = pe_transpose(xbf_fm, xbf_tm, (0, 1),
                          pre_twait=lambda t: (t.wait_ge(s_gx.h, gd_x),
                                               t.wait_ge(s_v.h, v_const)))
    free("xbf_tm")

    if _STOP_AFTER == "xbf":
        return finish(lambda: dump_tm("xbf", xbf_fm[:, :, :], FCH, T, BF16))

    dq_state = {}
    dq_qA = []
    dq_qB = []
    dq_qC = []
    dq_last = {}
    dq_total = [0]

    def make_dq_pieces(tag, w_src, n_och, cl, out_h):
        npiece = cl // PW
        ntr = PW // 128

        def stageA(c, pc, i):
            b = i % 4

            @block.gpsimd
            def _(g, c=c, pc=pc, b=b, i=i):
                if i >= 4:
                    g.wait_ge(s_g.h, dq_state[("gdq", i - 4)])
                g.dma_start(out=dq_in[:, b, :],
                            in_=w_src[c * 128:(c + 1) * 128,
                                      pc * PW:(pc + 1) * PW]).then_inc(s_dqld[b].h, 16)
                dq_state[("ld", i)] = s_dqld[b].inc(16)

        def stageB(c, pc, i):
            b = i % 4

            @block.vector
            def _(v, b=b, i=i):
                v.wait_ge(s_dqld[b].h, dq_state[("ld", i)])
                v.tensor_reduce(dq_bm[:, b, :],
                                dq_in[:, b, :].rearrange("p (n q) -> p n q", q=32),
                                AX.X, ALU.max,
                                apply_absolute_value=True).then_inc(s_dv.h, 1)
                if i >= 1:
                    b1 = (i - 1) % 4
                    v.wait_ge(s_dv.h, dq_state[("red", i - 1)])
                    v.tensor_scalar(dq_s[:, b1, :], dq_bm[:, b1, :], INV_QW,
                                    1e-12, ALU.mult,
                                    ALU.max).then_inc(s_dv.h, 1)
                if i >= 2:
                    b2 = (i - 2) % 4
                    v.wait_ge(s_dv.h, dq_state[("tsd", i - 2)])
                    v.reciprocal(dq_si[:, b2, :], dq_s[:, b2, :])
                    v.engine_nop().then_inc(s_dv.h, 1)
            dq_state[("red", i)] = s_dv.inc(1)
            if i >= 1:
                dq_state[("tsd", i - 1)] = s_dv.inc(1)
            if i >= 2:
                dq_state[("vsc", i - 2)] = s_dv.inc(1)

            if i >= 2:
                j = i - 2

                @block.gpsimd
                def _(g, j=j):
                    g.wait_ge(s_dv.h, dq_state[("vsc", j)])
                    if j >= 4:
                        g.wait_ge(s_dqtr2[j % 2].h, dq_state[("tr", j - 4)])
                    bj = j % 4
                    sib = dq_si[:, bj, :].unsqueeze(2).broadcast_to([128, DQNB, 32])
                    w3 = dq_in[:, bj, :].rearrange("p (n q) -> p n q", q=32)
                    g.tensor_tensor(w3, w3, sib, ALU.mult).then_inc(s_g.h, 1)
                dq_state[("gmul", j)] = s_g.inc(1)

        def stageC(c, pc, i):
            b = i % 4

            @block.vector
            def _(v, b=b, i=i):
                v.wait_ge(s_g.h, dq_state[("gmul", i)])
                v.tensor_scalar(dq_in[:, b, :], dq_in[:, b, :], MAGIC, MAGIC,
                                ALU.add, ALU.subtract).then_inc(s_dv.h, 1)
            dq_state[("vrnd", i)] = s_dv.inc(1)

            @block.gpsimd
            def _(g, b=b, i=i):
                g.wait_ge(s_dv.h, dq_state[("vrnd", i)])
                sb_ = dq_s[:, b, :].unsqueeze(2).broadcast_to([128, DQNB, 32])
                w3 = dq_in[:, b, :].rearrange("p (n q) -> p n q", q=32)
                bf3 = dq_bf[:, b, :].rearrange("p (n q) -> p n q", q=32)
                g.tensor_tensor(bf3, w3, sb_, ALU.mult)
                g.engine_nop().then_inc(s_g.h, 1)
            dq_state[("gdq", i)] = s_g.inc(1)

            if b % 2 == 0:
                @block.sync
                def _(q, b=b, i=i):
                    q.wait_ge(s_g.h, dq_state[("gdq", i)])
                    for f in range(ntr):
                        q.dma_start_transpose(
                            dq_tr[:, b, f, :],
                            dq_bf[:, b, f * 128:(f + 1) * 128]).then_inc(s_dqtr2[b % 2].h, 16)
            else:
                @block.sync
                def _(q, b=b, i=i):
                    q.wait_ge(s_g.h, dq_state[("gdq", i)])
                    for f in range(ntr):
                        q.dma_start_transpose(
                            dq_tr[:, b, f, :],
                            dq_bf[:, b, f * 128:(f + 1) * 128]).then_inc(s_dqtr2[b % 2].h, 16)
            dq_state[("tr", i)] = s_dqtr2[b % 2].inc(16 * ntr)

            @block.gpsimd
            def _(g, c=c, pc=pc, b=b, i=i):
                g.wait_ge(s_dqtr2[b % 2].h, dq_state[("tr", i)])
                g.dma_start(
                    out=out_h[pc * ntr:(pc + 1) * ntr, :,
                              c * 128:(c + 1) * 128].transpose([1, 0, 2]),
                    in_=dq_tr[:, b, 0:ntr, :]).then_inc(s_dqst[tag].h, 16)
            dq_last[tag] = s_dqst[tag].inc(16)

        for c in range(n_och):
            for pc in range(npiece):
                i = dq_total[0]
                dq_qA.append((tag, stageA, c, pc, i))
                dq_qB.append((tag, stageB, c, pc, i))
                dq_qC.append((tag, stageC, c, pc, i))
                dq_total[0] += 1

    def drain_B():
        N = dq_total[0]
        for vi in (N, N + 1):
            @block.vector
            def _(v, vi=vi, N=N):
                if vi - 1 < N:
                    b1 = (vi - 1) % 4
                    v.wait_ge(s_dv.h, dq_state[("red", vi - 1)])
                    v.tensor_scalar(dq_s[:, b1, :], dq_bm[:, b1, :], INV_QW,
                                    1e-12, ALU.mult,
                                    ALU.max).then_inc(s_dv.h, 1)
                b2 = (vi - 2) % 4
                v.wait_ge(s_dv.h, dq_state[("tsd", vi - 2)])
                v.reciprocal(dq_si[:, b2, :], dq_s[:, b2, :])
                v.engine_nop().then_inc(s_dv.h, 1)
            if vi - 1 < N:
                dq_state[("tsd", vi - 1)] = s_dv.inc(1)
            dq_state[("vsc", vi - 2)] = s_dv.inc(1)

            j = vi - 2

            @block.gpsimd
            def _(g, j=j):
                g.wait_ge(s_dv.h, dq_state[("vsc", j)])
                if j >= 4:
                    g.wait_ge(s_dqtr2[j % 2].h, dq_state[("tr", j - 4)])
                bj = j % 4
                sib = dq_si[:, bj, :].unsqueeze(2).broadcast_to([128, DQNB, 32])
                w3 = dq_in[:, bj, :].rearrange("p (n q) -> p n q", q=32)
                g.tensor_tensor(w3, w3, sib, ALU.mult).then_inc(s_g.h, 1)
            dq_state[("gmul", j)] = s_g.inc(1)

    dq_drained = [False]

    def pump_dq(k):
        for _ in range(k):
            total0 = dq_total[0]
            cA = total0 - len(dq_qA)
            cB = total0 - len(dq_qB)
            cC = total0 - len(dq_qC)
            if dq_qA and cA <= cC + 3:
                tg, fn, c, pc, i = dq_qA.pop(0)
                fn(c, pc, i)
                cA += 1
            if dq_qB and cB < cA and cB <= cC + 3:
                tg, fn, c, pc, i = dq_qB.pop(0)
                fn(c, pc, i)
                cB += 1
                if not dq_qB and not dq_drained[0]:
                    drain_B()
                    dq_drained[0] = True
            if dq_qC and (cC <= cB - 3 or (dq_drained[0] and cC < cB)):
                tg, fn, c, pc, i = dq_qC.pop(0)
                fn(c, pc, i)

    def pump_dq_tag(tag):
        while ((dq_qA and dq_qA[0][0] == tag) or (dq_qB and dq_qB[0][0] == tag)
               or (dq_qC and dq_qC[0][0] == tag)):
            pump_dq(1)

    k_tm = alloc("k_tm", [128, TCH, D], BF16)
    v_tm = alloc("v_tm", [128, TCH, D], BF16)
    tdone = {}

    def run_tm_proj(wname, bidx, nbank, kind, dst):
        bi = "qkvgo".index(wname[1])
        slot = wslot[wname2slot[wname]]
        t0, v0, s0 = s_t.v, s_v.v, s_s.v
        ntile = TCH * 2

        def tdn(j):
            return t0 + j + 1

        if kind == "fm":
            def vmin(j): return v0 + 4 * (j // 2) + (j % 2) + 1
            def stt(j): return v0 + 4 * (j // 2) + (j % 2) + 3
            def sexp(j): return s0 + j + 1
            rel = ("v", stt)
        else:
            def scp(j): return s0 + j + 1
            rel = ("s", scp)

        GT = 8
        for g0 in range(0, ntile, GT):
            @block.tensor
            def _(t, g0=g0):
                if g0 == 0:
                    t.wait_ge(s_v.h, tr_xbf)
                    t.wait_ge(gd_w[wname][0].h, gd_w[wname][1])
                    t.wait_ge(s_gb.h, gd_bias)
                    t.wait_ge(s_v.h, v_const)
                for j in range(g0, g0 + GT):
                    m, n = divmod(j, 2)
                    pi = bidx + j % nbank
                    if j < nbank:
                        bank_wait(t, pi)
                    else:
                        t.wait_ge(semmap[rel[0]].h, rel[1](j - nbank))
                    for kc in range(FCH):
                        t.matmul(ps[pi][:, :],
                                 xbf_fm[:, kc, m * 128:(m + 1) * 128],
                                 slot[:, kc, n * 512:(n + 1) * 512],
                                 start=(kc == 0), stop=False)
                    t.matmul(ps[pi][:, :],
                             ones_bf[0:1, 0:128],
                             bias_sb[0:1, bi, n * 512:(n + 1) * 512],
                             start=False, stop=True).then_inc(s_t.h, 1)
                    s_t.inc(1)
                    tdone[(wname, m, n)] = s_t.v

            if kind == "fm":
                @block.vector
                def _(v, g0=g0):
                    for j0 in range(g0, g0 + GT, 2):
                        for j in (j0, j0 + 1):
                            pi = bidx + j % nbank
                            v.wait_ge(s_t.h, tdn(j))
                            v.tensor_scalar(fs_a[:, (j % 2) * 512:(j % 2) * 512 + 512], ps[pi][:, :],
                                            0.0, None, ALU.min)
                            v.engine_nop().then_inc(s_v.h, 1)
                            s_v.inc(1)
                        for j in (j0, j0 + 1):
                            m, n = divmod(j, 2)
                            pi = bidx + j % nbank
                            v.wait_ge(s_s.h, sexp(j))
                            v.scalar_tensor_tensor(dst[:, m, n * 512:(n + 1) * 512],
                                                   ps[pi][:, :], 0.0, fs_b[:, (j % 2) * 512:(j % 2) * 512 + 512],
                                                   ALU.max, ALU.add).then_inc(s_v.h, 1)
                            s_v.inc(1)

                @block.scalar
                def _(s, g0=g0):
                    for j in range(g0, g0 + GT):
                        s.wait_ge(s_v.h, vmin(j))
                        s.activation(out=fs_b[:, (j % 2) * 512:(j % 2) * 512 + 512], in_=fs_a[:, (j % 2) * 512:(j % 2) * 512 + 512],
                                     func=AF.Exp).then_inc(s_s.h, 1)
                        s_s.inc(1)
            else:
                @block.scalar
                def _(s, g0=g0):
                    for j in range(g0, g0 + GT):
                        m, n = divmod(j, 2)
                        pi = bidx + j % nbank
                        s.wait_ge(s_t.h, tdn(j))
                        s.activation(out=dst[:, m, n * 512:(n + 1) * 512],
                                     in_=ps[pi][:, :], func=kind).then_inc(s_s.h, 1)
                        s_s.inc(1)
            pump_dq(6)

        for j in range(ntile - nbank, ntile):
            bank_free[bidx + j % nbank] = (rel[0], rel[1](j))

    make_dq_pieces("up", wup_d, ICH, D, wupT_h)
    make_dq_pieces("gt", wgt_d, ICH, D, wgtT_h)
    make_dq_pieces("dn", wdn_d, FCH, II, wdnT_h)

    run_tm_proj("wk", 0, 8, "fm", k_tm)
    run_tm_proj("wv", 0, 8, AF.Identity, v_tm)
    v_kevac = s_v.v
    s_vevac = s_s.v

    w_guard["wg"] = ("t", tdone[("wk", TCH - 1, 1)])
    w_guard["wq"] = ("t", tdone[("wv", TCH - 1, 1)])
    emit_wload("wg")
    emit_wload("wq")

    if _STOP_AFTER == "ktm":
        return finish(lambda: (dump_tm("ktm", k_tm[:, :, :], TCH, D, BF16),
                               dump_tm("vtm", v_tm[:, :, :], TCH, D, BF16)))

    @block.tensor
    def _(t):
        t.wait_ge(s_v.h, v_kevac)
        t.wait_ge(s_s.h, s_vevac)
        for p in range(NPAIR):
            bank_wait(t, p)
        for p in range(NPAIR):
            for c in range(TCH):
                t.matmul(ps[p][:, 0:128],
                         k_tm[:, c, p * 128:(p + 1) * 128],
                         v_tm[:, c, p * 128:(p + 1) * 128],
                         start=(c == 0), stop=(c == TCH - 1),
                         skip_group_check=True)
            for c in range(TCH):
                mm = t.matmul(ps[p][:, 128:129],
                              k_tm[:, c, p * 128:(p + 1) * 128],
                              ones_col[:, :],
                              start=(c == 0), stop=(c == TCH - 1),
                              skip_group_check=True)
        mm.then_inc(s_t.h, 1)
    t_kv = s_t.inc(1)

    @block.vector
    def _(v):
        v.wait_ge(s_t.h, t_kv)
        for p in range(NPAIR):
            v.tensor_copy(kvaug[0:64, p, 0:64], ps[p][0:64, 0:64])
            v.tensor_copy(kvaug[64:128, p, 64:128], ps[p][64:128, 64:128])
            v.tensor_copy(zk[0:64, 4 * p:4 * p + 1], ps[p][0:64, 128:129])
            last = v.tensor_copy(zk[64:128, 4 * p + 2:4 * p + 3], ps[p][64:128, 128:129])
        fence(v, s_v, last)
        for p in range(NPAIR):
            v.tensor_tensor(zk[0:64, 4 * p + 1:4 * p + 2], ps[p][0:64, 128:129],
                            zk[0:64, 4 * p:4 * p + 1], ALU.subtract)
            v.tensor_tensor(zk[64:128, 4 * p + 3:4 * p + 4], ps[p][64:128, 128:129],
                            zk[64:128, 4 * p + 2:4 * p + 3], ALU.subtract)
        v.engine_nop().then_inc(s_v.h, 1)
    v_kv = s_v.inc(1)
    for i in range(8):
        bank_free[i] = ("v", v_kv)

    free("v_tm")
    pump_dq(3)

    if _STOP_AFTER == "kv":
        def d():
            o = dbg_out("kvaug", [128, NPAIR * 128], BF16)
            o2 = dbg_out("zkd", [128, 32], BF16)

            @block.gpsimd
            def _(g):
                g.wait_ge(s_v.h, v_kv)
                g.dma_start(out=o[:, :], in_=kvaug[:, :, :]).then_inc(s_out.h, 16)
                g.dma_start(out=o2[:, :], in_=zk[:, :]).then_inc(s_out.h, 16)
                g.wait_ge(s_out.h, s_out.inc(32))
        return finish(d)

    gate_tm = alloc("gate_tm", [128, TCH, D], BF16)
    run_tm_proj("wg", 0, 8, AF.Sigmoid, gate_tm)

    w_guard["wo"] = ("t", tdone[("wg", TCH - 1, 1)])
    emit_wload("wo")

    q_fm = alloc("q_fm", [128, FCH, T], BF16)

    _qt0, _qv0, _qs0 = s_t.v, s_v.v, s_s.v

    for g0 in range(0, 32, 8):
        @block.tensor
        def _(t, g0=g0):
            if g0 == 0:
                t.wait_ge(gd_w['wq'][0].h, gd_w['wq'][1])
            for j in range(g0, g0 + 8):
                m, n = divmod(j, 4)
                pi = j % 8
                if j < 8:
                    bank_wait(t, pi)
                else:
                    jj = j - 8
                    t.wait_ge(s_v.h, _qv0 + 4 * (jj // 2) + (jj % 2) + 3)
                for kc in range(FCH):
                    t.matmul(ps[pi][:, :],
                             wslot[1][:, kc, m * 128:(m + 1) * 128],
                             xbf_fm[:, kc, n * 512:(n + 1) * 512],
                             start=(kc == 0), stop=False)
                t.matmul(ps[pi][:, :],
                         bias_sb[0:1, 0, m * 128:(m + 1) * 128],
                         ones_bf[0:1, n * 512:(n + 1) * 512],
                         start=False, stop=True).then_inc(s_t.h, 1)
                s_t.inc(1)
                tdone[("q", m, n)] = s_t.v

        @block.vector
        def _(v, g0=g0):
            for j0 in range(g0, g0 + 8, 2):
                for j in (j0, j0 + 1):
                    pi = j % 8
                    v.wait_ge(s_t.h, _qt0 + j + 1)
                    v.tensor_scalar(fs_a[:, (j % 2) * 512:(j % 2) * 512 + 512], ps[pi][:, :], 0.0, None, ALU.min)
                    v.engine_nop().then_inc(s_v.h, 1)
                    s_v.inc(1)
                for j in (j0, j0 + 1):
                    m, n = divmod(j, 4)
                    pi = j % 8
                    v.wait_ge(s_s.h, _qs0 + j + 1)
                    v.scalar_tensor_tensor(q_fm[:, m, n * 512:(n + 1) * 512],
                                           ps[pi][:, :], 0.0, fs_b[:, (j % 2) * 512:(j % 2) * 512 + 512],
                                           ALU.max, ALU.add).then_inc(s_v.h, 1)
                    s_v.inc(1)

        @block.scalar
        def _(s, g0=g0):
            for j in range(g0, g0 + 8):
                s.wait_ge(s_v.h, _qv0 + 4 * (j // 2) + (j % 2) + 1)
                s.activation(out=fs_b[:, (j % 2) * 512:(j % 2) * 512 + 512], in_=fs_a[:, (j % 2) * 512:(j % 2) * 512 + 512],
                             func=AF.Exp).then_inc(s_s.h, 1)
                s_s.inc(1)
        pump_dq(4)

    for j in range(24, 32):
        bank_free[j % 8] = ("v", _qv0 + 4 * (j // 2) + (j % 2) + 3)

    free("wslotB")

    if _STOP_AFTER == "qgate":
        return finish(lambda: (dump_tm("qfm", q_fm[:, :, :], FCH, T, BF16),
                               dump_tm("gatetm", gate_tm[:, :, :], TCH, D, BF16)))

    gm_tm = k_tm
    sc_t = {}
    gate_all = s_s.v
    q_all = s_v.v

    for c in range(TCH):
        pa, pb, pz = (0, 1, 2) if c % 2 == 0 else (3, 4, 5)

        @block.tensor
        def _(t, c=c, pa=pa, pb=pb, pz=pz):
            if c == 0:
                t.wait_ge(s_v.h, max(v_kv, q_all))
            for pi in (pa, pb, pz):
                bank_wait(t, pi)
            for p in range(NPAIR):
                dst = ps[pa] if p < 4 else ps[pb]
                t.matmul(dst[:, (p % 4) * 128:(p % 4) * 128 + 128],
                         q_fm[:, p, c * 128:(c + 1) * 128],
                         kvaug[:, p, :], start=True, stop=True,
                         skip_group_check=True)
                mm = t.matmul(ps[pz][:, p * 4:(p + 1) * 4],
                              q_fm[:, p, c * 128:(c + 1) * 128],
                              zk[:, p * 4:(p + 1) * 4], start=True, stop=True,
                              skip_group_check=True)
            mm.then_inc(s_t.h, 1)
        sc_t[c] = s_t.inc(1)

        @block.vector
        def _(v, c=c, pa=pa, pb=pb, pz=pz):
            v.wait_ge(s_t.h, sc_t[c])
            if c == 0:
                v.wait_ge(s_s.h, gate_all)
            inst = v.tensor_copy(zs32[:, :], ps[pz][:, 0:32])
            fence(v, s_v, inst)
            z3 = zs32[:, :].rearrange("p (h t) -> p h t", t=2)
            inst = v.scalar_tensor_tensor(zsum[:, :], z3[:, :, 0], 1e-6,
                                          z3[:, :, 1], ALU.add, ALU.add)
            fence(v, s_v, inst)
            inst = v.reciprocal(zbuf[:, :], zsum[:, :])
            fence(v, s_v, inst)
            for half, pp in ((0, pa), (1, pb)):
                zv = zbuf[:, half * 8:(half + 1) * 8].unsqueeze(2).broadcast_to([128, 8, 64])
                inst = v.tensor_tensor(fs_a[:, 0:512].rearrange("p (h e) -> p h e", e=64),
                                       ps[pp][:, :].rearrange("p (h e) -> p h e", e=64),
                                       zv, ALU.mult)
                fence(v, s_v, inst)
                v.tensor_tensor(gm_tm[:, c, half * 512:(half + 1) * 512],
                                fs_a[:, 0:512],
                                gate_tm[:, c, half * 512:(half + 1) * 512],
                                ALU.mult)
            v.engine_nop().then_inc(s_v.h, 1)
        val = s_v.inc(1)
        for pi in (pa, pb, pz):
            bank_free[pi] = ("v", val)
        if c % 2 == 1:
            pump_dq(2)
    v_gm_all = s_v.v

    if _STOP_AFTER == "gm":
        return finish(lambda: dump_tm("gmtm", gm_tm[:, :, :], TCH, D, BF16))

    gm_fm = xbf_fm
    tr_gm = pe_transpose(gm_fm, gm_tm, (6, 7))

    free("q_fm"); free("gate_tm"); free("k_tm")
    free("fs_b"); free("fs_a"); free("zk"); free("kvaug")
    x_tm = alloc("x_tm", [128, TCH, D], F32)

    @block.sync
    def _(sync):
        sync.wait_ge(s_v.h, tr_gm)
        sync.dma_start(out=x_tm[:, :, :],
                       in_=x_d.rearrange("(t p) d -> p t d", p=128)).then_inc(s_x2.h, 16)
    ld_x2 = s_x2.inc(16)

    o_t = {}
    _ot0, _ov0 = s_t.v, s_v.v

    for g0 in range(0, 32, 8):
        @block.tensor
        def _(t, g0=g0):
            if g0 == 0:
                t.wait_ge(s_v.h, tr_gm)
                t.wait_ge(gd_w['wo'][0].h, gd_w['wo'][1])
            for j in range(g0, g0 + 8):
                m, n = divmod(j, 2)
                pi = j % 8
                if j < 8:
                    bank_wait(t, pi)
                else:
                    t.wait_ge(s_v.h, _ov0 + (j - 8) // 2 + 1)
                for kc in range(FCH):
                    t.matmul(ps[pi][:, :],
                             gm_fm[:, kc, m * 128:(m + 1) * 128],
                             wslot[0][:, kc, n * 512:(n + 1) * 512],
                             start=(kc == 0), stop=False)
                t.matmul(ps[pi][:, :],
                         ones_bf[0:1, 0:128],
                         bias_sb[0:1, 4, n * 512:(n + 1) * 512],
                         start=False, stop=True).then_inc(s_t.h, 1)
                s_t.inc(1)
                o_t[(m, n)] = s_t.v
        pump_dq(2)

    res_last = {}

    @block.vector
    def _(v):
        v.wait_ge(s_x2.h, ld_x2)
        for m in range(TCH):
            for n in range(2):
                v.wait_ge(s_t.h, o_t[(m, n)])
                v.tensor_tensor(x_tm[:, m, n * 512:(n + 1) * 512],
                                ps[(m * 2 + n) % 8][:, :],
                                x_tm[:, m, n * 512:(n + 1) * 512], ALU.add)
            v.engine_nop().then_inc(s_v.h, 1)
            s_v.inc(1)
            res_last[m] = s_v.v
    for m in range(TCH - 4, TCH):
        for n in range(2):
            bank_free[(m * 2 + n) % 8] = ("v", res_last[m])

    last_vres = s_v.v

    @block.vector
    def _(v):
        v.wait_ge(s_v.h, last_vres)
        for m in range(TCH):
            v.bn_stats(bn6[:, m, 0:6], x_tm[:, m, 0:512])
            inst = v.bn_stats(bn6[:, m, 6:12], x_tm[:, m, 512:1024])
        fence(v, s_v, inst)
        for m in range(TCH):
            v.bn_aggr(mvall[:, m, :], bn6[:, m, 0:12])
        v.engine_nop().then_inc(s_v.h, 1)
    v_bn1 = s_v.inc(1)

    @block.scalar
    def _(s):
        s.wait_ge(s_v.h, v_bn1)
        s.activation(out=sdall[:, :], in_=mvall[:, :, 1], func=AF.Sqrt,
                     bias=ccol[:, 1:2], scale=1.0).then_inc(s_s.h, 1)
    s_sd1 = s_s.inc(1)

    @block.vector
    def _(v):
        v.wait_ge(s_s.h, s_sd1)
        inst = v.reciprocal(rstd[:, :], sdall[:, :])
        fence(v, s_v, inst)
        v.scalar_tensor_tensor(nmr[:, :], mvall[:, :, 0], -1.0, rstd[:, :],
                               ALU.mult, ALU.mult)
        v.engine_nop().then_inc(s_v.h, 1)
    v_r1 = s_v.inc(1)

    ln1_done = {}

    @block.scalar
    def _(s):
        s.wait_ge(s_v.h, v_r1)
        for m in range(TCH):
            s.activation(out=x_tm[:, m, :], in_=x_tm[:, m, :], func=AF.Identity,
                         bias=nmr[:, m:m + 1], scale=rstd[:, m:m + 1]).then_inc(s_s.h, 1)
            s_s.inc(1)
            ln1_done[m] = s_s.v

    if _STOP_AFTER == "ln1":
        return finish(lambda: dump_tm("xln", x_tm[:, :, :], TCH, D, F32))

    def barrier(bid, nvals, stat_ap, post_vector, extra_vwait=None):
        @block.tensor
        def _(t):
            t.wait_ge(s_id.h, ld_ident)
            if extra_vwait:
                t.wait_ge(s_v.h, extra_vwait)
            bank_wait(t, 7)
            t.transpose(ps[7][0:nvals, 0:128], stat_ap, ident[:]).then_inc(s_t.h, 1)
        tt = s_t.inc(1)

        @block.vector
        def _(v):
            v.wait_ge(s_t.h, tt)
            v.tensor_reduce(lmax[0:nvals, 0:1], ps[7][0:nvals, :], AX.X,
                            ALU.max).then_inc(s_v.h, 1)
        vv = s_v.inc(1)
        bank_free[7] = ("v", vv)

        @block.scalar
        def _(s):
            s.wait_ge(s_v.h, vv)
            s.wait_ge(s_id.h, ld_cc0)
            s.dma_start(out=cc_in[bid][0:nvals, 0:1], in_=lmax[0:nvals, 0:1],
                        allow_slow_non_contiguous=True).then_inc(s_tr.h, 16)
        dd = s_tr.inc(16)

        @block.gpsimd
        def _(g):
            g.wait_ge(s_tr.h, dd)
            g.collective_compute(
                "AllGather", ALU.bypass,
                replica_groups=[list(range(8))],
                ins=[cc_in[bid][:, :].opt()],
                outs=[cc_out[bid][:, :].opt()]).then_inc(s_cc.h, 1)
        cc = s_cc.inc(1)

        @block.scalar
        def _(s):
            s.wait_ge(s_cc.h, cc)
            s.dma_start(out=gth[0:1, 0:16],
                        in_=cc_out[bid][:, 0:1].rearrange("a b -> (b a)").unsqueeze(0),
                        allow_slow_non_contiguous=True).then_inc(s_tr.h, 16)
        gg = s_tr.inc(16)

        @block.vector
        def _(v):
            v.wait_ge(s_tr.h, gg)
            g3 = gth[0:1, :].rearrange("p (r t) -> p t r", t=2)
            inst = v.tensor_reduce(smax[0:1, 0:nvals], g3[:, 0:nvals, :], AX.X,
                                   ALU.max)
            fence(v, s_v, inst)
            post_vector(v)
            v.engine_nop().then_inc(s_v.h, 1)
        return s_v.inc(1)

    def bcast(off, ncols, src_ap, wait_v):
        @block.tensor
        def _(t):
            t.wait_ge(s_v.h, wait_v)
            bank_wait(t, 6)
            t.matmul(ps[6][:, 0:ncols], ones_f[0:1, :], src_ap,
                     start=True, stop=True).then_inc(s_t.h, 1)
        tt = s_t.inc(1)

        @block.vector
        def _(v):
            v.wait_ge(s_t.h, tt)
            v.tensor_copy(bc[:, off:off + ncols], ps[6][:, 0:ncols]).then_inc(s_v.h, 1)
        vv = s_v.inc(1)
        bank_free[6] = ("v", vv)
        return vv

    @block.vector
    def _(v):
        for m in range(TCH):
            v.wait_ge(s_s.h, ln1_done[m])
            inst = v.tensor_reduce(am24[:, m:m + 1], x_tm[:, m, :], AX.X, ALU.max,
                                   apply_absolute_value=True)
        fence(v, s_v, inst)
        v.tensor_reduce(am1[:, 0:1], am24[:, 0:TCH], AX.X, ALU.max).then_inc(s_v.h, 1)
    v_amx = s_v.inc(1)

    def post_b0(v):
        inst = v.tensor_scalar(smax[0:1, 2:3], smax[0:1, 0:1], INV_Q16, 1e-12,
                               ALU.mult, ALU.max)
        fence(v, s_v, inst)
        v.reciprocal(smax[0:1, 3:4], smax[0:1, 2:3])

    pb0 = barrier(0, 1, am1[:, 0:1], post_b0, extra_vwait=v_amx)
    vb0 = bcast(0, 2, smax[0:1, 2:4], pb0)

    @block.vector
    def _(v):
        v.wait_ge(s_v.h, vb0)
        v.tensor_scalar(nmcol[:, 0:1], bc[:, 0:1], -MAGIC, None, ALU.mult)
        v.engine_nop().then_inc(s_v.h, 1)
    v_nm0 = s_v.inc(1)

    xq_tm = alloc("xq_tm", [128, TCH, D], BF16)
    xq_stage = alloc("xq_stage", [128, 2, D], F32)
    xq_done = {}

    @block.scalar
    def _(s):
        s.wait_ge(s_v.h, v_nm0)
        for m in range(TCH):
            b = m % 2
            inst = s.activation(out=xq_stage[:, b, :], in_=x_tm[:, m, :],
                                func=AF.Identity, bias=ccol[:, 0:1], scale=bc[:, 1:2])
            fence(s, s_s, inst)
            s.activation(out=xq_tm[:, m, :], in_=xq_stage[:, b, :], func=AF.Identity,
                         bias=nmcol[:, 0:1], scale=bc[:, 0:1]).then_inc(s_s.h, 1)
            s_s.inc(1)
            xq_done[m] = s_s.v

    xq_fm = xbf_fm

    _xq_seen = set()

    def _xq_idx_wait(t, c, f):
        if c not in _xq_seen:
            _xq_seen.add(c)
            t.wait_ge(s_s.h, xq_done[c])
    tr_xq = pe_transpose(xq_fm, xq_tm, (0, 1), idx_wait=_xq_idx_wait)

    @block.scalar
    def _(s):
        s.wait_ge(s_s.h, ln1_done[TCH - 1])
        s.dma_start(out=xln_h[:, :, :].transpose([1, 0, 2]),
                    in_=x_tm[:, :, :]).then_inc(s_xsp.h, 16)
    ld_xlnspill = s_xsp.inc(16)

    if _STOP_AFTER == "xq":
        def d():
            dump_tm("xq", xq_fm[:, :, :], FCH, T, BF16)
            o2 = dbg_out("scx", [128, 8], F32)

            @block.gpsimd
            def _(g):
                g.wait_ge(s_v.h, s_v.v)
                g.dma_start(out=o2[:, :], in_=bc[:, :]).then_inc(s_out.h, 16)
                g.wait_ge(s_out.h, s_out.inc(16))
        return finish(d)

    free("xq_stage")
    free("xq_tm")
    free("bias_sb")
    free("ones_col")
    free("ones_bf")
    free("wslotA")
    wupT = alloc("wupT", [128, FCH, II], BF16)
    x_flat = x_tm[:, :, :].rearrange("p a b -> p (a b)")
    f1_stage = [x_flat[:, 0:T], x_flat[:, T:2 * T]]
    wgtT = x_flat[:, 2 * T:8 * T].bitcast(BF16).rearrange("p (a b) -> p a b", b=II)

    t_attn_done = s_t.v
    s_xq_done = s_s.v

    pump_dq_tag("up")
    pump_dq_tag("gt")

    @block.sync
    def _(sync):
        sync.wait_ge(s_t.h, t_attn_done)
        sync.wait_ge(s_v.h, tr_xq)
        sync.wait_ge(s_s.h, s_xq_done)
        sync.wait_ge(s_dqst["up"].h, dq_last["up"])
        sync.dma_start(out=wupT[:, :, :],
                       in_=wupT_h[:, :, :].transpose([1, 0, 2])).then_inc(s_wup.h, 16)
        sync.wait_ge(s_dqst["gt"].h, dq_last["gt"])
        sync.dma_start(out=wgtT[:, :, :],
                       in_=wgtT_h[:, :, :].transpose([1, 0, 2])).then_inc(s_wgt.h, 16)
    ld_wup = s_wup.inc(16)
    ld_wgt = s_wgt.inc(16)

    def f1_pass(wT, ld_w, out_h, amdst, skey, prev_spill=None):
        t0, v0, s0 = s_t.v, s_v.v, s_s.v
        ftd = {}
        sp_done = {}

        def sdnv(m, n):
            return s0 + 4 * m + n + 1

        def vredv(m):
            return v0 + m + 1

        GM = 4
        for m0 in range(0, ICH, GM):
            @block.tensor
            def _(t, m0=m0):
                if m0 == 0:
                    t.wait_ge(ld_w[0].h, ld_w[1])
                    t.wait_ge(s_v.h, tr_xq)
                for m in range(m0, m0 + GM):
                    for n in range(4):
                        j = m * 4 + n
                        pi = j % 8
                        if j < 8:
                            bank_wait(t, pi)
                        else:
                            pj = j - 8
                            t.wait_ge(s_s.h, sdnv(pj // 4, pj % 4))
                        for kc in range(FCH):
                            mm = t.matmul(ps[pi][:, :],
                                          wT[:, kc, m * 128:(m + 1) * 128],
                                          xq_fm[:, kc, n * 512:(n + 1) * 512],
                                          start=(kc == 0), stop=(kc == FCH - 1))
                        mm.then_inc(s_t.h, 1)
                        s_t.inc(1)
                        ftd[(m, n)] = s_t.v

            @block.scalar
            def _(s, m0=m0):
                if m0 == 0:
                    s.wait_ge(s_xsp.h, ld_xlnspill)
                for m in range(m0, m0 + GM):
                    b = m % 2
                    if m >= 2:
                        s.wait_ge(s_f1sp[(skey, b)].h, sp_done[m - 2])
                    elif prev_spill is not None:
                        s.wait_ge(s_f1sp[(prev_spill[2], b)].h, prev_spill[b])
                    for n in range(4):
                        s.wait_ge(s_t.h, ftd[(m, n)])
                        s.activation(out=f1_stage[b][:, n * 512:(n + 1) * 512],
                                     in_=ps[(m * 4 + n) % 8][:, :],
                                     func=AF.Identity).then_inc(s_s.h, 1)
                        s_s.inc(1)
                    sp_done[m] = 16 * (m // 2 + 1)

            @block.vector
            def _(v, m0=m0):
                for m in range(m0, m0 + GM):
                    b = m % 2
                    v.wait_ge(s_s.h, sdnv(m, 3))
                    v.tensor_reduce(amdst[:, m:m + 1], f1_stage[b], AX.X,
                                    ALU.max, apply_absolute_value=True).then_inc(s_v.h, 1)
                    s_v.inc(1)

            @block.sync
            def _(sync, m0=m0):
                for m in range(m0, m0 + GM):
                    b = m % 2
                    sync.wait_ge(s_v.h, vredv(m))
                    sync.dma_start(out=out_h[m, :, :],
                                   in_=f1_stage[b]).then_inc(s_f1sp[(skey, b)].h, 16)
                    s_f1sp[(skey, b)].inc(16)
            pump_dq(4)

        for n in range(4):
            j = ICH * 4 - 4 + n
            bank_free[j % 8] = ("s", sdnv(ICH - 1, n))
        for n in range(4):
            j = ICH * 4 - 8 + n
            bank_free[j % 8] = ("s", sdnv(ICH - 2, n))
        return (s_f1sp[(skey, 0)].v, s_f1sp[(skey, 1)].v, skey)

    up_spilled = f1_pass(wupT, (s_wup, ld_wup), up_h, am24, 'u')
    gt_spilled = f1_pass(wgtT, (s_wgt, ld_wgt), gate_h, am24b, 'g',
                         prev_spill=up_spilled)
    t_f1_done = s_t.v

    @block.vector
    def _(v):
        inst = v.engine_nop()
        fence(v, s_v, inst)
        v.tensor_reduce(am1[:, 0:1], am24[:, 0:ICH], AX.X, ALU.max)
        v.tensor_reduce(am1[:, 1:2], am24b[:, 0:ICH], AX.X, ALU.max)
        v.engine_nop().then_inc(s_v.h, 1)
    v_amug = s_v.inc(1)

    def post_b1(v):
        v.tensor_scalar(smax[0:1, 2:3], smax[0:1, 0:1], INV_Q16, 1e-12, ALU.mult, ALU.max)
        inst = v.tensor_scalar(smax[0:1, 4:5], smax[0:1, 1:2], INV_Q16, 1e-12, ALU.mult, ALU.max)
        fence(v, s_v, inst)
        v.reciprocal(smax[0:1, 3:4], smax[0:1, 2:3])
        v.reciprocal(smax[0:1, 5:6], smax[0:1, 4:5])

    pb1 = barrier(1, 2, am1[:, 0:2], post_b1, extra_vwait=v_amug)
    vb1 = bcast(0, 4, smax[0:1, 2:6], pb1)

    @block.vector
    def _(v):
        v.wait_ge(s_v.h, vb1)
        v.tensor_scalar(nmcol[:, 1:2], bc[:, 2:3], -MAGIC, None, ALU.mult)
        v.engine_nop().then_inc(s_v.h, 1)
    v_nm1 = s_v.inc(1)

    if _STOP_AFTER == "f1":
        def d():
            o = dbg_out("scug", [128, 8], F32)

            @block.gpsimd
            def _(g):
                g.wait_ge(s_v.h, v_nm1)
                g.dma_start(out=o[:, :], in_=bc[:, :]).then_inc(s_out.h, 16)
                g.wait_ge(s_out.h, s_out.inc(16))
        return finish(d)

    g2_keep = {}
    for m in range(8):
        g2_keep[m] = x_flat[:, m * T:(m + 1) * T]
    wupT_f = wupT[:, :, :].rearrange("p a b -> p (a b)").bitcast(F32)
    for m in range(6):
        g2_keep[8 + m] = wupT_f[:, m * T:(m + 1) * T]
    xbf_f = xbf_fm[:, :, :].rearrange("p a b -> p (a b)").bitcast(F32)
    for m in range(4):
        g2_keep[14 + m] = xbf_f[:, m * T:(m + 1) * T]

    pump_dq_tag("dn")
    assert not (dq_qA or dq_qB or dq_qC), "dq pieces left unpumped"
    free("dq_tr"); free("dq_bf"); free("dq_si"); free("dq_s")
    free("dq_bm"); free("dq_in")
    fr1 = alloc("fr1", [128, 3 * T], F32)
    fr2 = alloc("fr2", [128, 2 * T], F32)
    fr4 = alloc("fr4", [128, T], F32)

    def p3(buf, m):
        b = m % 3
        return buf[:, b * T:(b + 1) * T]

    def p2(buf, m):
        b = m % 2
        return buf[:, b * T:(b + 1) * T]

    pslc = p3

    def f2a_dst(m):
        return g2_keep[m] if m < N_KEEP else p2(fr2, m)

    f2a_ld = {}
    fA, fS, fR = {}, {}, {}
    fSV1, fS2, fS1, fSP = {}, {}, {}, {}
    for m in range(ICH):
        if m >= 1:
            fA[m - 1] = s_v.inc(1)
        if m >= 2:
            fR[m - 2] = s_v.inc(1)
        if m >= 1:
            fS[m - 1] = s_v.inc(1)
        fSV1[m] = s_s.inc(1)
        if m >= 1:
            fS2[m - 1] = s_s.inc(1)
        fS1[m] = s_s.inc(1)
    fS2[ICH - 1] = s_s.inc(1)
    fA[ICH - 1] = s_v.inc(1)
    fR[ICH - 2] = s_v.inc(1)
    fS[ICH - 1] = s_v.inc(1)
    fR[ICH - 1] = s_v.inc(1)
    for m in range(N_KEEP, ICH):
        fSP[m] = s_g2st.inc(16)
    for m in range(ICH):
        f2a_ld[m] = s_gr[m % 3].inc(16)

    @block.sync
    def _(sync):
        for m in range(ICH):
            if m == 0:
                sync.wait_ge(s_f1sp[("g", 0)].h, gt_spilled[0])
                sync.wait_ge(s_f1sp[("g", 1)].h, gt_spilled[1])
            if m >= 3:
                sync.wait_ge(s_v.h, fS[m - 3])
            sync.dma_start(out=p3(fr1, m),
                           in_=gate_h[m, :, :]).then_inc(s_gr[m % 3].h, 16)

    @block.gpsimd
    def _(g):
        for m in range(N_KEEP, ICH):
            g.wait_ge(s_v.h, fS[m])
            g.dma_start(out=g2_h[m, :, :], in_=p2(fr2, m)).then_inc(s_g2st.h, 16)

    @block.scalar
    def _(s):
        for m in range(ICH):
            s.wait_ge(s_gr[m % 3].h, f2a_ld[m])
            if m == 0:
                s.wait_ge(s_v.h, v_nm1)
            s.activation(out=p3(fr1, m), in_=p3(fr1, m), func=AF.Identity,
                         bias=ccol[:, 0:1], scale=bc[:, 3:4]).then_inc(s_s.h, 1)
            if m >= 1:
                s.wait_ge(s_s.h, fS1[m - 1])
                s.activation(out=p2(fr2, m - 1), in_=p2(fr2, m - 1),
                             func=AF.Identity, bias=ccol[:, 0:1],
                             scale=ccol[:, 2:3]).then_inc(s_s.h, 1)
            s.wait_ge(s_s.h, fSV1[m])
            if m >= 2:
                s.wait_ge(s_v.h, fA[m - 2])
                if m - 2 >= N_KEEP:
                    s.wait_ge(s_v.h, fR[m - 2])
                    s.wait_ge(s_g2st.h, fSP[m - 2])
            s.activation(out=p2(fr2, m), in_=p3(fr1, m), func=AF.Sigmoid,
                         bias=nmcol[:, 1:2], scale=bc[:, 2:3]).then_inc(s_s.h, 1)
        s.wait_ge(s_s.h, fS1[ICH - 1])
        s.activation(out=p2(fr2, ICH - 1), in_=p2(fr2, ICH - 1),
                     func=AF.Identity, bias=ccol[:, 0:1],
                     scale=ccol[:, 2:3]).then_inc(s_s.h, 1)

    @block.vector
    def _(v):
        for m in range(ICH + 1):
            if m >= 1:
                v.wait_ge(s_s.h, fS2[m - 1])
                v.tensor_scalar(fr4[:, :], p2(fr2, m - 1), MAGIC, 65535.0,
                                ALU.subtract, ALU.min).then_inc(s_v.h, 1)
            if m >= 2:
                v.wait_ge(s_v.h, fS[m - 2])
                v.tensor_reduce(am24[:, m - 2:m - 1], f2a_dst(m - 2), AX.X,
                                ALU.max,
                                apply_absolute_value=True).then_inc(s_v.h, 1)
            if m >= 1:
                v.wait_ge(s_v.h, fA[m - 1])
                v.scalar_tensor_tensor(f2a_dst(m - 1), p3(fr1, m - 1), MAGIC,
                                       fr4[:, :], ALU.subtract,
                                       ALU.mult).then_inc(s_v.h, 1)
        v.wait_ge(s_v.h, fS[ICH - 1])
        v.tensor_reduce(am24[:, ICH - 1:ICH], f2a_dst(ICH - 1), AX.X, ALU.max,
                        apply_absolute_value=True).then_inc(s_v.h, 1)
        v.wait_ge(s_v.h, fR[ICH - 1])
        v.tensor_reduce(am1[:, 0:1], am24[:, 0:ICH], AX.X, ALU.max)
        v.engine_nop().then_inc(s_v.h, 1)
    v_am2 = s_v.inc(1)

    def post_b2(v):
        inst = v.tensor_scalar(smax[0:1, 6:7], smax[0:1, 4:5], INV_2P16, None, ALU.mult)
        fence(v, s_v, inst)
        inst = v.tensor_tensor(smax[0:1, 8:9], smax[0:1, 0:1], smax[0:1, 6:7], ALU.mult)
        fence(v, s_v, inst)
        inst = v.tensor_scalar(smax[0:1, 7:8], smax[0:1, 8:9], INV_Q16, 1e-12,
                               ALU.mult, ALU.max)
        fence(v, s_v, inst)
        inst = v.reciprocal(smax[0:1, 8:9], smax[0:1, 7:8])
        fence(v, s_v, inst)
        v.tensor_tensor(smax[0:1, 9:10], smax[0:1, 6:7], smax[0:1, 8:9], ALU.mult)

    pb2 = barrier(2, 1, am1[:, 0:1], post_b2, extra_vwait=v_am2)
    vb2 = bcast(4, 1, smax[0:1, 9:10], pb2)

    if _STOP_AFTER == "f2a":
        def d():
            o = dbg_out("scg2", [1, 12], F32)

            @block.gpsimd
            def _(g):
                g.wait_ge(s_v.h, s_v.v)
                g.dma_start(out=o[:, :], in_=smax[0:1, :]).then_inc(s_out.h, 16)
                g.wait_ge(s_out.h, s_out.inc(16))
        return finish(d)

    def f3_src(m):
        return g2_keep[m] if m < N_KEEP else p3(fr1, m)

    f3_ldg, f3_ldu = {}, {}
    gA, gS, gR = {}, {}, {}
    gSV, gSU, gSP = {}, {}, {}
    for m in range(ICH):
        if m >= 1:
            gA[m - 1] = s_v.inc(1)
        if m >= 2:
            gR[m - 2] = s_v.inc(1)
        if m >= 1:
            gS[m - 1] = s_v.inc(1)
        gSV[m] = s_s.inc(1)
        gSU[m] = s_s.inc(1)
    gA[ICH - 1] = s_v.inc(1)
    gR[ICH - 2] = s_v.inc(1)
    gS[ICH - 1] = s_v.inc(1)
    gR[ICH - 1] = s_v.inc(1)
    for m in range(N_KEEP, ICH):
        gSP[m] = s_opst.inc(16)
    for m in range(ICH):
        if m >= N_KEEP:
            f3_ldg[m] = s_g2r[m % 2].inc(16)
        f3_ldu[m] = s_ur[m % 2].inc(16)

    @block.sync
    def _(sync):
        for m in range(ICH):
            if m == 0:
                sync.wait_ge(s_f1sp[("u", 0)].h, up_spilled[0])
                sync.wait_ge(s_f1sp[("u", 1)].h, up_spilled[1])
            if m < 2:
                sync.wait_ge(s_v.h, fR[ICH - 2 + m])
                sync.wait_ge(s_g2st.h, fSP[ICH - 2 + m])
            else:
                sync.wait_ge(s_v.h, gA[m - 2])
            sync.dma_start(out=p2(fr2, m),
                           in_=up_h[m, :, :]).then_inc(s_ur[m % 2].h, 16)

    @block.gpsimd
    def _(g):
        for m in range(N_KEEP, ICH):
            if m - 3 >= N_KEEP:
                g.wait_ge(s_v.h, gS[m - 3])
                g.dma_start(out=op_h[m - 3, :, :],
                            in_=p3(fr1, m - 3)).then_inc(s_opst.h, 16)
            g.wait_ge(s_v.h, gS[m - 3])
            if m - 3 >= N_KEEP:
                g.wait_ge(s_v.h, gR[m - 3])
                g.wait_ge(s_opst.h, gSP[m - 3])
            g.wait_ge(s_g2st.h, fSP[m])
            g.dma_start(out=p3(fr1, m),
                        in_=g2_h[m, :, :]).then_inc(s_g2r[m % 2].h, 16)
        for m in range(ICH - 3, ICH):
            g.wait_ge(s_v.h, gS[m])
            g.dma_start(out=op_h[m, :, :],
                        in_=p3(fr1, m)).then_inc(s_opst.h, 16)

    @block.scalar
    def _(s):
        for m in range(ICH):
            if m == 0:
                s.wait_ge(s_v.h, vb2)
            if m >= N_KEEP:
                s.wait_ge(s_g2r[m % 2].h, f3_ldg[m])
            elif m >= 3:
                s.wait_ge(s_v.h, gS[m - 3])
            s.activation(out=p3(fr1, m), in_=f3_src(m), func=AF.Identity,
                         bias=ccol[:, 0:1], scale=bc[:, 4:5]).then_inc(s_s.h, 1)
            s.wait_ge(s_ur[m % 2].h, f3_ldu[m])
            s.activation(out=p2(fr2, m), in_=p2(fr2, m), func=AF.Identity,
                         bias=ccol[:, 0:1], scale=bc[:, 1:2]).then_inc(s_s.h, 1)

    @block.vector
    def _(v):
        for m in range(ICH + 1):
            if m >= 1:
                v.wait_ge(s_s.h, gSU[m - 1])
                v.tensor_scalar(fr4[:, :], p2(fr2, m - 1), MAGIC, None,
                                ALU.subtract).then_inc(s_v.h, 1)
            if m >= 2:
                v.wait_ge(s_v.h, gS[m - 2])
                v.tensor_reduce(am24b[:, m - 2:m - 1], f3_src(m - 2), AX.X,
                                ALU.max,
                                apply_absolute_value=True).then_inc(s_v.h, 1)
            if m >= 1:
                v.wait_ge(s_v.h, gA[m - 1])
                v.scalar_tensor_tensor(f3_src(m - 1), p3(fr1, m - 1), MAGIC,
                                       fr4[:, :], ALU.subtract,
                                       ALU.mult).then_inc(s_v.h, 1)
        v.wait_ge(s_v.h, gS[ICH - 1])
        v.tensor_reduce(am24b[:, ICH - 1:ICH], f3_src(ICH - 1), AX.X, ALU.max,
                        apply_absolute_value=True).then_inc(s_v.h, 1)
        v.wait_ge(s_v.h, gR[ICH - 1])
        v.tensor_reduce(am1[:, 0:1], am24b[:, 0:ICH], AX.X, ALU.max)
        v.engine_nop().then_inc(s_v.h, 1)
    v_am3 = s_v.inc(1)
    free("fr4")
    wdn_buf = alloc("wdn_buf", [128, 2, 2, D], BF16, side="right")

    def post_b3(v):
        inst = v.tensor_tensor(smax[0:1, 10:11], smax[0:1, 7:8], smax[0:1, 2:3], ALU.mult)
        fence(v, s_v, inst)
        inst = v.tensor_tensor(smax[0:1, 11:12], smax[0:1, 0:1], smax[0:1, 10:11], ALU.mult)
        fence(v, s_v, inst)
        inst = v.tensor_scalar(smax[0:1, 11:12], smax[0:1, 11:12], INV_Q16, 1e-12,
                               ALU.mult, ALU.max)
        fence(v, s_v, inst)
        inst = v.reciprocal(smax[0:1, 0:1], smax[0:1, 11:12])
        fence(v, s_v, inst)
        v.tensor_tensor(smax[0:1, 10:11], smax[0:1, 10:11], smax[0:1, 0:1], ALU.mult)

    pb3 = barrier(3, 1, am1[:, 0:1], post_b3, extra_vwait=v_am3)
    vb3 = bcast(5, 2, smax[0:1, 10:12], pb3)

    @block.vector
    def _(v):
        v.wait_ge(s_v.h, vb3)
        v.tensor_scalar(nmcol[:, 2:3], bc[:, 6:7], -MAGIC, None, ALU.mult)
        v.engine_nop().then_inc(s_v.h, 1)
    v_nm3 = s_v.inc(1)

    x_bf_flat = x_flat.bitcast(BF16)
    wupT_bfflat = wupT[:, :, :].rearrange("p a b -> p (a b)")
    obf = [x_bf_flat[:, k * T:(k + 1) * T] for k in range(16)]
    obf += [wupT_bfflat[:, k * T:(k + 1) * T] for k in range(8)]

    ob_done = {}
    ob_vo, ob_ldo = {}, {}
    for k in range(ICH):
        ob_vo[k] = s_v.inc(1)
        ob_done[k] = s_s.inc(1)
    for k in range(N_KEEP, ICH):
        ob_ldo[k] = s_opr[k % 2].inc(16)

    @block.gpsimd
    def _(g):
        for k in range(N_KEEP, ICH):
            g.wait_ge(s_s.h, ob_done[k - 3])
            if k - 3 >= N_KEEP:
                g.wait_ge(s_opr[(k - 3) % 2].h, ob_ldo[k - 3])
            g.wait_ge(s_opst.h, gSP[k])
            g.dma_start(out=p3(fr1, k),
                        in_=op_h[k, :, :]).then_inc(s_opr[k % 2].h, 16)

    @block.vector
    def _(v):
        for k in range(ICH):
            if k == 0:
                v.wait_ge(s_v.h, v_nm3)
                v.wait_ge(s_opst.h, gSP[ICH - 1])
            if k >= N_KEEP:
                v.wait_ge(s_opr[k % 2].h, ob_ldo[k])
            if k >= 3:
                v.wait_ge(s_s.h, ob_done[k - 3])
            v.tensor_scalar(p3(fr1, k), f3_src(k), bc[:, 5:6], MAGIC,
                            ALU.mult, ALU.add).then_inc(s_v.h, 1)

    @block.scalar
    def _(s):
        for k in range(ICH):
            s.wait_ge(s_v.h, ob_vo[k])
            s.activation(out=obf[k], in_=p3(fr1, k), func=AF.Identity,
                         bias=nmcol[:, 2:3], scale=bc[:, 6:7]).then_inc(s_s.h, 1)

    free("fr2")
    free("fr1")
    xlnrb = alloc("xlnrb", [128, 4, D], F32)
    y_stage = alloc("y_stage", [128, 4, D], F32)

    xln_rb = {}
    f3b_t = {}
    f3b_res = {}
    yout = {}
    wd_ld = {}
    wd_used = {}

    for w in range(4):
        @block.sync
        def _(sync, w=w):
            if w == 0:
                sync.wait_ge(s_xsp.h, 16)
                sync.wait_ge(s_s.h, ob_done[ICH - 1])
                sync.wait_ge(s_opr[0].h, ob_ldo[ICH - 2])
                sync.wait_ge(s_opr[1].h, ob_ldo[ICH - 1])
            else:
                sync.wait_ge(s_v.h, f3b_res[(w - 1, 3)])
            sync.dma_start(out=xlnrb[:, :, :],
                           in_=xln_h[w * 4:(w + 1) * 4, :, :].transpose([1, 0, 2])).then_inc(s_xl[w % 2].h, 16)
        xln_rb[w] = s_xl[w % 2].inc(16)

        for kg in range(12):
            gidx = w * 12 + kg

            @block.sync
            def _(sync, w=w, kg=kg, gidx=gidx):
                if gidx == 0:
                    sync.wait_ge(s_dqst["dn"].h, dq_last["dn"])
                    sync.wait_ge(s_t.h, t_f1_done)
                    sync.wait_ge(s_v.h, gS[ICH - 1])
                if gidx >= 2:
                    sync.wait_ge(s_t.h, wd_used[gidx - 2])
                sync.dma_start(out=wdn_buf[:, gidx % 2, :, :],
                               in_=wdnT_h[kg * 2:(kg + 1) * 2, :, :].transpose([1, 0, 2])).then_inc(s_wd[gidx % 2].h, 16)
            wd_ld[gidx] = s_wd[gidx % 2].inc(16)

            @block.tensor
            def _(t, w=w, kg=kg, gidx=gidx):
                if kg == 0:
                    for pi in range(8):
                        bank_wait(t, pi)
                t.wait_ge(s_wd[gidx % 2].h, wd_ld[gidx])
                for kk in range(2):
                    k = kg * 2 + kk
                    t.wait_ge(s_s.h, ob_done[k])
                    for m in range(4):
                        for n in range(2):
                            mm = t.matmul(ps[m * 2 + n][:, :],
                                          obf[k][:, (w * 4 + m) * 128:(w * 4 + m + 1) * 128],
                                          wdn_buf[:, gidx % 2, kk, n * 512:(n + 1) * 512],
                                          start=(k == 0), stop=(k == ICH - 1))
                mm.then_inc(s_t.h, 1)
            wd_used[gidx] = s_t.inc(1)
        f3b_t[w] = wd_used[w * 12 + 11]

        @block.vector
        def _(v, w=w):
            v.wait_ge(s_t.h, f3b_t[w])
            v.wait_ge(s_xl[w % 2].h, xln_rb[w])
            for m in range(4):
                if w >= 1:
                    v.wait_ge(s_tr.h, yout[(w - 1) * 4 + 3])
                for n in range(2):
                    inst = v.tensor_tensor(y_stage[:, m, n * 512:(n + 1) * 512],
                                           ps[m * 2 + n][:, :],
                                           xlnrb[:, m, n * 512:(n + 1) * 512],
                                           ALU.add)
                fence(v, s_v, inst)
                v.bn_stats(bn6[:, m, 0:6], y_stage[:, m, 0:512])
                inst = v.bn_stats(bn6[:, m, 6:12], y_stage[:, m, 512:1024])
                fence(v, s_v, inst)
                v.bn_aggr(mvw[:, m, :], bn6[:, m, 0:12])
                v.engine_nop().then_inc(s_v.h, 1)
                f3b_res[(w, m)] = s_v.inc(1)

        for pi in range(8):
            bank_free[pi] = ("v", f3b_res[(w, 3)])

        @block.scalar
        def _(s, w=w):
            s.wait_ge(s_v.h, f3b_res[(w, 3)])
            s.activation(out=sdw[:, :], in_=mvw[:, :, 1], func=AF.Sqrt,
                         bias=ccol[:, 1:2], scale=1.0).then_inc(s_s.h, 1)
        s_sdw = s_s.inc(1)

        @block.vector
        def _(v, w=w, s_sdw=s_sdw):
            v.wait_ge(s_s.h, s_sdw)
            inst = v.reciprocal(rstdw[:, :], sdw[:, :])
            fence(v, s_v, inst)
            v.scalar_tensor_tensor(nmrw[:, :], mvw[:, :, 0], -1.0, rstdw[:, :],
                                   ALU.mult, ALU.mult)
            v.engine_nop().then_inc(s_v.h, 1)
        v_rw = s_v.inc(1)

        for m in range(4):
            @block.scalar
            def _(s, w=w, m=m, v_rw=v_rw):
                if m == 0:
                    s.wait_ge(s_v.h, v_rw)
                b = m
                inst = s.activation(out=y_stage[:, b, :], in_=y_stage[:, b, :],
                                    func=AF.Identity, bias=nmrw[:, m:m + 1],
                                    scale=rstdw[:, m:m + 1])
                fence(s, s_s, inst)
                tc = w * 4 + m
                s.dma_start(out=y_d[tc * 128:(tc + 1) * 128, :],
                            in_=y_stage[:, b, :]).then_inc(s_tr.h, 16)
            yout[w * 4 + m] = s_tr.inc(16)

    @block.sync
    def _(sync):
        sync.wait_ge(s_tr.h, yout[15])

    return finish()


_CACHE = {}


def _get_program():
    key = (_STOP_AFTER, tuple(sorted(_DBG)))
    if key not in _CACHE:
        _CACHE[key] = build_program()
    return _CACHE[key]


def kernel(L0_attn_x, L0_attn_wg_w, L0_attn_wg_b, L0_attn_wq_w, L0_attn_wq_b,
           L0_attn_wk_w, L0_attn_wk_b, L0_attn_wv_w, L0_attn_wv_b,
           L0_attn_out_w, L0_attn_out_b, L0_ffn_w_gate, L0_ffn_w_up,
           L0_ffn_w_down):
    nc, _ = _get_program()
    x = np.ascontiguousarray(np.asarray(L0_attn_x, np.float32))
    com = {
        "wq": np.ascontiguousarray(L0_attn_wq_w, np.float32),
        "wk": np.ascontiguousarray(L0_attn_wk_w, np.float32),
        "wv": np.ascontiguousarray(L0_attn_wv_w, np.float32),
        "wg": np.ascontiguousarray(L0_attn_wg_w, np.float32),
        "wo": np.ascontiguousarray(L0_attn_out_w, np.float32),
        "bq": np.ascontiguousarray(L0_attn_wq_b, np.float32).reshape(1, D),
        "bk": np.ascontiguousarray(L0_attn_wk_b, np.float32).reshape(1, D),
        "bv": np.ascontiguousarray(L0_attn_wv_b, np.float32).reshape(1, D),
        "bg": np.ascontiguousarray(L0_attn_wg_b, np.float32).reshape(1, D),
        "bo": np.ascontiguousarray(L0_attn_out_b, np.float32).reshape(1, D),
        "wup": np.ascontiguousarray(L0_ffn_w_up, np.float32),
        "wgt": np.ascontiguousarray(L0_ffn_w_gate, np.float32),
        "wdn": np.ascontiguousarray(L0_ffn_w_down, np.float32),
        "ident": np.eye(128, dtype=np.float32),
    }
    in_maps = [dict(com, x=np.ascontiguousarray(x[b])) for b in range(8)]
    res = run_bass_kernel_spmd(nc, in_maps, core_ids=list(range(8)))
    return np.stack([res.results[b]["y"] for b in range(8)], axis=0)


# version 12
# speedup vs baseline: 1.0077x; 1.0077x over previous
import numpy as np
import concourse.bass as bass
import concourse.mybir as mybir
from concourse.bass_utils import run_bass_kernel_spmd

F32 = mybir.dt.float32
BF16 = mybir.dt.bfloat16
AF = mybir.ActivationFunctionType
ALU = mybir.AluOpType
AX = mybir.AxisListType

MAGIC = float(np.float32(1.5 * 2 ** 23))
INV_Q16 = float(np.float32(1.0 / 32767.0))
INV_QW = float(np.float32(1.0 / 7.0))
INV_2P16 = float(np.float32(1.0 / 65536.0))

T = 2048
D = 1024
TCH = 16
FCH = 8
II = 3072
ICH = 24
NPAIR = 8
N_KEEP = 18

_DBG = set()
_STOP_AFTER = None


class _Sem:
    def __init__(self, nc, name):
        self.h = nc.semaphore(name).__enter__()
        self.v = 0

    def inc(self, n=1):
        self.v += n
        return self.v


def build_program():
    nc = bass.Bass("TRN2", target_bir_lowering=False, debug=False,
                   detect_race_conditions=False)

    x_d = nc.dram_tensor("x", [T, D], F32, kind="ExternalInput").ap()
    ident_d = nc.dram_tensor("ident", [128, 128], F32, kind="ExternalInput").ap()
    w_d = {nm: nc.dram_tensor(nm, [D, D], F32, kind="ExternalInput").ap()
           for nm in ("wq", "wk", "wv", "wg", "wo")}
    b_d = {nm: nc.dram_tensor(nm, [1, D], F32, kind="ExternalInput").ap()
           for nm in ("bq", "bk", "bv", "bg", "bo")}
    wup_d = nc.dram_tensor("wup", [II, D], F32, kind="ExternalInput").ap()
    wgt_d = nc.dram_tensor("wgt", [II, D], F32, kind="ExternalInput").ap()
    wdn_d = nc.dram_tensor("wdn", [D, II], F32, kind="ExternalInput").ap()
    y_d = nc.dram_tensor("y", [T, D], F32, kind="ExternalOutput").ap()

    wupT_h = nc.dram_tensor("wupT_h", [FCH, 128, II], BF16).ap()
    wgtT_h = nc.dram_tensor("wgtT_h", [FCH, 128, II], BF16).ap()
    wdnT_h = nc.dram_tensor("wdnT_h", [ICH, 128, D], BF16).ap()
    up_h = nc.dram_tensor("up_h", [ICH, 128, T], F32).ap()
    gate_h = nc.dram_tensor("gate_h", [ICH, 128, T], F32).ap()
    g2_h = nc.dram_tensor("g2_h", [ICH, 128, T], F32).ap()
    op_h = nc.dram_tensor("op_h", [ICH, 128, T], F32).ap()
    xln_h = nc.dram_tensor("xln_h", [TCH, 128, D], F32).ap()
    cc_in = [nc.dram_tensor(f"cc_in{k}", [2, 8], F32).ap() for k in range(4)]
    cc_out = [nc.dram_tensor(f"cc_out{k}", [16, 8], F32).ap() for k in range(4)]

    dbg_d = {}

    def dbg_out(name, shape, dtype=F32):
        dbg_d[name] = nc.dram_tensor("dbg_" + name, shape, dtype,
                                     kind="ExternalOutput").ap()
        return dbg_d[name]

    live = {}

    def alloc(name, shape, dtype, side="left"):
        cm = nc.sbuf_tensor(name, shape, dtype, side=side)
        t = cm.__enter__()
        live[name] = cm
        return t

    def free(name):
        live.pop(name).__exit__(None, None, None)

    ident = alloc("identt", [128, 128], F32)
    ones_f = alloc("ones_f", [1, 128], F32)
    zbuf = alloc("zbuf", [128, 16], F32)
    zs32 = alloc("zs32", [128, 32], F32)
    ccol = alloc("ccol", [128, 3], F32)
    ident_bf = alloc("ident_bf", [128, 128], BF16)
    zrow = alloc("zrow", [2, 8], F32)
    zsum = alloc("zsum", [128, 16], F32)
    bn6 = alloc("bn6", [128, TCH, 12], F32)
    mvall = alloc("mvall", [128, TCH, 2], F32)
    sdall = alloc("sdall", [128, TCH], F32)
    rstd = alloc("rstd", [128, TCH], F32)
    nmr = alloc("nmr", [128, TCH], F32)
    am24 = alloc("am24", [128, ICH], F32)
    am24b = alloc("am24b", [128, ICH], F32)
    am1 = alloc("am1", [128, 2], F32)
    lmax = alloc("lmax", [2, 1], F32)
    gth = alloc("gth", [1, 16], F32)
    smax = alloc("smax", [1, 12], F32)
    bc = alloc("bc", [128, 8], F32)
    nmcol = alloc("nmcol", [128, 4], F32)
    mvw = alloc("mvw", [128, 4, 2], F32)
    sdw = alloc("sdw", [128, 4], F32)
    rstdw = alloc("rstdw", [128, 4], F32)
    nmrw = alloc("nmrw", [128, 4], F32)
    PW = 512
    DQNB = PW // 32
    dq_in = alloc("dq_in", [128, 4, PW], F32, side="right")
    dq_bm = alloc("dq_bm", [128, 4, DQNB], F32, side="right")
    dq_s = alloc("dq_s", [128, 4, DQNB], F32, side="right")
    dq_si = alloc("dq_si", [128, 4, DQNB], F32, side="right")
    dq_bf = alloc("dq_bf", [128, 4, PW], BF16, side="right")
    dq_tr = alloc("dq_tr", [128, 4, PW // 128, 128], BF16, side="right")
    wslotA = alloc("wslotA", [128, FCH, D], BF16, side="right")
    ones_bf = alloc("ones_bf", [1, T], BF16, side="right")
    ones_col = alloc("ones_col", [128, 1], BF16, side="right")
    bias_sb = alloc("bias_sb", [1, 5, D], BF16, side="right")
    kvaug = alloc("kvaug", [128, NPAIR, 128], BF16, side="right")
    zk = alloc("zk", [128, 32], BF16, side="right")
    fs_a = alloc("fs_a", [128, 1024], F32, side="right")
    fs_b = alloc("fs_b", [128, 1024], F32, side="right")
    wslotB = alloc("wslotB", [128, FCH, D], BF16, side="right")

    ps = [nc.psum_tensor(f"ps{i}", [128, 512], F32).__enter__() for i in range(8)]

    s_ld = _Sem(nc, "s_ld")
    s_tr = _Sem(nc, "s_tr")
    s_gd = _Sem(nc, "s_gd")
    s_v = _Sem(nc, "s_v")
    s_s = _Sem(nc, "s_s")
    s_t = _Sem(nc, "s_t")
    s_g = _Sem(nc, "s_g")
    s_cc = _Sem(nc, "s_cc")
    s_out = _Sem(nc, "s_out")
    s_id = _Sem(nc, "s_id")
    s_gx = _Sem(nc, "s_gx")
    s_gb = _Sem(nc, "s_gb")
    s_gwA = _Sem(nc, "s_gwA")
    s_gwB = _Sem(nc, "s_gwB")
    s_dqld = [_Sem(nc, f"s_dqld{i}") for i in range(4)]
    s_dqtr = _Sem(nc, "s_dqtr")
    s_dqtr2 = [_Sem(nc, "s_dqtrA"), _Sem(nc, "s_dqtrB")]
    s_dqst = {"up": _Sem(nc, "s_dqstU"), "gt": _Sem(nc, "s_dqstG"),
              "dn": _Sem(nc, "s_dqstD")}
    s_x2 = _Sem(nc, "s_x2")
    s_wup = _Sem(nc, "s_wup")
    s_wgt = _Sem(nc, "s_wgt")
    s_wdn = _Sem(nc, "s_wdn")
    s_f1sp = {("u", 0): _Sem(nc, "s_f1spU0"), ("u", 1): _Sem(nc, "s_f1spU1"),
              ("g", 0): _Sem(nc, "s_f1spG0"), ("g", 1): _Sem(nc, "s_f1spG1")}
    s_gr = [_Sem(nc, "s_gr0"), _Sem(nc, "s_gr1"), _Sem(nc, "s_gr2")]
    s_g2st = _Sem(nc, "s_g2st")
    s_ur = [_Sem(nc, "s_ur0"), _Sem(nc, "s_ur1"), _Sem(nc, "s_ur2")]
    s_g2r = [_Sem(nc, "s_g2r0"), _Sem(nc, "s_g2r1")]
    s_opst = _Sem(nc, "s_opst")
    s_opr = [_Sem(nc, "s_opr0"), _Sem(nc, "s_opr1")]
    s_xl = [_Sem(nc, "s_xl0"), _Sem(nc, "s_xl1")]
    s_wd = [_Sem(nc, "s_wd0"), _Sem(nc, "s_wd1")]
    s_dv = _Sem(nc, "s_dv")
    s_xsp = _Sem(nc, "s_xsp")
    all_dma_sems = ([s_ld, s_tr, s_gd, s_id, s_gx, s_gb, s_gwA, s_gwB, s_dqtr,
                     s_x2, s_wup, s_wgt, s_wdn, s_g2st, s_opst, s_xsp]
                    + s_dqld + list(s_dqst.values()) + list(s_f1sp.values())
                    + s_gr + s_ur + s_g2r + s_opr + s_xl + s_wd + s_dqtr2)

    semmap = {"v": s_v, "s": s_s, "g": s_g, "t": s_t, "ld": s_ld, "tr": s_tr}
    bank_free = [("v", 0)] * 8

    def bank_wait(eng, i):
        nm, val = bank_free[i]
        if val > 0:
            eng.wait_ge(semmap[nm].h, val)

    def fence(eng, sem, inst):
        inst.then_inc(sem.h, 1)
        eng.wait_ge(sem.h, sem.inc(1))

    def pe_transpose(dst_fm, src_tm, banks, pre_twait=None, idx_wait=None):
        base_t = s_t.v
        base_v = s_v.v
        nblk = TCH * FCH

        @block.tensor
        def _(t):
            if pre_twait is not None:
                pre_twait(t)
            for pb in banks:
                bank_wait(t, pb)
            for idx in range(nblk):
                c, f = divmod(idx, FCH)
                pb = banks[idx % 2]
                if idx_wait is not None:
                    idx_wait(t, c, f)
                if idx >= 2:
                    t.wait_ge(s_v.h, base_v + idx - 1)
                t.transpose(ps[pb][:, 0:64].bitcast(BF16),
                            src_tm[:, c, f * 128:(f + 1) * 128],
                            ident_bf[:, :]).then_inc(s_t.h, 1)
                s_t.inc(1)

        @block.vector
        def _(v):
            for idx in range(nblk):
                c, f = divmod(idx, FCH)
                pb = banks[idx % 2]
                v.wait_ge(s_t.h, base_t + idx + 1)
                v.tensor_copy(dst_fm[:, f, c * 128:(c + 1) * 128],
                              ps[pb][:, 0:64].bitcast(BF16)).then_inc(s_v.h, 1)
                s_v.inc(1)
        done = s_v.v
        for pb in banks:
            bank_free[pb] = ("v", done)
        return done

    blk_cm = nc.Block()
    block = blk_cm.__enter__()

    def finish(dump_emitter=None):
        if dump_emitter is not None:
            dump_emitter()
        blk_cm.__exit__(None, None, None)
        return nc, dbg_d

    def dump_tm(tag, buf_ap3, nch, w, dtype=F32):
        o = dbg_out(tag, [nch * 128, w], dtype)
        waits = [(s_v, s_v.v), (s_s, s_s.v), (s_t, s_t.v), (s_g, s_g.v)]
        waits += [(sm, sm.v) for sm in all_dma_sems]

        @block.gpsimd
        def _(g):
            for sm, val in waits:
                if val:
                    g.wait_ge(sm.h, val)
            g.dma_start(out=o.rearrange("(c p) w -> p c w", p=128),
                        in_=buf_ap3).then_inc(s_out.h, 16)
            g.wait_ge(s_out.h, s_out.inc(16))

    xbf_fm = alloc("xbf_fm", [128, FCH, T], BF16)
    xbf_tm = alloc("xbf_tm", [128, TCH, D], BF16, side="right")

    @block.sync
    def _(sync):
        sync.dma_start(out=ident[:], in_=ident_d[:, :]).then_inc(s_id.h, 16)
    ld_ident = s_id.inc(16)

    @block.gpsimd
    def _(g):
        g.dma_start(out=xbf_tm[:, :, :],
                    in_=x_d.rearrange("(t p) d -> p t d", p=128)).then_inc(s_gx.h, 16)
        for i, nm in enumerate(("bq", "bk", "bv", "bg", "bo")):
            g.dma_start(out=bias_sb[0:1, i, :], in_=b_d[nm][:, :]).then_inc(s_gb.h, 16)
    gd_x = s_gx.inc(16)
    gd_bias = s_gb.inc(16 * 5)

    wslot = [wslotA, wslotB]
    wname2slot = {"wk": 0, "wv": 1, "wg": 0, "wq": 1, "wo": 0}
    gd_w = {}
    w_guard = {}

    wsem = {0: s_gwA, 1: s_gwB}

    def emit_wload(nm):
        slot_sem = wsem[wname2slot[nm]]

        @block.gpsimd
        def _(g):
            gd = w_guard.get(nm)
            if gd is not None:
                g.wait_ge(semmap[gd[0]].h, gd[1])
            g.dma_start(out=wslot[wname2slot[nm]][:, :, :],
                        in_=w_d[nm].rearrange("(k p) d -> p k d", p=128)).then_inc(slot_sem.h, 16)
        gd_w[nm] = (slot_sem, slot_sem.inc(16))

    emit_wload("wk")
    emit_wload("wv")

    @block.vector
    def _(v):
        v.memset(ones_bf[:], 1.0)
        v.memset(ones_f[:], 1.0)
        v.memset(ones_col[:], 1.0)
        v.memset(zk[:], 0.0)
        v.memset(kvaug[:, :, :], 0.0)
        v.memset(ccol[:, 0:1], MAGIC)
        v.memset(ccol[:, 1:2], 1e-5)
        v.memset(ccol[:, 2:3], 65536.0)
        v.memset(zrow[:, :], 0.0)
        v.wait_ge(s_id.h, ld_ident)
        v.tensor_copy(ident_bf[:, :], ident[:, :])
        v.engine_nop().then_inc(s_v.h, 1)
    v_const = s_v.inc(1)

    @block.sync
    def _(sync):
        sync.wait_ge(s_v.h, v_const)
        for k in range(4):
            sync.dma_start(out=cc_in[k][:, :], in_=zrow[:, :]).then_inc(s_id.h, 16)
    ld_cc0 = s_id.inc(16 * 4)

    tr_xbf = pe_transpose(xbf_fm, xbf_tm, (0, 1),
                          pre_twait=lambda t: (t.wait_ge(s_gx.h, gd_x),
                                               t.wait_ge(s_v.h, v_const)))
    free("xbf_tm")

    if _STOP_AFTER == "xbf":
        return finish(lambda: dump_tm("xbf", xbf_fm[:, :, :], FCH, T, BF16))

    dq_state = {}
    dq_qA = []
    dq_qB = []
    dq_qC = []
    dq_last = {}
    dq_total = [0]

    def make_dq_pieces(tag, w_src, n_och, cl, out_h):
        npiece = cl // PW
        ntr = PW // 128

        def stageA(c, pc, i):
            b = i % 4

            @block.sync
            def _(sync, c=c, pc=pc, b=b, i=i):
                if i >= 4:
                    sync.wait_ge(s_g.h, dq_state[("gdq", i - 4)])
                sync.dma_start(out=dq_in[:, b, :],
                               in_=w_src[c * 128:(c + 1) * 128,
                                         pc * PW:(pc + 1) * PW]).then_inc(s_dqld[b].h, 16)
                dq_state[("ld", i)] = s_dqld[b].inc(16)

        def stageB(c, pc, i):
            b = i % 4

            @block.vector
            def _(v, b=b, i=i):
                v.wait_ge(s_dqld[b].h, dq_state[("ld", i)])
                v.tensor_reduce(dq_bm[:, b, :],
                                dq_in[:, b, :].rearrange("p (n q) -> p n q", q=32),
                                AX.X, ALU.max,
                                apply_absolute_value=True).then_inc(s_dv.h, 1)
                if i >= 1:
                    b1 = (i - 1) % 4
                    v.wait_ge(s_dv.h, dq_state[("red", i - 1)])
                    v.tensor_scalar(dq_s[:, b1, :], dq_bm[:, b1, :], INV_QW,
                                    1e-12, ALU.mult,
                                    ALU.max).then_inc(s_dv.h, 1)
                if i >= 2:
                    b2 = (i - 2) % 4
                    v.wait_ge(s_dv.h, dq_state[("tsd", i - 2)])
                    v.reciprocal(dq_si[:, b2, :], dq_s[:, b2, :])
                    v.engine_nop().then_inc(s_dv.h, 1)
            dq_state[("red", i)] = s_dv.inc(1)
            if i >= 1:
                dq_state[("tsd", i - 1)] = s_dv.inc(1)
            if i >= 2:
                dq_state[("vsc", i - 2)] = s_dv.inc(1)

            if i >= 2:
                j = i - 2

                @block.gpsimd
                def _(g, j=j):
                    g.wait_ge(s_dv.h, dq_state[("vsc", j)])
                    if j >= 4:
                        g.wait_ge(s_dqtr2[j % 2].h, dq_state[("tr", j - 4)])
                    bj = j % 4
                    sib = dq_si[:, bj, :].unsqueeze(2).broadcast_to([128, DQNB, 32])
                    w3 = dq_in[:, bj, :].rearrange("p (n q) -> p n q", q=32)
                    g.tensor_tensor(w3, w3, sib, ALU.mult).then_inc(s_g.h, 1)
                dq_state[("gmul", j)] = s_g.inc(1)

        def stageC(c, pc, i):
            b = i % 4

            @block.vector
            def _(v, b=b, i=i):
                v.wait_ge(s_g.h, dq_state[("gmul", i)])
                v.tensor_scalar(dq_in[:, b, :], dq_in[:, b, :], MAGIC, MAGIC,
                                ALU.add, ALU.subtract).then_inc(s_dv.h, 1)
            dq_state[("vrnd", i)] = s_dv.inc(1)

            @block.gpsimd
            def _(g, b=b, i=i):
                g.wait_ge(s_dv.h, dq_state[("vrnd", i)])
                sb_ = dq_s[:, b, :].unsqueeze(2).broadcast_to([128, DQNB, 32])
                w3 = dq_in[:, b, :].rearrange("p (n q) -> p n q", q=32)
                bf3 = dq_bf[:, b, :].rearrange("p (n q) -> p n q", q=32)
                g.tensor_tensor(bf3, w3, sb_, ALU.mult)
                g.engine_nop().then_inc(s_g.h, 1)
            dq_state[("gdq", i)] = s_g.inc(1)

            if b % 2 == 0:
                @block.sync
                def _(q, b=b, i=i):
                    q.wait_ge(s_g.h, dq_state[("gdq", i)])
                    for f in range(ntr):
                        q.dma_start_transpose(
                            dq_tr[:, b, f, :],
                            dq_bf[:, b, f * 128:(f + 1) * 128]).then_inc(s_dqtr2[b % 2].h, 16)
            else:
                @block.sync
                def _(q, b=b, i=i):
                    q.wait_ge(s_g.h, dq_state[("gdq", i)])
                    for f in range(ntr):
                        q.dma_start_transpose(
                            dq_tr[:, b, f, :],
                            dq_bf[:, b, f * 128:(f + 1) * 128]).then_inc(s_dqtr2[b % 2].h, 16)
            dq_state[("tr", i)] = s_dqtr2[b % 2].inc(16 * ntr)

            @block.sync
            def _(sync, c=c, pc=pc, b=b, i=i):
                sync.wait_ge(s_dqtr2[b % 2].h, dq_state[("tr", i)])
                sync.dma_start(
                    out=out_h[pc * ntr:(pc + 1) * ntr, :,
                              c * 128:(c + 1) * 128].transpose([1, 0, 2]),
                    in_=dq_tr[:, b, 0:ntr, :]).then_inc(s_dqst[tag].h, 16)
            dq_last[tag] = s_dqst[tag].inc(16)

        for c in range(n_och):
            for pc in range(npiece):
                i = dq_total[0]
                dq_qA.append((tag, stageA, c, pc, i))
                dq_qB.append((tag, stageB, c, pc, i))
                dq_qC.append((tag, stageC, c, pc, i))
                dq_total[0] += 1

    def drain_B():
        N = dq_total[0]
        for vi in (N, N + 1):
            @block.vector
            def _(v, vi=vi, N=N):
                if vi - 1 < N:
                    b1 = (vi - 1) % 4
                    v.wait_ge(s_dv.h, dq_state[("red", vi - 1)])
                    v.tensor_scalar(dq_s[:, b1, :], dq_bm[:, b1, :], INV_QW,
                                    1e-12, ALU.mult,
                                    ALU.max).then_inc(s_dv.h, 1)
                b2 = (vi - 2) % 4
                v.wait_ge(s_dv.h, dq_state[("tsd", vi - 2)])
                v.reciprocal(dq_si[:, b2, :], dq_s[:, b2, :])
                v.engine_nop().then_inc(s_dv.h, 1)
            if vi - 1 < N:
                dq_state[("tsd", vi - 1)] = s_dv.inc(1)
            dq_state[("vsc", vi - 2)] = s_dv.inc(1)

            j = vi - 2

            @block.gpsimd
            def _(g, j=j):
                g.wait_ge(s_dv.h, dq_state[("vsc", j)])
                if j >= 4:
                    g.wait_ge(s_dqtr2[j % 2].h, dq_state[("tr", j - 4)])
                bj = j % 4
                sib = dq_si[:, bj, :].unsqueeze(2).broadcast_to([128, DQNB, 32])
                w3 = dq_in[:, bj, :].rearrange("p (n q) -> p n q", q=32)
                g.tensor_tensor(w3, w3, sib, ALU.mult).then_inc(s_g.h, 1)
            dq_state[("gmul", j)] = s_g.inc(1)

    dq_drained = [False]

    def pump_dq(k):
        for _ in range(k):
            total0 = dq_total[0]
            cA = total0 - len(dq_qA)
            cB = total0 - len(dq_qB)
            cC = total0 - len(dq_qC)
            if dq_qA and cA <= cC + 3:
                tg, fn, c, pc, i = dq_qA.pop(0)
                fn(c, pc, i)
                cA += 1
            if dq_qB and cB < cA and cB <= cC + 3:
                tg, fn, c, pc, i = dq_qB.pop(0)
                fn(c, pc, i)
                cB += 1
                if not dq_qB and not dq_drained[0]:
                    drain_B()
                    dq_drained[0] = True
            if dq_qC and (cC <= cB - 3 or (dq_drained[0] and cC < cB)):
                tg, fn, c, pc, i = dq_qC.pop(0)
                fn(c, pc, i)

    def pump_dq_tag(tag):
        while ((dq_qA and dq_qA[0][0] == tag) or (dq_qB and dq_qB[0][0] == tag)
               or (dq_qC and dq_qC[0][0] == tag)):
            pump_dq(1)

    k_tm = alloc("k_tm", [128, TCH, D], BF16)
    v_tm = alloc("v_tm", [128, TCH, D], BF16)
    tdone = {}

    def run_tm_proj(wname, bidx, nbank, kind, dst):
        bi = "qkvgo".index(wname[1])
        slot = wslot[wname2slot[wname]]
        t0, v0, s0 = s_t.v, s_v.v, s_s.v
        ntile = TCH * 2

        def tdn(j):
            return t0 + j + 1

        if kind == "fm":
            def vmin(j): return v0 + 4 * (j // 2) + (j % 2) + 1
            def stt(j): return v0 + 4 * (j // 2) + (j % 2) + 3
            def sexp(j): return s0 + j + 1
            rel = ("v", stt)
        else:
            def scp(j): return s0 + j + 1
            rel = ("s", scp)

        GT = 8
        for g0 in range(0, ntile, GT):
            @block.tensor
            def _(t, g0=g0):
                if g0 == 0:
                    t.wait_ge(s_v.h, tr_xbf)
                    t.wait_ge(gd_w[wname][0].h, gd_w[wname][1])
                    t.wait_ge(s_gb.h, gd_bias)
                    t.wait_ge(s_v.h, v_const)
                for j in range(g0, g0 + GT):
                    m, n = divmod(j, 2)
                    pi = bidx + j % nbank
                    if j < nbank:
                        bank_wait(t, pi)
                    else:
                        t.wait_ge(semmap[rel[0]].h, rel[1](j - nbank))
                    for kc in range(FCH):
                        t.matmul(ps[pi][:, :],
                                 xbf_fm[:, kc, m * 128:(m + 1) * 128],
                                 slot[:, kc, n * 512:(n + 1) * 512],
                                 start=(kc == 0), stop=False)
                    t.matmul(ps[pi][:, :],
                             ones_bf[0:1, 0:128],
                             bias_sb[0:1, bi, n * 512:(n + 1) * 512],
                             start=False, stop=True).then_inc(s_t.h, 1)
                    s_t.inc(1)
                    tdone[(wname, m, n)] = s_t.v

            if kind == "fm":
                @block.vector
                def _(v, g0=g0):
                    for j0 in range(g0, g0 + GT, 2):
                        for j in (j0, j0 + 1):
                            pi = bidx + j % nbank
                            v.wait_ge(s_t.h, tdn(j))
                            v.tensor_scalar(fs_a[:, (j % 2) * 512:(j % 2) * 512 + 512], ps[pi][:, :],
                                            0.0, None, ALU.min)
                            v.engine_nop().then_inc(s_v.h, 1)
                            s_v.inc(1)
                        for j in (j0, j0 + 1):
                            m, n = divmod(j, 2)
                            pi = bidx + j % nbank
                            v.wait_ge(s_s.h, sexp(j))
                            v.scalar_tensor_tensor(dst[:, m, n * 512:(n + 1) * 512],
                                                   ps[pi][:, :], 0.0, fs_b[:, (j % 2) * 512:(j % 2) * 512 + 512],
                                                   ALU.max, ALU.add).then_inc(s_v.h, 1)
                            s_v.inc(1)

                @block.scalar
                def _(s, g0=g0):
                    for j in range(g0, g0 + GT):
                        s.wait_ge(s_v.h, vmin(j))
                        s.activation(out=fs_b[:, (j % 2) * 512:(j % 2) * 512 + 512], in_=fs_a[:, (j % 2) * 512:(j % 2) * 512 + 512],
                                     func=AF.Exp).then_inc(s_s.h, 1)
                        s_s.inc(1)
            else:
                @block.scalar
                def _(s, g0=g0):
                    for j in range(g0, g0 + GT):
                        m, n = divmod(j, 2)
                        pi = bidx + j % nbank
                        s.wait_ge(s_t.h, tdn(j))
                        s.activation(out=dst[:, m, n * 512:(n + 1) * 512],
                                     in_=ps[pi][:, :], func=kind).then_inc(s_s.h, 1)
                        s_s.inc(1)
            pump_dq(6)

        for j in range(ntile - nbank, ntile):
            bank_free[bidx + j % nbank] = (rel[0], rel[1](j))

    make_dq_pieces("up", wup_d, ICH, D, wupT_h)
    make_dq_pieces("gt", wgt_d, ICH, D, wgtT_h)
    make_dq_pieces("dn", wdn_d, FCH, II, wdnT_h)

    run_tm_proj("wk", 0, 8, "fm", k_tm)
    run_tm_proj("wv", 0, 8, AF.Identity, v_tm)
    v_kevac = s_v.v
    s_vevac = s_s.v

    w_guard["wg"] = ("t", tdone[("wk", TCH - 1, 1)])
    w_guard["wq"] = ("t", tdone[("wv", TCH - 1, 1)])
    emit_wload("wg")
    emit_wload("wq")

    if _STOP_AFTER == "ktm":
        return finish(lambda: (dump_tm("ktm", k_tm[:, :, :], TCH, D, BF16),
                               dump_tm("vtm", v_tm[:, :, :], TCH, D, BF16)))

    @block.tensor
    def _(t):
        t.wait_ge(s_v.h, v_kevac)
        t.wait_ge(s_s.h, s_vevac)
        for p in range(NPAIR):
            bank_wait(t, p)
        for p in range(NPAIR):
            for c in range(TCH):
                t.matmul(ps[p][:, 0:128],
                         k_tm[:, c, p * 128:(p + 1) * 128],
                         v_tm[:, c, p * 128:(p + 1) * 128],
                         start=(c == 0), stop=(c == TCH - 1),
                         skip_group_check=True)
            for c in range(TCH):
                mm = t.matmul(ps[p][:, 128:129],
                              k_tm[:, c, p * 128:(p + 1) * 128],
                              ones_col[:, :],
                              start=(c == 0), stop=(c == TCH - 1),
                              skip_group_check=True)
        mm.then_inc(s_t.h, 1)
    t_kv = s_t.inc(1)

    @block.vector
    def _(v):
        v.wait_ge(s_t.h, t_kv)
        for p in range(NPAIR):
            v.tensor_copy(kvaug[0:64, p, 0:64], ps[p][0:64, 0:64])
            v.tensor_copy(kvaug[64:128, p, 64:128], ps[p][64:128, 64:128])
            v.tensor_copy(zk[0:64, 4 * p:4 * p + 1], ps[p][0:64, 128:129])
            last = v.tensor_copy(zk[64:128, 4 * p + 2:4 * p + 3], ps[p][64:128, 128:129])
        fence(v, s_v, last)
        for p in range(NPAIR):
            v.tensor_tensor(zk[0:64, 4 * p + 1:4 * p + 2], ps[p][0:64, 128:129],
                            zk[0:64, 4 * p:4 * p + 1], ALU.subtract)
            v.tensor_tensor(zk[64:128, 4 * p + 3:4 * p + 4], ps[p][64:128, 128:129],
                            zk[64:128, 4 * p + 2:4 * p + 3], ALU.subtract)
        v.engine_nop().then_inc(s_v.h, 1)
    v_kv = s_v.inc(1)
    for i in range(8):
        bank_free[i] = ("v", v_kv)

    free("v_tm")
    pump_dq(3)

    if _STOP_AFTER == "kv":
        def d():
            o = dbg_out("kvaug", [128, NPAIR * 128], BF16)
            o2 = dbg_out("zkd", [128, 32], BF16)

            @block.gpsimd
            def _(g):
                g.wait_ge(s_v.h, v_kv)
                g.dma_start(out=o[:, :], in_=kvaug[:, :, :]).then_inc(s_out.h, 16)
                g.dma_start(out=o2[:, :], in_=zk[:, :]).then_inc(s_out.h, 16)
                g.wait_ge(s_out.h, s_out.inc(32))
        return finish(d)

    gate_tm = alloc("gate_tm", [128, TCH, D], BF16)
    run_tm_proj("wg", 0, 8, AF.Sigmoid, gate_tm)

    w_guard["wo"] = ("t", tdone[("wg", TCH - 1, 1)])
    emit_wload("wo")

    q_fm = alloc("q_fm", [128, FCH, T], BF16)

    _qt0, _qv0, _qs0 = s_t.v, s_v.v, s_s.v

    for g0 in range(0, 32, 8):
        @block.tensor
        def _(t, g0=g0):
            if g0 == 0:
                t.wait_ge(gd_w['wq'][0].h, gd_w['wq'][1])
            for j in range(g0, g0 + 8):
                m, n = divmod(j, 4)
                pi = j % 8
                if j < 8:
                    bank_wait(t, pi)
                else:
                    jj = j - 8
                    t.wait_ge(s_v.h, _qv0 + 4 * (jj // 2) + (jj % 2) + 3)
                for kc in range(FCH):
                    t.matmul(ps[pi][:, :],
                             wslot[1][:, kc, m * 128:(m + 1) * 128],
                             xbf_fm[:, kc, n * 512:(n + 1) * 512],
                             start=(kc == 0), stop=False)
                t.matmul(ps[pi][:, :],
                         bias_sb[0:1, 0, m * 128:(m + 1) * 128],
                         ones_bf[0:1, n * 512:(n + 1) * 512],
                         start=False, stop=True).then_inc(s_t.h, 1)
                s_t.inc(1)
                tdone[("q", m, n)] = s_t.v

        @block.vector
        def _(v, g0=g0):
            for j0 in range(g0, g0 + 8, 2):
                for j in (j0, j0 + 1):
                    pi = j % 8
                    v.wait_ge(s_t.h, _qt0 + j + 1)
                    v.tensor_scalar(fs_a[:, (j % 2) * 512:(j % 2) * 512 + 512], ps[pi][:, :], 0.0, None, ALU.min)
                    v.engine_nop().then_inc(s_v.h, 1)
                    s_v.inc(1)
                for j in (j0, j0 + 1):
                    m, n = divmod(j, 4)
                    pi = j % 8
                    v.wait_ge(s_s.h, _qs0 + j + 1)
                    v.scalar_tensor_tensor(q_fm[:, m, n * 512:(n + 1) * 512],
                                           ps[pi][:, :], 0.0, fs_b[:, (j % 2) * 512:(j % 2) * 512 + 512],
                                           ALU.max, ALU.add).then_inc(s_v.h, 1)
                    s_v.inc(1)

        @block.scalar
        def _(s, g0=g0):
            for j in range(g0, g0 + 8):
                s.wait_ge(s_v.h, _qv0 + 4 * (j // 2) + (j % 2) + 1)
                s.activation(out=fs_b[:, (j % 2) * 512:(j % 2) * 512 + 512], in_=fs_a[:, (j % 2) * 512:(j % 2) * 512 + 512],
                             func=AF.Exp).then_inc(s_s.h, 1)
                s_s.inc(1)
        pump_dq(4)

    for j in range(24, 32):
        bank_free[j % 8] = ("v", _qv0 + 4 * (j // 2) + (j % 2) + 3)

    free("wslotB")

    if _STOP_AFTER == "qgate":
        return finish(lambda: (dump_tm("qfm", q_fm[:, :, :], FCH, T, BF16),
                               dump_tm("gatetm", gate_tm[:, :, :], TCH, D, BF16)))

    gm_tm = k_tm
    sc_t = {}
    gate_all = s_s.v
    q_all = s_v.v

    for c in range(TCH):
        pa, pb, pz = (0, 1, 2) if c % 2 == 0 else (3, 4, 5)

        @block.tensor
        def _(t, c=c, pa=pa, pb=pb, pz=pz):
            if c == 0:
                t.wait_ge(s_v.h, max(v_kv, q_all))
            for pi in (pa, pb, pz):
                bank_wait(t, pi)
            for p in range(NPAIR):
                dst = ps[pa] if p < 4 else ps[pb]
                t.matmul(dst[:, (p % 4) * 128:(p % 4) * 128 + 128],
                         q_fm[:, p, c * 128:(c + 1) * 128],
                         kvaug[:, p, :], start=True, stop=True,
                         skip_group_check=True)
                mm = t.matmul(ps[pz][:, p * 4:(p + 1) * 4],
                              q_fm[:, p, c * 128:(c + 1) * 128],
                              zk[:, p * 4:(p + 1) * 4], start=True, stop=True,
                              skip_group_check=True)
            mm.then_inc(s_t.h, 1)
        sc_t[c] = s_t.inc(1)

        @block.vector
        def _(v, c=c, pa=pa, pb=pb, pz=pz):
            v.wait_ge(s_t.h, sc_t[c])
            if c == 0:
                v.wait_ge(s_s.h, gate_all)
            inst = v.tensor_copy(zs32[:, :], ps[pz][:, 0:32])
            fence(v, s_v, inst)
            z3 = zs32[:, :].rearrange("p (h t) -> p h t", t=2)
            inst = v.scalar_tensor_tensor(zsum[:, :], z3[:, :, 0], 1e-6,
                                          z3[:, :, 1], ALU.add, ALU.add)
            fence(v, s_v, inst)
            inst = v.reciprocal(zbuf[:, :], zsum[:, :])
            fence(v, s_v, inst)
            for half, pp in ((0, pa), (1, pb)):
                zv = zbuf[:, half * 8:(half + 1) * 8].unsqueeze(2).broadcast_to([128, 8, 64])
                inst = v.tensor_tensor(fs_a[:, 0:512].rearrange("p (h e) -> p h e", e=64),
                                       ps[pp][:, :].rearrange("p (h e) -> p h e", e=64),
                                       zv, ALU.mult)
                fence(v, s_v, inst)
                v.tensor_tensor(gm_tm[:, c, half * 512:(half + 1) * 512],
                                fs_a[:, 0:512],
                                gate_tm[:, c, half * 512:(half + 1) * 512],
                                ALU.mult)
            v.engine_nop().then_inc(s_v.h, 1)
        val = s_v.inc(1)
        for pi in (pa, pb, pz):
            bank_free[pi] = ("v", val)
        if c % 2 == 1:
            pump_dq(2)
    v_gm_all = s_v.v

    if _STOP_AFTER == "gm":
        return finish(lambda: dump_tm("gmtm", gm_tm[:, :, :], TCH, D, BF16))

    gm_fm = xbf_fm
    tr_gm = pe_transpose(gm_fm, gm_tm, (6, 7))

    free("q_fm"); free("gate_tm"); free("k_tm")
    free("fs_b"); free("fs_a"); free("zk"); free("kvaug")
    x_tm = alloc("x_tm", [128, TCH, D], F32)

    @block.sync
    def _(sync):
        sync.wait_ge(s_v.h, tr_gm)
        sync.dma_start(out=x_tm[:, :, :],
                       in_=x_d.rearrange("(t p) d -> p t d", p=128)).then_inc(s_x2.h, 16)
    ld_x2 = s_x2.inc(16)

    o_t = {}
    _ot0, _ov0 = s_t.v, s_v.v

    for g0 in range(0, 32, 8):
        @block.tensor
        def _(t, g0=g0):
            if g0 == 0:
                t.wait_ge(s_v.h, tr_gm)
                t.wait_ge(gd_w['wo'][0].h, gd_w['wo'][1])
            for j in range(g0, g0 + 8):
                m, n = divmod(j, 2)
                pi = j % 8
                if j < 8:
                    bank_wait(t, pi)
                else:
                    t.wait_ge(s_v.h, _ov0 + (j - 8) // 2 + 1)
                for kc in range(FCH):
                    t.matmul(ps[pi][:, :],
                             gm_fm[:, kc, m * 128:(m + 1) * 128],
                             wslot[0][:, kc, n * 512:(n + 1) * 512],
                             start=(kc == 0), stop=False)
                t.matmul(ps[pi][:, :],
                         ones_bf[0:1, 0:128],
                         bias_sb[0:1, 4, n * 512:(n + 1) * 512],
                         start=False, stop=True).then_inc(s_t.h, 1)
                s_t.inc(1)
                o_t[(m, n)] = s_t.v
        pump_dq(2)

    res_last = {}

    @block.vector
    def _(v):
        v.wait_ge(s_x2.h, ld_x2)
        for m in range(TCH):
            for n in range(2):
                v.wait_ge(s_t.h, o_t[(m, n)])
                v.tensor_tensor(x_tm[:, m, n * 512:(n + 1) * 512],
                                ps[(m * 2 + n) % 8][:, :],
                                x_tm[:, m, n * 512:(n + 1) * 512], ALU.add)
            v.engine_nop().then_inc(s_v.h, 1)
            s_v.inc(1)
            res_last[m] = s_v.v
    for m in range(TCH - 4, TCH):
        for n in range(2):
            bank_free[(m * 2 + n) % 8] = ("v", res_last[m])

    last_vres = s_v.v

    @block.vector
    def _(v):
        v.wait_ge(s_v.h, last_vres)
        for m in range(TCH):
            v.bn_stats(bn6[:, m, 0:6], x_tm[:, m, 0:512])
            inst = v.bn_stats(bn6[:, m, 6:12], x_tm[:, m, 512:1024])
        fence(v, s_v, inst)
        for m in range(TCH):
            v.bn_aggr(mvall[:, m, :], bn6[:, m, 0:12])
        v.engine_nop().then_inc(s_v.h, 1)
    v_bn1 = s_v.inc(1)

    @block.scalar
    def _(s):
        s.wait_ge(s_v.h, v_bn1)
        s.activation(out=sdall[:, :], in_=mvall[:, :, 1], func=AF.Sqrt,
                     bias=ccol[:, 1:2], scale=1.0).then_inc(s_s.h, 1)
    s_sd1 = s_s.inc(1)

    @block.vector
    def _(v):
        v.wait_ge(s_s.h, s_sd1)
        inst = v.reciprocal(rstd[:, :], sdall[:, :])
        fence(v, s_v, inst)
        v.scalar_tensor_tensor(nmr[:, :], mvall[:, :, 0], -1.0, rstd[:, :],
                               ALU.mult, ALU.mult)
        v.engine_nop().then_inc(s_v.h, 1)
    v_r1 = s_v.inc(1)

    ln1_done = {}

    @block.scalar
    def _(s):
        s.wait_ge(s_v.h, v_r1)
        for m in range(TCH):
            s.activation(out=x_tm[:, m, :], in_=x_tm[:, m, :], func=AF.Identity,
                         bias=nmr[:, m:m + 1], scale=rstd[:, m:m + 1]).then_inc(s_s.h, 1)
            s_s.inc(1)
            ln1_done[m] = s_s.v

    if _STOP_AFTER == "ln1":
        return finish(lambda: dump_tm("xln", x_tm[:, :, :], TCH, D, F32))

    def barrier(bid, nvals, stat_ap, post_vector, extra_vwait=None):
        @block.tensor
        def _(t):
            t.wait_ge(s_id.h, ld_ident)
            if extra_vwait:
                t.wait_ge(s_v.h, extra_vwait)
            bank_wait(t, 7)
            t.transpose(ps[7][0:nvals, 0:128], stat_ap, ident[:]).then_inc(s_t.h, 1)
        tt = s_t.inc(1)

        @block.vector
        def _(v):
            v.wait_ge(s_t.h, tt)
            v.tensor_reduce(lmax[0:nvals, 0:1], ps[7][0:nvals, :], AX.X,
                            ALU.max).then_inc(s_v.h, 1)
        vv = s_v.inc(1)
        bank_free[7] = ("v", vv)

        @block.scalar
        def _(s):
            s.wait_ge(s_v.h, vv)
            s.wait_ge(s_id.h, ld_cc0)
            s.dma_start(out=cc_in[bid][0:nvals, 0:1], in_=lmax[0:nvals, 0:1],
                        allow_slow_non_contiguous=True).then_inc(s_tr.h, 16)
        dd = s_tr.inc(16)

        @block.gpsimd
        def _(g):
            g.wait_ge(s_tr.h, dd)
            g.collective_compute(
                "AllGather", ALU.bypass,
                replica_groups=[list(range(8))],
                ins=[cc_in[bid][:, :].opt()],
                outs=[cc_out[bid][:, :].opt()]).then_inc(s_cc.h, 1)
        cc = s_cc.inc(1)

        @block.scalar
        def _(s):
            s.wait_ge(s_cc.h, cc)
            s.dma_start(out=gth[0:1, 0:16],
                        in_=cc_out[bid][:, 0:1].rearrange("a b -> (b a)").unsqueeze(0),
                        allow_slow_non_contiguous=True).then_inc(s_tr.h, 16)
        gg = s_tr.inc(16)

        @block.vector
        def _(v):
            v.wait_ge(s_tr.h, gg)
            g3 = gth[0:1, :].rearrange("p (r t) -> p t r", t=2)
            inst = v.tensor_reduce(smax[0:1, 0:nvals], g3[:, 0:nvals, :], AX.X,
                                   ALU.max)
            fence(v, s_v, inst)
            post_vector(v)
            v.engine_nop().then_inc(s_v.h, 1)
        return s_v.inc(1)

    def bcast(off, ncols, src_ap, wait_v):
        @block.tensor
        def _(t):
            t.wait_ge(s_v.h, wait_v)
            bank_wait(t, 6)
            t.matmul(ps[6][:, 0:ncols], ones_f[0:1, :], src_ap,
                     start=True, stop=True).then_inc(s_t.h, 1)
        tt = s_t.inc(1)

        @block.vector
        def _(v):
            v.wait_ge(s_t.h, tt)
            v.tensor_copy(bc[:, off:off + ncols], ps[6][:, 0:ncols]).then_inc(s_v.h, 1)
        vv = s_v.inc(1)
        bank_free[6] = ("v", vv)
        return vv

    @block.vector
    def _(v):
        for m in range(TCH):
            v.wait_ge(s_s.h, ln1_done[m])
            inst = v.tensor_reduce(am24[:, m:m + 1], x_tm[:, m, :], AX.X, ALU.max,
                                   apply_absolute_value=True)
        fence(v, s_v, inst)
        v.tensor_reduce(am1[:, 0:1], am24[:, 0:TCH], AX.X, ALU.max).then_inc(s_v.h, 1)
    v_amx = s_v.inc(1)

    def post_b0(v):
        inst = v.tensor_scalar(smax[0:1, 2:3], smax[0:1, 0:1], INV_Q16, 1e-12,
                               ALU.mult, ALU.max)
        fence(v, s_v, inst)
        v.reciprocal(smax[0:1, 3:4], smax[0:1, 2:3])

    pb0 = barrier(0, 1, am1[:, 0:1], post_b0, extra_vwait=v_amx)
    vb0 = bcast(0, 2, smax[0:1, 2:4], pb0)

    @block.vector
    def _(v):
        v.wait_ge(s_v.h, vb0)
        v.tensor_scalar(nmcol[:, 0:1], bc[:, 0:1], -MAGIC, None, ALU.mult)
        v.engine_nop().then_inc(s_v.h, 1)
    v_nm0 = s_v.inc(1)

    xq_tm = alloc("xq_tm", [128, TCH, D], BF16)
    xq_stage = alloc("xq_stage", [128, 2, D], F32)
    xq_done = {}

    @block.scalar
    def _(s):
        s.wait_ge(s_v.h, v_nm0)
        for m in range(TCH):
            b = m % 2
            inst = s.activation(out=xq_stage[:, b, :], in_=x_tm[:, m, :],
                                func=AF.Identity, bias=ccol[:, 0:1], scale=bc[:, 1:2])
            fence(s, s_s, inst)
            s.activation(out=xq_tm[:, m, :], in_=xq_stage[:, b, :], func=AF.Identity,
                         bias=nmcol[:, 0:1], scale=bc[:, 0:1]).then_inc(s_s.h, 1)
            s_s.inc(1)
            xq_done[m] = s_s.v

    xq_fm = xbf_fm

    _xq_seen = set()

    def _xq_idx_wait(t, c, f):
        if c not in _xq_seen:
            _xq_seen.add(c)
            t.wait_ge(s_s.h, xq_done[c])
    tr_xq = pe_transpose(xq_fm, xq_tm, (0, 1), idx_wait=_xq_idx_wait)

    @block.scalar
    def _(s):
        s.wait_ge(s_s.h, ln1_done[TCH - 1])
        s.dma_start(out=xln_h[:, :, :].transpose([1, 0, 2]),
                    in_=x_tm[:, :, :]).then_inc(s_xsp.h, 16)
    ld_xlnspill = s_xsp.inc(16)

    if _STOP_AFTER == "xq":
        def d():
            dump_tm("xq", xq_fm[:, :, :], FCH, T, BF16)
            o2 = dbg_out("scx", [128, 8], F32)

            @block.gpsimd
            def _(g):
                g.wait_ge(s_v.h, s_v.v)
                g.dma_start(out=o2[:, :], in_=bc[:, :]).then_inc(s_out.h, 16)
                g.wait_ge(s_out.h, s_out.inc(16))
        return finish(d)

    free("xq_stage")
    free("xq_tm")
    free("bias_sb")
    free("ones_col")
    free("ones_bf")
    free("wslotA")
    wupT = alloc("wupT", [128, FCH, II], BF16)
    x_flat = x_tm[:, :, :].rearrange("p a b -> p (a b)")
    f1_stage = [x_flat[:, 0:T], x_flat[:, T:2 * T]]
    wgtT = x_flat[:, 2 * T:8 * T].bitcast(BF16).rearrange("p (a b) -> p a b", b=II)

    t_attn_done = s_t.v
    s_xq_done = s_s.v

    pump_dq_tag("up")
    pump_dq_tag("gt")

    @block.sync
    def _(sync):
        sync.wait_ge(s_t.h, t_attn_done)
        sync.wait_ge(s_v.h, tr_xq)
        sync.wait_ge(s_s.h, s_xq_done)
        sync.wait_ge(s_dqst["up"].h, dq_last["up"])
        sync.dma_start(out=wupT[:, :, :],
                       in_=wupT_h[:, :, :].transpose([1, 0, 2])).then_inc(s_wup.h, 16)
        sync.wait_ge(s_dqst["gt"].h, dq_last["gt"])
        sync.dma_start(out=wgtT[:, :, :],
                       in_=wgtT_h[:, :, :].transpose([1, 0, 2])).then_inc(s_wgt.h, 16)
    ld_wup = s_wup.inc(16)
    ld_wgt = s_wgt.inc(16)

    def f1_pass(wT, ld_w, out_h, amdst, skey, prev_spill=None):
        t0, v0, s0 = s_t.v, s_v.v, s_s.v
        ftd = {}
        sp_done = {}

        def sdnv(m, n):
            return s0 + 4 * m + n + 1

        def vredv(m):
            return v0 + m + 1

        GM = 4
        for m0 in range(0, ICH, GM):
            @block.tensor
            def _(t, m0=m0):
                if m0 == 0:
                    t.wait_ge(ld_w[0].h, ld_w[1])
                    t.wait_ge(s_v.h, tr_xq)
                for m in range(m0, m0 + GM):
                    for n in range(4):
                        j = m * 4 + n
                        pi = j % 8
                        if j < 8:
                            bank_wait(t, pi)
                        else:
                            pj = j - 8
                            t.wait_ge(s_s.h, sdnv(pj // 4, pj % 4))
                        for kc in range(FCH):
                            mm = t.matmul(ps[pi][:, :],
                                          wT[:, kc, m * 128:(m + 1) * 128],
                                          xq_fm[:, kc, n * 512:(n + 1) * 512],
                                          start=(kc == 0), stop=(kc == FCH - 1))
                        mm.then_inc(s_t.h, 1)
                        s_t.inc(1)
                        ftd[(m, n)] = s_t.v

            @block.scalar
            def _(s, m0=m0):
                if m0 == 0:
                    s.wait_ge(s_xsp.h, ld_xlnspill)
                for m in range(m0, m0 + GM):
                    b = m % 2
                    if m >= 2:
                        s.wait_ge(s_f1sp[(skey, b)].h, sp_done[m - 2])
                    elif prev_spill is not None:
                        s.wait_ge(s_f1sp[(prev_spill[2], b)].h, prev_spill[b])
                    for n in range(4):
                        s.wait_ge(s_t.h, ftd[(m, n)])
                        s.activation(out=f1_stage[b][:, n * 512:(n + 1) * 512],
                                     in_=ps[(m * 4 + n) % 8][:, :],
                                     func=AF.Identity).then_inc(s_s.h, 1)
                        s_s.inc(1)
                    sp_done[m] = 16 * (m // 2 + 1)

            @block.vector
            def _(v, m0=m0):
                for m in range(m0, m0 + GM):
                    b = m % 2
                    v.wait_ge(s_s.h, sdnv(m, 3))
                    v.tensor_reduce(amdst[:, m:m + 1], f1_stage[b], AX.X,
                                    ALU.max, apply_absolute_value=True).then_inc(s_v.h, 1)
                    s_v.inc(1)

            @block.sync
            def _(sync, m0=m0):
                for m in range(m0, m0 + GM):
                    b = m % 2
                    sync.wait_ge(s_v.h, vredv(m))
                    sync.dma_start(out=out_h[m, :, :],
                                   in_=f1_stage[b]).then_inc(s_f1sp[(skey, b)].h, 16)
                    s_f1sp[(skey, b)].inc(16)
            pump_dq(4)

        for n in range(4):
            j = ICH * 4 - 4 + n
            bank_free[j % 8] = ("s", sdnv(ICH - 1, n))
        for n in range(4):
            j = ICH * 4 - 8 + n
            bank_free[j % 8] = ("s", sdnv(ICH - 2, n))
        return (s_f1sp[(skey, 0)].v, s_f1sp[(skey, 1)].v, skey)

    up_spilled = f1_pass(wupT, (s_wup, ld_wup), up_h, am24, 'u')
    gt_spilled = f1_pass(wgtT, (s_wgt, ld_wgt), gate_h, am24b, 'g',
                         prev_spill=up_spilled)
    t_f1_done = s_t.v

    @block.vector
    def _(v):
        inst = v.engine_nop()
        fence(v, s_v, inst)
        v.tensor_reduce(am1[:, 0:1], am24[:, 0:ICH], AX.X, ALU.max)
        v.tensor_reduce(am1[:, 1:2], am24b[:, 0:ICH], AX.X, ALU.max)
        v.engine_nop().then_inc(s_v.h, 1)
    v_amug = s_v.inc(1)

    def post_b1(v):
        v.tensor_scalar(smax[0:1, 2:3], smax[0:1, 0:1], INV_Q16, 1e-12, ALU.mult, ALU.max)
        inst = v.tensor_scalar(smax[0:1, 4:5], smax[0:1, 1:2], INV_Q16, 1e-12, ALU.mult, ALU.max)
        fence(v, s_v, inst)
        v.reciprocal(smax[0:1, 3:4], smax[0:1, 2:3])
        v.reciprocal(smax[0:1, 5:6], smax[0:1, 4:5])

    pb1 = barrier(1, 2, am1[:, 0:2], post_b1, extra_vwait=v_amug)
    vb1 = bcast(0, 4, smax[0:1, 2:6], pb1)

    @block.vector
    def _(v):
        v.wait_ge(s_v.h, vb1)
        v.tensor_scalar(nmcol[:, 1:2], bc[:, 2:3], -MAGIC, None, ALU.mult)
        v.engine_nop().then_inc(s_v.h, 1)
    v_nm1 = s_v.inc(1)

    if _STOP_AFTER == "f1":
        def d():
            o = dbg_out("scug", [128, 8], F32)

            @block.gpsimd
            def _(g):
                g.wait_ge(s_v.h, v_nm1)
                g.dma_start(out=o[:, :], in_=bc[:, :]).then_inc(s_out.h, 16)
                g.wait_ge(s_out.h, s_out.inc(16))
        return finish(d)

    g2_keep = {}
    for m in range(8):
        g2_keep[m] = x_flat[:, m * T:(m + 1) * T]
    wupT_f = wupT[:, :, :].rearrange("p a b -> p (a b)").bitcast(F32)
    for m in range(6):
        g2_keep[8 + m] = wupT_f[:, m * T:(m + 1) * T]
    xbf_f = xbf_fm[:, :, :].rearrange("p a b -> p (a b)").bitcast(F32)
    for m in range(4):
        g2_keep[14 + m] = xbf_f[:, m * T:(m + 1) * T]

    pump_dq_tag("dn")
    assert not (dq_qA or dq_qB or dq_qC), "dq pieces left unpumped"
    free("dq_tr"); free("dq_bf"); free("dq_si"); free("dq_s")
    free("dq_bm"); free("dq_in")
    fr1 = alloc("fr1", [128, 3 * T], F32)
    fr2 = alloc("fr2", [128, 2 * T], F32)
    fr4 = alloc("fr4", [128, T], F32)

    def p3(buf, m):
        b = m % 3
        return buf[:, b * T:(b + 1) * T]

    def p2(buf, m):
        b = m % 2
        return buf[:, b * T:(b + 1) * T]

    pslc = p3

    def f2a_dst(m):
        return g2_keep[m] if m < N_KEEP else p2(fr2, m)

    f2a_ld = {}
    fA, fS, fR = {}, {}, {}
    fSV1, fS2, fS1, fSP = {}, {}, {}, {}
    for m in range(ICH):
        if m >= 1:
            fA[m - 1] = s_v.inc(1)
        if m >= 2:
            fR[m - 2] = s_v.inc(1)
        if m >= 1:
            fS[m - 1] = s_v.inc(1)
        fSV1[m] = s_s.inc(1)
        if m >= 1:
            fS2[m - 1] = s_s.inc(1)
        fS1[m] = s_s.inc(1)
    fS2[ICH - 1] = s_s.inc(1)
    fA[ICH - 1] = s_v.inc(1)
    fR[ICH - 2] = s_v.inc(1)
    fS[ICH - 1] = s_v.inc(1)
    fR[ICH - 1] = s_v.inc(1)
    for m in range(N_KEEP, ICH):
        fSP[m] = s_g2st.inc(16)
    for m in range(ICH):
        f2a_ld[m] = s_gr[m % 3].inc(16)

    @block.sync
    def _(sync):
        for m in range(ICH):
            if m == 0:
                sync.wait_ge(s_f1sp[("g", 0)].h, gt_spilled[0])
                sync.wait_ge(s_f1sp[("g", 1)].h, gt_spilled[1])
            if m >= 3:
                sync.wait_ge(s_v.h, fS[m - 3])
            sync.dma_start(out=p3(fr1, m),
                           in_=gate_h[m, :, :]).then_inc(s_gr[m % 3].h, 16)

    @block.gpsimd
    def _(g):
        for m in range(N_KEEP, ICH):
            g.wait_ge(s_v.h, fS[m])
            g.dma_start(out=g2_h[m, :, :], in_=p2(fr2, m)).then_inc(s_g2st.h, 16)

    @block.scalar
    def _(s):
        for m in range(ICH):
            s.wait_ge(s_gr[m % 3].h, f2a_ld[m])
            if m == 0:
                s.wait_ge(s_v.h, v_nm1)
            s.activation(out=p3(fr1, m), in_=p3(fr1, m), func=AF.Identity,
                         bias=ccol[:, 0:1], scale=bc[:, 3:4]).then_inc(s_s.h, 1)
            if m >= 1:
                s.wait_ge(s_s.h, fS1[m - 1])
                s.activation(out=p2(fr2, m - 1), in_=p2(fr2, m - 1),
                             func=AF.Identity, bias=ccol[:, 0:1],
                             scale=ccol[:, 2:3]).then_inc(s_s.h, 1)
            s.wait_ge(s_s.h, fSV1[m])
            if m >= 2:
                s.wait_ge(s_v.h, fA[m - 2])
                if m - 2 >= N_KEEP:
                    s.wait_ge(s_v.h, fR[m - 2])
                    s.wait_ge(s_g2st.h, fSP[m - 2])
            s.activation(out=p2(fr2, m), in_=p3(fr1, m), func=AF.Sigmoid,
                         bias=nmcol[:, 1:2], scale=bc[:, 2:3]).then_inc(s_s.h, 1)
        s.wait_ge(s_s.h, fS1[ICH - 1])
        s.activation(out=p2(fr2, ICH - 1), in_=p2(fr2, ICH - 1),
                     func=AF.Identity, bias=ccol[:, 0:1],
                     scale=ccol[:, 2:3]).then_inc(s_s.h, 1)

    @block.vector
    def _(v):
        for m in range(ICH + 1):
            if m >= 1:
                v.wait_ge(s_s.h, fS2[m - 1])
                v.tensor_scalar(fr4[:, :], p2(fr2, m - 1), MAGIC, 65535.0,
                                ALU.subtract, ALU.min).then_inc(s_v.h, 1)
            if m >= 2:
                v.wait_ge(s_v.h, fS[m - 2])
                v.tensor_reduce(am24[:, m - 2:m - 1], f2a_dst(m - 2), AX.X,
                                ALU.max,
                                apply_absolute_value=True).then_inc(s_v.h, 1)
            if m >= 1:
                v.wait_ge(s_v.h, fA[m - 1])
                v.scalar_tensor_tensor(f2a_dst(m - 1), p3(fr1, m - 1), MAGIC,
                                       fr4[:, :], ALU.subtract,
                                       ALU.mult).then_inc(s_v.h, 1)
        v.wait_ge(s_v.h, fS[ICH - 1])
        v.tensor_reduce(am24[:, ICH - 1:ICH], f2a_dst(ICH - 1), AX.X, ALU.max,
                        apply_absolute_value=True).then_inc(s_v.h, 1)
        v.wait_ge(s_v.h, fR[ICH - 1])
        v.tensor_reduce(am1[:, 0:1], am24[:, 0:ICH], AX.X, ALU.max)
        v.engine_nop().then_inc(s_v.h, 1)
    v_am2 = s_v.inc(1)

    def post_b2(v):
        inst = v.tensor_scalar(smax[0:1, 6:7], smax[0:1, 4:5], INV_2P16, None, ALU.mult)
        fence(v, s_v, inst)
        inst = v.tensor_tensor(smax[0:1, 8:9], smax[0:1, 0:1], smax[0:1, 6:7], ALU.mult)
        fence(v, s_v, inst)
        inst = v.tensor_scalar(smax[0:1, 7:8], smax[0:1, 8:9], INV_Q16, 1e-12,
                               ALU.mult, ALU.max)
        fence(v, s_v, inst)
        inst = v.reciprocal(smax[0:1, 8:9], smax[0:1, 7:8])
        fence(v, s_v, inst)
        v.tensor_tensor(smax[0:1, 9:10], smax[0:1, 6:7], smax[0:1, 8:9], ALU.mult)

    pb2 = barrier(2, 1, am1[:, 0:1], post_b2, extra_vwait=v_am2)
    vb2 = bcast(4, 1, smax[0:1, 9:10], pb2)

    if _STOP_AFTER == "f2a":
        def d():
            o = dbg_out("scg2", [1, 12], F32)

            @block.gpsimd
            def _(g):
                g.wait_ge(s_v.h, s_v.v)
                g.dma_start(out=o[:, :], in_=smax[0:1, :]).then_inc(s_out.h, 16)
                g.wait_ge(s_out.h, s_out.inc(16))
        return finish(d)

    def f3_src(m):
        return g2_keep[m] if m < N_KEEP else p3(fr1, m)

    f3_ldg, f3_ldu = {}, {}
    gA, gS, gR = {}, {}, {}
    gSV, gSU, gSP = {}, {}, {}
    for m in range(ICH):
        if m >= 1:
            gA[m - 1] = s_v.inc(1)
        if m >= 2:
            gR[m - 2] = s_v.inc(1)
        if m >= 1:
            gS[m - 1] = s_v.inc(1)
        gSV[m] = s_s.inc(1)
        gSU[m] = s_s.inc(1)
    gA[ICH - 1] = s_v.inc(1)
    gR[ICH - 2] = s_v.inc(1)
    gS[ICH - 1] = s_v.inc(1)
    gR[ICH - 1] = s_v.inc(1)
    for m in range(N_KEEP, ICH):
        gSP[m] = s_opst.inc(16)
    for m in range(ICH):
        if m >= N_KEEP:
            f3_ldg[m] = s_g2r[m % 2].inc(16)
        f3_ldu[m] = s_ur[m % 2].inc(16)

    @block.sync
    def _(sync):
        for m in range(ICH):
            if m == 0:
                sync.wait_ge(s_f1sp[("u", 0)].h, up_spilled[0])
                sync.wait_ge(s_f1sp[("u", 1)].h, up_spilled[1])
            if m < 2:
                sync.wait_ge(s_v.h, fR[ICH - 2 + m])
                sync.wait_ge(s_g2st.h, fSP[ICH - 2 + m])
            else:
                sync.wait_ge(s_v.h, gA[m - 2])
            sync.dma_start(out=p2(fr2, m),
                           in_=up_h[m, :, :]).then_inc(s_ur[m % 2].h, 16)

    @block.gpsimd
    def _(g):
        for m in range(N_KEEP, ICH):
            if m - 3 >= N_KEEP:
                g.wait_ge(s_v.h, gS[m - 3])
                g.dma_start(out=op_h[m - 3, :, :],
                            in_=p3(fr1, m - 3)).then_inc(s_opst.h, 16)
            g.wait_ge(s_v.h, gS[m - 3])
            if m - 3 >= N_KEEP:
                g.wait_ge(s_v.h, gR[m - 3])
                g.wait_ge(s_opst.h, gSP[m - 3])
            g.wait_ge(s_g2st.h, fSP[m])
            g.dma_start(out=p3(fr1, m),
                        in_=g2_h[m, :, :]).then_inc(s_g2r[m % 2].h, 16)
        for m in range(ICH - 3, ICH):
            g.wait_ge(s_v.h, gS[m])
            g.dma_start(out=op_h[m, :, :],
                        in_=p3(fr1, m)).then_inc(s_opst.h, 16)

    @block.scalar
    def _(s):
        for m in range(ICH):
            if m == 0:
                s.wait_ge(s_v.h, vb2)
            if m >= N_KEEP:
                s.wait_ge(s_g2r[m % 2].h, f3_ldg[m])
            elif m >= 3:
                s.wait_ge(s_v.h, gS[m - 3])
            s.activation(out=p3(fr1, m), in_=f3_src(m), func=AF.Identity,
                         bias=ccol[:, 0:1], scale=bc[:, 4:5]).then_inc(s_s.h, 1)
            s.wait_ge(s_ur[m % 2].h, f3_ldu[m])
            s.activation(out=p2(fr2, m), in_=p2(fr2, m), func=AF.Identity,
                         bias=ccol[:, 0:1], scale=bc[:, 1:2]).then_inc(s_s.h, 1)

    @block.vector
    def _(v):
        for m in range(ICH + 1):
            if m >= 1:
                v.wait_ge(s_s.h, gSU[m - 1])
                v.tensor_scalar(fr4[:, :], p2(fr2, m - 1), MAGIC, None,
                                ALU.subtract).then_inc(s_v.h, 1)
            if m >= 2:
                v.wait_ge(s_v.h, gS[m - 2])
                v.tensor_reduce(am24b[:, m - 2:m - 1], f3_src(m - 2), AX.X,
                                ALU.max,
                                apply_absolute_value=True).then_inc(s_v.h, 1)
            if m >= 1:
                v.wait_ge(s_v.h, gA[m - 1])
                v.scalar_tensor_tensor(f3_src(m - 1), p3(fr1, m - 1), MAGIC,
                                       fr4[:, :], ALU.subtract,
                                       ALU.mult).then_inc(s_v.h, 1)
        v.wait_ge(s_v.h, gS[ICH - 1])
        v.tensor_reduce(am24b[:, ICH - 1:ICH], f3_src(ICH - 1), AX.X, ALU.max,
                        apply_absolute_value=True).then_inc(s_v.h, 1)
        v.wait_ge(s_v.h, gR[ICH - 1])
        v.tensor_reduce(am1[:, 0:1], am24b[:, 0:ICH], AX.X, ALU.max)
        v.engine_nop().then_inc(s_v.h, 1)
    v_am3 = s_v.inc(1)
    free("fr4")
    wdn_buf = alloc("wdn_buf", [128, 2, 2, D], BF16, side="right")

    def post_b3(v):
        inst = v.tensor_tensor(smax[0:1, 10:11], smax[0:1, 7:8], smax[0:1, 2:3], ALU.mult)
        fence(v, s_v, inst)
        inst = v.tensor_tensor(smax[0:1, 11:12], smax[0:1, 0:1], smax[0:1, 10:11], ALU.mult)
        fence(v, s_v, inst)
        inst = v.tensor_scalar(smax[0:1, 11:12], smax[0:1, 11:12], INV_Q16, 1e-12,
                               ALU.mult, ALU.max)
        fence(v, s_v, inst)
        inst = v.reciprocal(smax[0:1, 0:1], smax[0:1, 11:12])
        fence(v, s_v, inst)
        v.tensor_tensor(smax[0:1, 10:11], smax[0:1, 10:11], smax[0:1, 0:1], ALU.mult)

    pb3 = barrier(3, 1, am1[:, 0:1], post_b3, extra_vwait=v_am3)
    vb3 = bcast(5, 2, smax[0:1, 10:12], pb3)

    @block.vector
    def _(v):
        v.wait_ge(s_v.h, vb3)
        v.tensor_scalar(nmcol[:, 2:3], bc[:, 6:7], -MAGIC, None, ALU.mult)
        v.engine_nop().then_inc(s_v.h, 1)
    v_nm3 = s_v.inc(1)

    x_bf_flat = x_flat.bitcast(BF16)
    wupT_bfflat = wupT[:, :, :].rearrange("p a b -> p (a b)")
    obf = [x_bf_flat[:, k * T:(k + 1) * T] for k in range(16)]
    obf += [wupT_bfflat[:, k * T:(k + 1) * T] for k in range(8)]

    ob_done = {}
    ob_vo, ob_ldo = {}, {}
    for k in range(ICH):
        ob_vo[k] = s_v.inc(1)
        ob_done[k] = s_s.inc(1)
    for k in range(N_KEEP, ICH):
        ob_ldo[k] = s_opr[k % 2].inc(16)

    @block.gpsimd
    def _(g):
        for k in range(N_KEEP, ICH):
            g.wait_ge(s_s.h, ob_done[k - 3])
            if k - 3 >= N_KEEP:
                g.wait_ge(s_opr[(k - 3) % 2].h, ob_ldo[k - 3])
            g.wait_ge(s_opst.h, gSP[k])
            g.dma_start(out=p3(fr1, k),
                        in_=op_h[k, :, :]).then_inc(s_opr[k % 2].h, 16)

    @block.vector
    def _(v):
        for k in range(ICH):
            if k == 0:
                v.wait_ge(s_v.h, v_nm3)
                v.wait_ge(s_opst.h, gSP[ICH - 1])
            if k >= N_KEEP:
                v.wait_ge(s_opr[k % 2].h, ob_ldo[k])
            if k >= 3:
                v.wait_ge(s_s.h, ob_done[k - 3])
            v.tensor_scalar(p3(fr1, k), f3_src(k), bc[:, 5:6], MAGIC,
                            ALU.mult, ALU.add).then_inc(s_v.h, 1)

    @block.scalar
    def _(s):
        for k in range(ICH):
            s.wait_ge(s_v.h, ob_vo[k])
            s.activation(out=obf[k], in_=p3(fr1, k), func=AF.Identity,
                         bias=nmcol[:, 2:3], scale=bc[:, 6:7]).then_inc(s_s.h, 1)

    free("fr2")
    free("fr1")
    xlnrb = alloc("xlnrb", [128, 4, D], F32)
    y_stage = alloc("y_stage", [128, 4, D], F32)

    xln_rb = {}
    f3b_t = {}
    f3b_res = {}
    yout = {}
    wd_ld = {}
    wd_used = {}

    for w in range(4):
        @block.sync
        def _(sync, w=w):
            if w == 0:
                sync.wait_ge(s_xsp.h, 16)
                sync.wait_ge(s_s.h, ob_done[ICH - 1])
                sync.wait_ge(s_opr[0].h, ob_ldo[ICH - 2])
                sync.wait_ge(s_opr[1].h, ob_ldo[ICH - 1])
            else:
                sync.wait_ge(s_v.h, f3b_res[(w - 1, 3)])
            sync.dma_start(out=xlnrb[:, :, :],
                           in_=xln_h[w * 4:(w + 1) * 4, :, :].transpose([1, 0, 2])).then_inc(s_xl[w % 2].h, 16)
        xln_rb[w] = s_xl[w % 2].inc(16)

        for kg in range(12):
            gidx = w * 12 + kg

            @block.sync
            def _(sync, w=w, kg=kg, gidx=gidx):
                if gidx == 0:
                    sync.wait_ge(s_dqst["dn"].h, dq_last["dn"])
                    sync.wait_ge(s_t.h, t_f1_done)
                    sync.wait_ge(s_v.h, gS[ICH - 1])
                if gidx >= 2:
                    sync.wait_ge(s_t.h, wd_used[gidx - 2])
                sync.dma_start(out=wdn_buf[:, gidx % 2, :, :],
                               in_=wdnT_h[kg * 2:(kg + 1) * 2, :, :].transpose([1, 0, 2])).then_inc(s_wd[gidx % 2].h, 16)
            wd_ld[gidx] = s_wd[gidx % 2].inc(16)

            @block.tensor
            def _(t, w=w, kg=kg, gidx=gidx):
                if kg == 0:
                    for pi in range(8):
                        bank_wait(t, pi)
                t.wait_ge(s_wd[gidx % 2].h, wd_ld[gidx])
                for kk in range(2):
                    k = kg * 2 + kk
                    t.wait_ge(s_s.h, ob_done[k])
                    for m in range(4):
                        for n in range(2):
                            mm = t.matmul(ps[m * 2 + n][:, :],
                                          obf[k][:, (w * 4 + m) * 128:(w * 4 + m + 1) * 128],
                                          wdn_buf[:, gidx % 2, kk, n * 512:(n + 1) * 512],
                                          start=(k == 0), stop=(k == ICH - 1))
                mm.then_inc(s_t.h, 1)
            wd_used[gidx] = s_t.inc(1)
        f3b_t[w] = wd_used[w * 12 + 11]

        @block.vector
        def _(v, w=w):
            v.wait_ge(s_t.h, f3b_t[w])
            v.wait_ge(s_xl[w % 2].h, xln_rb[w])
            for m in range(4):
                if w >= 1:
                    v.wait_ge(s_tr.h, yout[(w - 1) * 4 + 3])
                for n in range(2):
                    inst = v.tensor_tensor(y_stage[:, m, n * 512:(n + 1) * 512],
                                           ps[m * 2 + n][:, :],
                                           xlnrb[:, m, n * 512:(n + 1) * 512],
                                           ALU.add)
                fence(v, s_v, inst)
                v.bn_stats(bn6[:, m, 0:6], y_stage[:, m, 0:512])
                inst = v.bn_stats(bn6[:, m, 6:12], y_stage[:, m, 512:1024])
                fence(v, s_v, inst)
                v.bn_aggr(mvw[:, m, :], bn6[:, m, 0:12])
                v.engine_nop().then_inc(s_v.h, 1)
                f3b_res[(w, m)] = s_v.inc(1)

        for pi in range(8):
            bank_free[pi] = ("v", f3b_res[(w, 3)])

        @block.scalar
        def _(s, w=w):
            s.wait_ge(s_v.h, f3b_res[(w, 3)])
            s.activation(out=sdw[:, :], in_=mvw[:, :, 1], func=AF.Sqrt,
                         bias=ccol[:, 1:2], scale=1.0).then_inc(s_s.h, 1)
        s_sdw = s_s.inc(1)

        @block.vector
        def _(v, w=w, s_sdw=s_sdw):
            v.wait_ge(s_s.h, s_sdw)
            inst = v.reciprocal(rstdw[:, :], sdw[:, :])
            fence(v, s_v, inst)
            v.scalar_tensor_tensor(nmrw[:, :], mvw[:, :, 0], -1.0, rstdw[:, :],
                                   ALU.mult, ALU.mult)
            v.engine_nop().then_inc(s_v.h, 1)
        v_rw = s_v.inc(1)

        for m in range(4):
            @block.scalar
            def _(s, w=w, m=m, v_rw=v_rw):
                if m == 0:
                    s.wait_ge(s_v.h, v_rw)
                b = m
                inst = s.activation(out=y_stage[:, b, :], in_=y_stage[:, b, :],
                                    func=AF.Identity, bias=nmrw[:, m:m + 1],
                                    scale=rstdw[:, m:m + 1])
                fence(s, s_s, inst)
                tc = w * 4 + m
                s.dma_start(out=y_d[tc * 128:(tc + 1) * 128, :],
                            in_=y_stage[:, b, :]).then_inc(s_tr.h, 16)
            yout[w * 4 + m] = s_tr.inc(16)

    @block.sync
    def _(sync):
        sync.wait_ge(s_tr.h, yout[15])

    return finish()


_CACHE = {}


def _get_program():
    key = (_STOP_AFTER, tuple(sorted(_DBG)))
    if key not in _CACHE:
        _CACHE[key] = build_program()
    return _CACHE[key]


def kernel(L0_attn_x, L0_attn_wg_w, L0_attn_wg_b, L0_attn_wq_w, L0_attn_wq_b,
           L0_attn_wk_w, L0_attn_wk_b, L0_attn_wv_w, L0_attn_wv_b,
           L0_attn_out_w, L0_attn_out_b, L0_ffn_w_gate, L0_ffn_w_up,
           L0_ffn_w_down):
    nc, _ = _get_program()
    x = np.ascontiguousarray(np.asarray(L0_attn_x, np.float32))
    com = {
        "wq": np.ascontiguousarray(L0_attn_wq_w, np.float32),
        "wk": np.ascontiguousarray(L0_attn_wk_w, np.float32),
        "wv": np.ascontiguousarray(L0_attn_wv_w, np.float32),
        "wg": np.ascontiguousarray(L0_attn_wg_w, np.float32),
        "wo": np.ascontiguousarray(L0_attn_out_w, np.float32),
        "bq": np.ascontiguousarray(L0_attn_wq_b, np.float32).reshape(1, D),
        "bk": np.ascontiguousarray(L0_attn_wk_b, np.float32).reshape(1, D),
        "bv": np.ascontiguousarray(L0_attn_wv_b, np.float32).reshape(1, D),
        "bg": np.ascontiguousarray(L0_attn_wg_b, np.float32).reshape(1, D),
        "bo": np.ascontiguousarray(L0_attn_out_b, np.float32).reshape(1, D),
        "wup": np.ascontiguousarray(L0_ffn_w_up, np.float32),
        "wgt": np.ascontiguousarray(L0_ffn_w_gate, np.float32),
        "wdn": np.ascontiguousarray(L0_ffn_w_down, np.float32),
        "ident": np.eye(128, dtype=np.float32),
    }
    in_maps = [dict(com, x=np.ascontiguousarray(x[b])) for b in range(8)]
    res = run_bass_kernel_spmd(nc, in_maps, core_ids=list(range(8)))
    return np.stack([res.results[b]["y"] for b in range(8)], axis=0)


# version 13
# speedup vs baseline: 1.0302x; 1.0223x over previous
import numpy as np
import concourse.bass as bass
import concourse.mybir as mybir
from concourse.bass_utils import run_bass_kernel_spmd

F32 = mybir.dt.float32
BF16 = mybir.dt.bfloat16
AF = mybir.ActivationFunctionType
ALU = mybir.AluOpType
AX = mybir.AxisListType

MAGIC = float(np.float32(1.5 * 2 ** 23))
INV_Q16 = float(np.float32(1.0 / 32767.0))
INV_QW = float(np.float32(1.0 / 7.0))
INV_2P16 = float(np.float32(1.0 / 65536.0))

T = 2048
D = 1024
TCH = 16
FCH = 8
II = 3072
ICH = 24
NPAIR = 8
N_KEEP = 18

_DBG = set()
_STOP_AFTER = None


class _Sem:
    def __init__(self, nc, name):
        self.h = nc.semaphore(name).__enter__()
        self.v = 0

    def inc(self, n=1):
        self.v += n
        return self.v


def build_program():
    nc = bass.Bass("TRN2", target_bir_lowering=False, debug=False,
                   detect_race_conditions=False)

    x_d = nc.dram_tensor("x", [T, D], F32, kind="ExternalInput").ap()
    ident_d = nc.dram_tensor("ident", [128, 128], F32, kind="ExternalInput").ap()
    w_d = {nm: nc.dram_tensor(nm, [D, D], F32, kind="ExternalInput").ap()
           for nm in ("wq", "wk", "wv", "wg", "wo")}
    b_d = {nm: nc.dram_tensor(nm, [1, D], F32, kind="ExternalInput").ap()
           for nm in ("bq", "bk", "bv", "bg", "bo")}
    wup_d = nc.dram_tensor("wup", [II, D], F32, kind="ExternalInput").ap()
    wgt_d = nc.dram_tensor("wgt", [II, D], F32, kind="ExternalInput").ap()
    wdn_d = nc.dram_tensor("wdn", [D, II], F32, kind="ExternalInput").ap()
    y_d = nc.dram_tensor("y", [T, D], F32, kind="ExternalOutput").ap()

    wupT_h = nc.dram_tensor("wupT_h", [FCH, 128, II], BF16).ap()
    wgtT_h = nc.dram_tensor("wgtT_h", [FCH, 128, II], BF16).ap()
    wdnT_h = nc.dram_tensor("wdnT_h", [ICH, 128, D], BF16).ap()
    up_h = nc.dram_tensor("up_h", [ICH, 128, T], F32).ap()
    gate_h = nc.dram_tensor("gate_h", [ICH, 128, T], F32).ap()
    g2_h = nc.dram_tensor("g2_h", [ICH, 128, T], F32).ap()
    op_h = nc.dram_tensor("op_h", [ICH, 128, T], F32).ap()
    xln_h = nc.dram_tensor("xln_h", [TCH, 128, D], F32).ap()
    cc_in = [nc.dram_tensor(f"cc_in{k}", [2, 8], F32).ap() for k in range(4)]
    cc_out = [nc.dram_tensor(f"cc_out{k}", [16, 8], F32).ap() for k in range(4)]

    dbg_d = {}

    def dbg_out(name, shape, dtype=F32):
        dbg_d[name] = nc.dram_tensor("dbg_" + name, shape, dtype,
                                     kind="ExternalOutput").ap()
        return dbg_d[name]

    live = {}

    def alloc(name, shape, dtype, side="left"):
        cm = nc.sbuf_tensor(name, shape, dtype, side=side)
        t = cm.__enter__()
        live[name] = cm
        return t

    def free(name):
        live.pop(name).__exit__(None, None, None)

    ident = alloc("identt", [128, 128], F32)
    ones_f = alloc("ones_f", [1, 128], F32)
    zbuf = alloc("zbuf", [128, 16], F32)
    zs32 = alloc("zs32", [128, 32], F32)
    ccol = alloc("ccol", [128, 3], F32)
    ident_bf = alloc("ident_bf", [128, 128], BF16)
    zrow = alloc("zrow", [2, 8], F32)
    zsum = alloc("zsum", [128, 16], F32)
    bn6 = alloc("bn6", [128, TCH, 12], F32)
    mvall = alloc("mvall", [128, TCH, 2], F32)
    sdall = alloc("sdall", [128, TCH], F32)
    rstd = alloc("rstd", [128, TCH], F32)
    nmr = alloc("nmr", [128, TCH], F32)
    am24 = alloc("am24", [128, ICH], F32)
    am24b = alloc("am24b", [128, ICH], F32)
    am1 = alloc("am1", [128, 2], F32)
    lmax = alloc("lmax", [2, 1], F32)
    gth = alloc("gth", [1, 16], F32)
    smax = alloc("smax", [1, 12], F32)
    bc = alloc("bc", [128, 8], F32)
    nmcol = alloc("nmcol", [128, 4], F32)
    mvw = alloc("mvw", [128, 4, 2], F32)
    sdw = alloc("sdw", [128, 4], F32)
    rstdw = alloc("rstdw", [128, 4], F32)
    nmrw = alloc("nmrw", [128, 4], F32)
    PW = 512
    DQNB = PW // 32
    dq_in = alloc("dq_in", [128, 4, PW], F32, side="right")
    dq_bm = alloc("dq_bm", [128, 4, DQNB], F32, side="right")
    dq_s = alloc("dq_s", [128, 4, DQNB], F32, side="right")
    dq_si = alloc("dq_si", [128, 4, DQNB], F32, side="right")
    dq_bf = alloc("dq_bf", [128, 4, PW], BF16, side="right")
    dq_tr = alloc("dq_tr", [128, 4, PW // 128, 128], BF16, side="right")
    wslotA = alloc("wslotA", [128, FCH, D], BF16, side="right")
    ones_bf = alloc("ones_bf", [1, T], BF16, side="right")
    ones_col = alloc("ones_col", [128, 1], BF16, side="right")
    bias_sb = alloc("bias_sb", [1, 5, D], BF16, side="right")
    kvaug = alloc("kvaug", [128, NPAIR, 128], BF16, side="right")
    zk = alloc("zk", [128, 32], BF16, side="right")
    fs_a = alloc("fs_a", [128, 1024], F32, side="right")
    fs_b = alloc("fs_b", [128, 1024], F32, side="right")
    wslotB = alloc("wslotB", [128, FCH, D], BF16, side="right")

    ps = [nc.psum_tensor(f"ps{i}", [128, 512], F32).__enter__() for i in range(8)]

    s_ld = _Sem(nc, "s_ld")
    s_tr = _Sem(nc, "s_tr")
    s_gd = _Sem(nc, "s_gd")
    s_v = _Sem(nc, "s_v")
    s_s = _Sem(nc, "s_s")
    s_t = _Sem(nc, "s_t")
    s_g = _Sem(nc, "s_g")
    s_cc = _Sem(nc, "s_cc")
    s_out = _Sem(nc, "s_out")
    s_id = _Sem(nc, "s_id")
    s_gx = _Sem(nc, "s_gx")
    s_gb = _Sem(nc, "s_gb")
    s_gwA = _Sem(nc, "s_gwA")
    s_gwB = _Sem(nc, "s_gwB")
    s_dqld = [_Sem(nc, f"s_dqld{i}") for i in range(4)]
    s_dqtr = _Sem(nc, "s_dqtr")
    s_dqtr2 = [_Sem(nc, "s_dqtrA"), _Sem(nc, "s_dqtrB")]
    s_dqst = {"up": _Sem(nc, "s_dqstU"), "gt": _Sem(nc, "s_dqstG"),
              "dn": _Sem(nc, "s_dqstD")}
    s_x2 = _Sem(nc, "s_x2")
    s_wup = _Sem(nc, "s_wup")
    s_wgt = _Sem(nc, "s_wgt")
    s_wdn = _Sem(nc, "s_wdn")
    s_f1sp = {("u", 0): _Sem(nc, "s_f1spU0"), ("u", 1): _Sem(nc, "s_f1spU1"),
              ("g", 0): _Sem(nc, "s_f1spG0"), ("g", 1): _Sem(nc, "s_f1spG1")}
    s_gr = [_Sem(nc, "s_gr0"), _Sem(nc, "s_gr1"), _Sem(nc, "s_gr2")]
    s_g2st = _Sem(nc, "s_g2st")
    s_ur = [_Sem(nc, "s_ur0"), _Sem(nc, "s_ur1"), _Sem(nc, "s_ur2")]
    s_g2r = [_Sem(nc, "s_g2r0"), _Sem(nc, "s_g2r1")]
    s_opst = _Sem(nc, "s_opst")
    s_opr = [_Sem(nc, "s_opr0"), _Sem(nc, "s_opr1")]
    s_xl = [_Sem(nc, "s_xl0"), _Sem(nc, "s_xl1")]
    s_wd = [_Sem(nc, "s_wd0"), _Sem(nc, "s_wd1")]
    s_dv = _Sem(nc, "s_dv")
    s_xsp = _Sem(nc, "s_xsp")
    all_dma_sems = ([s_ld, s_tr, s_gd, s_id, s_gx, s_gb, s_gwA, s_gwB, s_dqtr,
                     s_x2, s_wup, s_wgt, s_wdn, s_g2st, s_opst, s_xsp]
                    + s_dqld + list(s_dqst.values()) + list(s_f1sp.values())
                    + s_gr + s_ur + s_g2r + s_opr + s_xl + s_wd + s_dqtr2)

    semmap = {"v": s_v, "s": s_s, "g": s_g, "t": s_t, "ld": s_ld, "tr": s_tr}
    bank_free = [("v", 0)] * 8

    def bank_wait(eng, i):
        nm, val = bank_free[i]
        if val > 0:
            eng.wait_ge(semmap[nm].h, val)

    def fence(eng, sem, inst):
        inst.then_inc(sem.h, 1)
        eng.wait_ge(sem.h, sem.inc(1))

    def pe_transpose(dst_fm, src_tm, banks, pre_twait=None, idx_wait=None):
        base_t = s_t.v
        base_v = s_v.v
        nblk = TCH * FCH

        @block.tensor
        def _(t):
            if pre_twait is not None:
                pre_twait(t)
            for pb in banks:
                bank_wait(t, pb)
            for idx in range(nblk):
                c, f = divmod(idx, FCH)
                pb = banks[idx % 2]
                if idx_wait is not None:
                    idx_wait(t, c, f)
                if idx >= 2:
                    t.wait_ge(s_v.h, base_v + idx - 1)
                t.transpose(ps[pb][:, 0:64].bitcast(BF16),
                            src_tm[:, c, f * 128:(f + 1) * 128],
                            ident_bf[:, :]).then_inc(s_t.h, 1)
                s_t.inc(1)

        @block.vector
        def _(v):
            for idx in range(nblk):
                c, f = divmod(idx, FCH)
                pb = banks[idx % 2]
                v.wait_ge(s_t.h, base_t + idx + 1)
                v.tensor_copy(dst_fm[:, f, c * 128:(c + 1) * 128],
                              ps[pb][:, 0:64].bitcast(BF16)).then_inc(s_v.h, 1)
                s_v.inc(1)
        done = s_v.v
        for pb in banks:
            bank_free[pb] = ("v", done)
        return done

    blk_cm = nc.Block()
    block = blk_cm.__enter__()

    def finish(dump_emitter=None):
        if dump_emitter is not None:
            dump_emitter()
        blk_cm.__exit__(None, None, None)
        return nc, dbg_d

    def dump_tm(tag, buf_ap3, nch, w, dtype=F32):
        o = dbg_out(tag, [nch * 128, w], dtype)
        waits = [(s_v, s_v.v), (s_s, s_s.v), (s_t, s_t.v), (s_g, s_g.v)]
        waits += [(sm, sm.v) for sm in all_dma_sems]

        @block.gpsimd
        def _(g):
            for sm, val in waits:
                if val:
                    g.wait_ge(sm.h, val)
            g.dma_start(out=o.rearrange("(c p) w -> p c w", p=128),
                        in_=buf_ap3).then_inc(s_out.h, 16)
            g.wait_ge(s_out.h, s_out.inc(16))

    xbf_fm = alloc("xbf_fm", [128, FCH, T], BF16)
    xbf_tm = alloc("xbf_tm", [128, TCH, D], BF16, side="right")

    @block.sync
    def _(sync):
        sync.dma_start(out=ident[:], in_=ident_d[:, :]).then_inc(s_id.h, 16)
    ld_ident = s_id.inc(16)

    @block.gpsimd
    def _(g):
        g.dma_start(out=xbf_tm[:, :, :],
                    in_=x_d.rearrange("(t p) d -> p t d", p=128)).then_inc(s_gx.h, 16)
        for i, nm in enumerate(("bq", "bk", "bv", "bg", "bo")):
            g.dma_start(out=bias_sb[0:1, i, :], in_=b_d[nm][:, :]).then_inc(s_gb.h, 16)
    gd_x = s_gx.inc(16)
    gd_bias = s_gb.inc(16 * 5)

    wslot = [wslotA, wslotB]
    wname2slot = {"wk": 0, "wv": 1, "wg": 0, "wq": 1, "wo": 0}
    gd_w = {}
    w_guard = {}

    wsem = {0: s_gwA, 1: s_gwB}

    def emit_wload(nm):
        slot_sem = wsem[wname2slot[nm]]

        @block.gpsimd
        def _(g):
            gd = w_guard.get(nm)
            if gd is not None:
                g.wait_ge(semmap[gd[0]].h, gd[1])
            g.dma_start(out=wslot[wname2slot[nm]][:, :, :],
                        in_=w_d[nm].rearrange("(k p) d -> p k d", p=128)).then_inc(slot_sem.h, 16)
        gd_w[nm] = (slot_sem, slot_sem.inc(16))

    emit_wload("wk")
    emit_wload("wv")

    @block.vector
    def _(v):
        v.memset(ones_bf[:], 1.0)
        v.memset(ones_f[:], 1.0)
        v.memset(ones_col[:], 1.0)
        v.memset(zk[:], 0.0)
        v.memset(kvaug[:, :, :], 0.0)
        v.memset(ccol[:, 0:1], MAGIC)
        v.memset(ccol[:, 1:2], 1e-5)
        v.memset(ccol[:, 2:3], 65536.0)
        v.memset(zrow[:, :], 0.0)
        v.wait_ge(s_id.h, ld_ident)
        v.tensor_copy(ident_bf[:, :], ident[:, :])
        v.engine_nop().then_inc(s_v.h, 1)
    v_const = s_v.inc(1)

    @block.sync
    def _(sync):
        sync.wait_ge(s_v.h, v_const)
        for k in range(4):
            sync.dma_start(out=cc_in[k][:, :], in_=zrow[:, :]).then_inc(s_id.h, 16)
    ld_cc0 = s_id.inc(16 * 4)

    tr_xbf = pe_transpose(xbf_fm, xbf_tm, (0, 1),
                          pre_twait=lambda t: (t.wait_ge(s_gx.h, gd_x),
                                               t.wait_ge(s_v.h, v_const)))
    free("xbf_tm")

    if _STOP_AFTER == "xbf":
        return finish(lambda: dump_tm("xbf", xbf_fm[:, :, :], FCH, T, BF16))

    dq_state = {}
    dq_qA = []
    dq_qB = []
    dq_qC = []
    dq_last = {}
    dq_total = [0]

    def make_dq_pieces(tag, w_src, n_och, cl, out_h):
        npiece = cl // PW
        ntr = PW // 128

        def stageA(c, pc, i):
            b = i % 4

            @block.sync
            def _(sync, c=c, pc=pc, b=b, i=i):
                if i >= 4:
                    sync.wait_ge(s_g.h, dq_state[("gdq", i - 4)])
                sync.dma_start(out=dq_in[:, b, :],
                               in_=w_src[c * 128:(c + 1) * 128,
                                         pc * PW:(pc + 1) * PW]).then_inc(s_dqld[b].h, 16)
                dq_state[("ld", i)] = s_dqld[b].inc(16)

        def stageB(c, pc, i):
            b = i % 4

            @block.vector
            def _(v, b=b, i=i):
                v.wait_ge(s_dqld[b].h, dq_state[("ld", i)])
                v.tensor_reduce(dq_bm[:, b, :],
                                dq_in[:, b, :].rearrange("p (n q) -> p n q", q=32),
                                AX.X, ALU.max,
                                apply_absolute_value=True).then_inc(s_dv.h, 1)
                if i >= 1:
                    b1 = (i - 1) % 4
                    v.wait_ge(s_dv.h, dq_state[("red", i - 1)])
                    v.tensor_scalar(dq_s[:, b1, :], dq_bm[:, b1, :], INV_QW,
                                    1e-12, ALU.mult,
                                    ALU.max).then_inc(s_dv.h, 1)
                if i >= 2:
                    b2 = (i - 2) % 4
                    v.wait_ge(s_dv.h, dq_state[("tsd", i - 2)])
                    v.reciprocal(dq_si[:, b2, :], dq_s[:, b2, :])
                    v.engine_nop().then_inc(s_dv.h, 1)
            dq_state[("red", i)] = s_dv.inc(1)
            if i >= 1:
                dq_state[("tsd", i - 1)] = s_dv.inc(1)
            if i >= 2:
                dq_state[("vsc", i - 2)] = s_dv.inc(1)

            if i >= 2:
                j = i - 2

                @block.gpsimd
                def _(g, j=j):
                    g.wait_ge(s_dv.h, dq_state[("vsc", j)])
                    if j >= 4:
                        g.wait_ge(s_dqtr2[j % 2].h, dq_state[("tr", j - 4)])
                    bj = j % 4
                    sib = dq_si[:, bj, :].unsqueeze(2).broadcast_to([128, DQNB, 32])
                    w3 = dq_in[:, bj, :].rearrange("p (n q) -> p n q", q=32)
                    g.tensor_tensor(w3, w3, sib, ALU.mult).then_inc(s_g.h, 1)
                dq_state[("gmul", j)] = s_g.inc(1)

        def stageC(c, pc, i):
            b = i % 4

            @block.vector
            def _(v, b=b, i=i):
                v.wait_ge(s_g.h, dq_state[("gmul", i)])
                v.tensor_scalar(dq_in[:, b, :], dq_in[:, b, :], MAGIC, MAGIC,
                                ALU.add, ALU.subtract).then_inc(s_dv.h, 1)
            dq_state[("vrnd", i)] = s_dv.inc(1)

            @block.gpsimd
            def _(g, b=b, i=i):
                g.wait_ge(s_dv.h, dq_state[("vrnd", i)])
                sb_ = dq_s[:, b, :].unsqueeze(2).broadcast_to([128, DQNB, 32])
                w3 = dq_in[:, b, :].rearrange("p (n q) -> p n q", q=32)
                bf3 = dq_bf[:, b, :].rearrange("p (n q) -> p n q", q=32)
                g.tensor_tensor(bf3, w3, sb_, ALU.mult)
                g.engine_nop().then_inc(s_g.h, 1)
            dq_state[("gdq", i)] = s_g.inc(1)

            if b % 2 == 0:
                @block.sync
                def _(q, b=b, i=i):
                    q.wait_ge(s_g.h, dq_state[("gdq", i)])
                    for f in range(ntr):
                        q.dma_start_transpose(
                            dq_tr[:, b, f, :],
                            dq_bf[:, b, f * 128:(f + 1) * 128]).then_inc(s_dqtr2[b % 2].h, 16)
            else:
                @block.sync
                def _(q, b=b, i=i):
                    q.wait_ge(s_g.h, dq_state[("gdq", i)])
                    for f in range(ntr):
                        q.dma_start_transpose(
                            dq_tr[:, b, f, :],
                            dq_bf[:, b, f * 128:(f + 1) * 128]).then_inc(s_dqtr2[b % 2].h, 16)
            dq_state[("tr", i)] = s_dqtr2[b % 2].inc(16 * ntr)

            @block.sync
            def _(sync, c=c, pc=pc, b=b, i=i):
                sync.wait_ge(s_dqtr2[b % 2].h, dq_state[("tr", i)])
                sync.dma_start(
                    out=out_h[pc * ntr:(pc + 1) * ntr, :,
                              c * 128:(c + 1) * 128].transpose([1, 0, 2]),
                    in_=dq_tr[:, b, 0:ntr, :]).then_inc(s_dqst[tag].h, 16)
            dq_last[tag] = s_dqst[tag].inc(16)

        for c in range(n_och):
            for pc in range(npiece):
                i = dq_total[0]
                dq_qA.append((tag, stageA, c, pc, i))
                dq_qB.append((tag, stageB, c, pc, i))
                dq_qC.append((tag, stageC, c, pc, i))
                dq_total[0] += 1

    def drain_B():
        N = dq_total[0]
        for vi in (N, N + 1):
            @block.vector
            def _(v, vi=vi, N=N):
                if vi - 1 < N:
                    b1 = (vi - 1) % 4
                    v.wait_ge(s_dv.h, dq_state[("red", vi - 1)])
                    v.tensor_scalar(dq_s[:, b1, :], dq_bm[:, b1, :], INV_QW,
                                    1e-12, ALU.mult,
                                    ALU.max).then_inc(s_dv.h, 1)
                b2 = (vi - 2) % 4
                v.wait_ge(s_dv.h, dq_state[("tsd", vi - 2)])
                v.reciprocal(dq_si[:, b2, :], dq_s[:, b2, :])
                v.engine_nop().then_inc(s_dv.h, 1)
            if vi - 1 < N:
                dq_state[("tsd", vi - 1)] = s_dv.inc(1)
            dq_state[("vsc", vi - 2)] = s_dv.inc(1)

            j = vi - 2

            @block.gpsimd
            def _(g, j=j):
                g.wait_ge(s_dv.h, dq_state[("vsc", j)])
                if j >= 4:
                    g.wait_ge(s_dqtr2[j % 2].h, dq_state[("tr", j - 4)])
                bj = j % 4
                sib = dq_si[:, bj, :].unsqueeze(2).broadcast_to([128, DQNB, 32])
                w3 = dq_in[:, bj, :].rearrange("p (n q) -> p n q", q=32)
                g.tensor_tensor(w3, w3, sib, ALU.mult).then_inc(s_g.h, 1)
            dq_state[("gmul", j)] = s_g.inc(1)

    dq_drained = [False]

    def pump_dq(k):
        for _ in range(k):
            total0 = dq_total[0]
            cA = total0 - len(dq_qA)
            cB = total0 - len(dq_qB)
            cC = total0 - len(dq_qC)
            if dq_qA and cA <= cC + 3:
                tg, fn, c, pc, i = dq_qA.pop(0)
                fn(c, pc, i)
                cA += 1
            if dq_qB and cB < cA and cB <= cC + 3:
                tg, fn, c, pc, i = dq_qB.pop(0)
                fn(c, pc, i)
                cB += 1
                if not dq_qB and not dq_drained[0]:
                    drain_B()
                    dq_drained[0] = True
            if dq_qC and (cC <= cB - 3 or (dq_drained[0] and cC < cB)):
                tg, fn, c, pc, i = dq_qC.pop(0)
                fn(c, pc, i)

    def pump_dq_tag(tag):
        while ((dq_qA and dq_qA[0][0] == tag) or (dq_qB and dq_qB[0][0] == tag)
               or (dq_qC and dq_qC[0][0] == tag)):
            pump_dq(1)

    k_tm = alloc("k_tm", [128, TCH, D], BF16)
    v_tm = alloc("v_tm", [128, TCH, D], BF16)
    tdone = {}

    def run_tm_proj(wname, bidx, nbank, kind, dst):
        bi = "qkvgo".index(wname[1])
        slot = wslot[wname2slot[wname]]
        t0, v0, s0 = s_t.v, s_v.v, s_s.v
        ntile = TCH * 2

        def tdn(j):
            return t0 + j + 1

        if kind == "fm":
            def sexp(j): return s0 + 2 * j + 2
            def stt(j): return v0 + j + 1
            rel = ("v", stt)
        else:
            def scp(j): return s0 + j + 1
            rel = ("s", scp)

        GT = 8
        for g0 in range(0, ntile, GT):
            @block.tensor
            def _(t, g0=g0):
                if g0 == 0:
                    t.wait_ge(s_v.h, tr_xbf)
                    t.wait_ge(gd_w[wname][0].h, gd_w[wname][1])
                    t.wait_ge(s_gb.h, gd_bias)
                    t.wait_ge(s_v.h, v_const)
                for j in range(g0, g0 + GT):
                    m, n = divmod(j, 2)
                    pi = bidx + j % nbank
                    if j < nbank:
                        bank_wait(t, pi)
                    else:
                        t.wait_ge(semmap[rel[0]].h, rel[1](j - nbank))
                    for kc in range(FCH):
                        t.matmul(ps[pi][:, :],
                                 xbf_fm[:, kc, m * 128:(m + 1) * 128],
                                 slot[:, kc, n * 512:(n + 1) * 512],
                                 start=(kc == 0), stop=False)
                    t.matmul(ps[pi][:, :],
                             ones_bf[0:1, 0:128],
                             bias_sb[0:1, bi, n * 512:(n + 1) * 512],
                             start=False, stop=True).then_inc(s_t.h, 1)
                    s_t.inc(1)
                    tdone[(wname, m, n)] = s_t.v

            if kind == "fm":
                @block.scalar
                def _(s, g0=g0):
                    for j in range(g0, g0 + GT):
                        if j >= 1:
                            jj = j - 1
                            s.wait_ge(s_s.h, s0 + 2 * jj + 1)
                            if jj >= 2:
                                s.wait_ge(s_v.h, v0 + jj - 1)
                            s.activation(out=fs_b[:, (jj % 2) * 512:(jj % 2) * 512 + 512],
                                         in_=fs_a[:, (jj % 2) * 512:(jj % 2) * 512 + 512],
                                         func=AF.Exp, scale=-1.0).then_inc(s_s.h, 1)
                            s_s.inc(1)
                        s.wait_ge(s_t.h, tdn(j))
                        s.activation(out=fs_a[:, (j % 2) * 512:(j % 2) * 512 + 512],
                                     in_=ps[bidx + j % nbank][:, :],
                                     func=AF.Relu, scale=-1.0).then_inc(s_s.h, 1)
                        s_s.inc(1)
                    if g0 + GT == ntile:
                        jj = ntile - 1
                        s.wait_ge(s_s.h, s0 + 2 * jj + 1)
                        s.wait_ge(s_v.h, v0 + jj - 1)
                        s.activation(out=fs_b[:, (jj % 2) * 512:(jj % 2) * 512 + 512],
                                     in_=fs_a[:, (jj % 2) * 512:(jj % 2) * 512 + 512],
                                     func=AF.Exp, scale=-1.0).then_inc(s_s.h, 1)
                        s_s.inc(1)

                @block.vector
                def _(v, g0=g0):
                    for j in range(g0, g0 + GT):
                        m, n = divmod(j, 2)
                        pi = bidx + j % nbank
                        v.wait_ge(s_s.h, sexp(j))
                        v.scalar_tensor_tensor(dst[:, m, n * 512:(n + 1) * 512],
                                               ps[pi][:, :], 0.0,
                                               fs_b[:, (j % 2) * 512:(j % 2) * 512 + 512],
                                               ALU.max, ALU.add).then_inc(s_v.h, 1)
                        s_v.inc(1)
            else:
                @block.scalar
                def _(s, g0=g0):
                    for j in range(g0, g0 + GT):
                        m, n = divmod(j, 2)
                        pi = bidx + j % nbank
                        s.wait_ge(s_t.h, tdn(j))
                        s.activation(out=dst[:, m, n * 512:(n + 1) * 512],
                                     in_=ps[pi][:, :], func=kind).then_inc(s_s.h, 1)
                        s_s.inc(1)
            pump_dq(4)

        for j in range(ntile - nbank, ntile):
            bank_free[bidx + j % nbank] = (rel[0], rel[1](j))

    make_dq_pieces("up", wup_d, ICH, D, wupT_h)
    make_dq_pieces("gt", wgt_d, ICH, D, wgtT_h)
    make_dq_pieces("dn", wdn_d, FCH, II, wdnT_h)

    run_tm_proj("wk", 0, 8, "fm", k_tm)
    run_tm_proj("wv", 0, 8, AF.Identity, v_tm)
    v_kevac = s_v.v
    s_vevac = s_s.v

    w_guard["wg"] = ("t", tdone[("wk", TCH - 1, 1)])
    w_guard["wq"] = ("t", tdone[("wv", TCH - 1, 1)])
    emit_wload("wg")
    emit_wload("wq")

    if _STOP_AFTER == "ktm":
        return finish(lambda: (dump_tm("ktm", k_tm[:, :, :], TCH, D, BF16),
                               dump_tm("vtm", v_tm[:, :, :], TCH, D, BF16)))

    @block.tensor
    def _(t):
        t.wait_ge(s_v.h, v_kevac)
        t.wait_ge(s_s.h, s_vevac)
        for p in range(NPAIR):
            bank_wait(t, p)
        for p in range(NPAIR):
            for c in range(TCH):
                t.matmul(ps[p][:, 0:128],
                         k_tm[:, c, p * 128:(p + 1) * 128],
                         v_tm[:, c, p * 128:(p + 1) * 128],
                         start=(c == 0), stop=(c == TCH - 1),
                         skip_group_check=True)
            for c in range(TCH):
                mm = t.matmul(ps[p][:, 128:129],
                              k_tm[:, c, p * 128:(p + 1) * 128],
                              ones_col[:, :],
                              start=(c == 0), stop=(c == TCH - 1),
                              skip_group_check=True)
        mm.then_inc(s_t.h, 1)
    t_kv = s_t.inc(1)

    @block.vector
    def _(v):
        v.wait_ge(s_t.h, t_kv)
        for p in range(NPAIR):
            v.tensor_copy(kvaug[0:64, p, 0:64], ps[p][0:64, 0:64])
            v.tensor_copy(kvaug[64:128, p, 64:128], ps[p][64:128, 64:128])
            v.tensor_copy(zk[0:64, 4 * p:4 * p + 1], ps[p][0:64, 128:129])
            last = v.tensor_copy(zk[64:128, 4 * p + 2:4 * p + 3], ps[p][64:128, 128:129])
        fence(v, s_v, last)
        for p in range(NPAIR):
            v.tensor_tensor(zk[0:64, 4 * p + 1:4 * p + 2], ps[p][0:64, 128:129],
                            zk[0:64, 4 * p:4 * p + 1], ALU.subtract)
            v.tensor_tensor(zk[64:128, 4 * p + 3:4 * p + 4], ps[p][64:128, 128:129],
                            zk[64:128, 4 * p + 2:4 * p + 3], ALU.subtract)
        v.engine_nop().then_inc(s_v.h, 1)
    v_kv = s_v.inc(1)
    for i in range(8):
        bank_free[i] = ("v", v_kv)

    free("v_tm")
    pump_dq(3)

    if _STOP_AFTER == "kv":
        def d():
            o = dbg_out("kvaug", [128, NPAIR * 128], BF16)
            o2 = dbg_out("zkd", [128, 32], BF16)

            @block.gpsimd
            def _(g):
                g.wait_ge(s_v.h, v_kv)
                g.dma_start(out=o[:, :], in_=kvaug[:, :, :]).then_inc(s_out.h, 16)
                g.dma_start(out=o2[:, :], in_=zk[:, :]).then_inc(s_out.h, 16)
                g.wait_ge(s_out.h, s_out.inc(32))
        return finish(d)

    gate_tm = alloc("gate_tm", [128, TCH, D], BF16)
    run_tm_proj("wg", 0, 8, AF.Sigmoid, gate_tm)

    w_guard["wo"] = ("t", tdone[("wg", TCH - 1, 1)])
    emit_wload("wo")

    q_fm = alloc("q_fm", [128, FCH, T], BF16)

    _qt0, _qv0, _qs0 = s_t.v, s_v.v, s_s.v

    for g0 in range(0, 32, 8):
        @block.tensor
        def _(t, g0=g0):
            if g0 == 0:
                t.wait_ge(gd_w['wq'][0].h, gd_w['wq'][1])
            for j in range(g0, g0 + 8):
                m, n = divmod(j, 4)
                pi = j % 8
                if j < 8:
                    bank_wait(t, pi)
                else:
                    t.wait_ge(s_v.h, _qv0 + (j - 8) + 1)
                for kc in range(FCH):
                    t.matmul(ps[pi][:, :],
                             wslot[1][:, kc, m * 128:(m + 1) * 128],
                             xbf_fm[:, kc, n * 512:(n + 1) * 512],
                             start=(kc == 0), stop=False)
                t.matmul(ps[pi][:, :],
                         bias_sb[0:1, 0, m * 128:(m + 1) * 128],
                         ones_bf[0:1, n * 512:(n + 1) * 512],
                         start=False, stop=True).then_inc(s_t.h, 1)
                s_t.inc(1)
                tdone[("q", m, n)] = s_t.v

        @block.scalar
        def _(s, g0=g0):
            for j in range(g0, g0 + 8):
                if j >= 1:
                    jj = j - 1
                    s.wait_ge(s_s.h, _qs0 + 2 * jj + 1)
                    if jj >= 2:
                        s.wait_ge(s_v.h, _qv0 + jj - 1)
                    s.activation(out=fs_b[:, (jj % 2) * 512:(jj % 2) * 512 + 512],
                                 in_=fs_a[:, (jj % 2) * 512:(jj % 2) * 512 + 512],
                                 func=AF.Exp, scale=-1.0).then_inc(s_s.h, 1)
                    s_s.inc(1)
                s.wait_ge(s_t.h, _qt0 + j + 1)
                s.activation(out=fs_a[:, (j % 2) * 512:(j % 2) * 512 + 512],
                             in_=ps[j % 8][:, :],
                             func=AF.Relu, scale=-1.0).then_inc(s_s.h, 1)
                s_s.inc(1)
            if g0 + 8 == 32:
                jj = 31
                s.wait_ge(s_s.h, _qs0 + 2 * jj + 1)
                s.wait_ge(s_v.h, _qv0 + jj - 1)
                s.activation(out=fs_b[:, (jj % 2) * 512:(jj % 2) * 512 + 512],
                             in_=fs_a[:, (jj % 2) * 512:(jj % 2) * 512 + 512],
                             func=AF.Exp, scale=-1.0).then_inc(s_s.h, 1)
                s_s.inc(1)

        @block.vector
        def _(v, g0=g0):
            for j in range(g0, g0 + 8):
                m, n = divmod(j, 4)
                pi = j % 8
                v.wait_ge(s_s.h, _qs0 + 2 * j + 2)
                v.scalar_tensor_tensor(q_fm[:, m, n * 512:(n + 1) * 512],
                                       ps[pi][:, :], 0.0,
                                       fs_b[:, (j % 2) * 512:(j % 2) * 512 + 512],
                                       ALU.max, ALU.add).then_inc(s_v.h, 1)
                s_v.inc(1)
        pump_dq(4)

    for j in range(24, 32):
        bank_free[j % 8] = ("v", _qv0 + j + 1)

    free("wslotB")

    if _STOP_AFTER == "qgate":
        return finish(lambda: (dump_tm("qfm", q_fm[:, :, :], FCH, T, BF16),
                               dump_tm("gatetm", gate_tm[:, :, :], TCH, D, BF16)))

    gm_tm = k_tm
    sc_t = {}
    gate_all = s_s.v
    q_all = s_v.v

    for c in range(TCH):
        pa, pb, pz = (0, 1, 2) if c % 2 == 0 else (3, 4, 5)

        @block.tensor
        def _(t, c=c, pa=pa, pb=pb, pz=pz):
            if c == 0:
                t.wait_ge(s_v.h, max(v_kv, q_all))
            for pi in (pa, pb, pz):
                bank_wait(t, pi)
            for p in range(NPAIR):
                dst = ps[pa] if p < 4 else ps[pb]
                t.matmul(dst[:, (p % 4) * 128:(p % 4) * 128 + 128],
                         q_fm[:, p, c * 128:(c + 1) * 128],
                         kvaug[:, p, :], start=True, stop=True,
                         skip_group_check=True)
                mm = t.matmul(ps[pz][:, p * 4:(p + 1) * 4],
                              q_fm[:, p, c * 128:(c + 1) * 128],
                              zk[:, p * 4:(p + 1) * 4], start=True, stop=True,
                              skip_group_check=True)
            mm.then_inc(s_t.h, 1)
        sc_t[c] = s_t.inc(1)

        @block.vector
        def _(v, c=c, pa=pa, pb=pb, pz=pz):
            v.wait_ge(s_t.h, sc_t[c])
            if c == 0:
                v.wait_ge(s_s.h, gate_all)
            inst = v.tensor_copy(zs32[:, :], ps[pz][:, 0:32])
            fence(v, s_v, inst)
            z3 = zs32[:, :].rearrange("p (h t) -> p h t", t=2)
            inst = v.scalar_tensor_tensor(zsum[:, :], z3[:, :, 0], 1e-6,
                                          z3[:, :, 1], ALU.add, ALU.add)
            fence(v, s_v, inst)
            inst = v.reciprocal(zbuf[:, :], zsum[:, :])
            fence(v, s_v, inst)
            for half, pp in ((0, pa), (1, pb)):
                zv = zbuf[:, half * 8:(half + 1) * 8].unsqueeze(2).broadcast_to([128, 8, 64])
                inst = v.tensor_tensor(fs_a[:, 0:512].rearrange("p (h e) -> p h e", e=64),
                                       ps[pp][:, :].rearrange("p (h e) -> p h e", e=64),
                                       zv, ALU.mult)
                fence(v, s_v, inst)
                v.tensor_tensor(gm_tm[:, c, half * 512:(half + 1) * 512],
                                fs_a[:, 0:512],
                                gate_tm[:, c, half * 512:(half + 1) * 512],
                                ALU.mult)
            v.engine_nop().then_inc(s_v.h, 1)
        val = s_v.inc(1)
        for pi in (pa, pb, pz):
            bank_free[pi] = ("v", val)
        if c % 2 == 1:
            pump_dq(2)
    v_gm_all = s_v.v

    if _STOP_AFTER == "gm":
        return finish(lambda: dump_tm("gmtm", gm_tm[:, :, :], TCH, D, BF16))

    gm_fm = xbf_fm
    tr_gm = pe_transpose(gm_fm, gm_tm, (6, 7))

    free("q_fm"); free("gate_tm"); free("k_tm")
    free("fs_b"); free("fs_a"); free("zk"); free("kvaug")
    x_tm = alloc("x_tm", [128, TCH, D], F32)

    @block.sync
    def _(sync):
        sync.wait_ge(s_v.h, tr_gm)
        sync.dma_start(out=x_tm[:, :, :],
                       in_=x_d.rearrange("(t p) d -> p t d", p=128)).then_inc(s_x2.h, 16)
    ld_x2 = s_x2.inc(16)

    o_t = {}
    _ot0, _ov0 = s_t.v, s_v.v

    for g0 in range(0, 32, 8):
        @block.tensor
        def _(t, g0=g0):
            if g0 == 0:
                t.wait_ge(s_v.h, tr_gm)
                t.wait_ge(gd_w['wo'][0].h, gd_w['wo'][1])
            for j in range(g0, g0 + 8):
                m, n = divmod(j, 2)
                pi = j % 8
                if j < 8:
                    bank_wait(t, pi)
                else:
                    t.wait_ge(s_v.h, _ov0 + (j - 8) // 2 + 1)
                for kc in range(FCH):
                    t.matmul(ps[pi][:, :],
                             gm_fm[:, kc, m * 128:(m + 1) * 128],
                             wslot[0][:, kc, n * 512:(n + 1) * 512],
                             start=(kc == 0), stop=False)
                t.matmul(ps[pi][:, :],
                         ones_bf[0:1, 0:128],
                         bias_sb[0:1, 4, n * 512:(n + 1) * 512],
                         start=False, stop=True).then_inc(s_t.h, 1)
                s_t.inc(1)
                o_t[(m, n)] = s_t.v
        pump_dq(2)

    res_last = {}

    @block.vector
    def _(v):
        v.wait_ge(s_x2.h, ld_x2)
        for m in range(TCH):
            for n in range(2):
                v.wait_ge(s_t.h, o_t[(m, n)])
                v.tensor_tensor(x_tm[:, m, n * 512:(n + 1) * 512],
                                ps[(m * 2 + n) % 8][:, :],
                                x_tm[:, m, n * 512:(n + 1) * 512], ALU.add)
            v.engine_nop().then_inc(s_v.h, 1)
            s_v.inc(1)
            res_last[m] = s_v.v
    for m in range(TCH - 4, TCH):
        for n in range(2):
            bank_free[(m * 2 + n) % 8] = ("v", res_last[m])

    last_vres = s_v.v

    @block.vector
    def _(v):
        v.wait_ge(s_v.h, last_vres)
        for m in range(TCH):
            v.bn_stats(bn6[:, m, 0:6], x_tm[:, m, 0:512])
            inst = v.bn_stats(bn6[:, m, 6:12], x_tm[:, m, 512:1024])
        fence(v, s_v, inst)
        for m in range(TCH):
            v.bn_aggr(mvall[:, m, :], bn6[:, m, 0:12])
        v.engine_nop().then_inc(s_v.h, 1)
    v_bn1 = s_v.inc(1)

    @block.scalar
    def _(s):
        s.wait_ge(s_v.h, v_bn1)
        s.activation(out=sdall[:, :], in_=mvall[:, :, 1], func=AF.Sqrt,
                     bias=ccol[:, 1:2], scale=1.0).then_inc(s_s.h, 1)
    s_sd1 = s_s.inc(1)

    @block.vector
    def _(v):
        v.wait_ge(s_s.h, s_sd1)
        inst = v.reciprocal(rstd[:, :], sdall[:, :])
        fence(v, s_v, inst)
        v.scalar_tensor_tensor(nmr[:, :], mvall[:, :, 0], -1.0, rstd[:, :],
                               ALU.mult, ALU.mult)
        v.engine_nop().then_inc(s_v.h, 1)
    v_r1 = s_v.inc(1)

    ln1_done = {}

    @block.scalar
    def _(s):
        s.wait_ge(s_v.h, v_r1)
        for m in range(TCH):
            s.activation(out=x_tm[:, m, :], in_=x_tm[:, m, :], func=AF.Identity,
                         bias=nmr[:, m:m + 1], scale=rstd[:, m:m + 1]).then_inc(s_s.h, 1)
            s_s.inc(1)
            ln1_done[m] = s_s.v

    if _STOP_AFTER == "ln1":
        return finish(lambda: dump_tm("xln", x_tm[:, :, :], TCH, D, F32))

    def barrier(bid, nvals, stat_ap, post_vector, extra_vwait=None):
        @block.tensor
        def _(t):
            t.wait_ge(s_id.h, ld_ident)
            if extra_vwait:
                t.wait_ge(s_v.h, extra_vwait)
            bank_wait(t, 7)
            t.transpose(ps[7][0:nvals, 0:128], stat_ap, ident[:]).then_inc(s_t.h, 1)
        tt = s_t.inc(1)

        @block.vector
        def _(v):
            v.wait_ge(s_t.h, tt)
            v.tensor_reduce(lmax[0:nvals, 0:1], ps[7][0:nvals, :], AX.X,
                            ALU.max).then_inc(s_v.h, 1)
        vv = s_v.inc(1)
        bank_free[7] = ("v", vv)

        @block.scalar
        def _(s):
            s.wait_ge(s_v.h, vv)
            s.wait_ge(s_id.h, ld_cc0)
            s.dma_start(out=cc_in[bid][0:nvals, 0:1], in_=lmax[0:nvals, 0:1],
                        allow_slow_non_contiguous=True).then_inc(s_tr.h, 16)
        dd = s_tr.inc(16)

        @block.gpsimd
        def _(g):
            g.wait_ge(s_tr.h, dd)
            g.collective_compute(
                "AllGather", ALU.bypass,
                replica_groups=[list(range(8))],
                ins=[cc_in[bid][:, :].opt()],
                outs=[cc_out[bid][:, :].opt()]).then_inc(s_cc.h, 1)
        cc = s_cc.inc(1)

        @block.scalar
        def _(s):
            s.wait_ge(s_cc.h, cc)
            s.dma_start(out=gth[0:1, 0:16],
                        in_=cc_out[bid][:, 0:1].rearrange("a b -> (b a)").unsqueeze(0),
                        allow_slow_non_contiguous=True).then_inc(s_tr.h, 16)
        gg = s_tr.inc(16)

        @block.vector
        def _(v):
            v.wait_ge(s_tr.h, gg)
            g3 = gth[0:1, :].rearrange("p (r t) -> p t r", t=2)
            inst = v.tensor_reduce(smax[0:1, 0:nvals], g3[:, 0:nvals, :], AX.X,
                                   ALU.max)
            fence(v, s_v, inst)
            post_vector(v)
            v.engine_nop().then_inc(s_v.h, 1)
        return s_v.inc(1)

    def bcast(off, ncols, src_ap, wait_v):
        @block.tensor
        def _(t):
            t.wait_ge(s_v.h, wait_v)
            bank_wait(t, 6)
            t.matmul(ps[6][:, 0:ncols], ones_f[0:1, :], src_ap,
                     start=True, stop=True).then_inc(s_t.h, 1)
        tt = s_t.inc(1)

        @block.vector
        def _(v):
            v.wait_ge(s_t.h, tt)
            v.tensor_copy(bc[:, off:off + ncols], ps[6][:, 0:ncols]).then_inc(s_v.h, 1)
        vv = s_v.inc(1)
        bank_free[6] = ("v", vv)
        return vv

    @block.vector
    def _(v):
        for m in range(TCH):
            v.wait_ge(s_s.h, ln1_done[m])
            inst = v.tensor_reduce(am24[:, m:m + 1], x_tm[:, m, :], AX.X, ALU.max,
                                   apply_absolute_value=True)
        fence(v, s_v, inst)
        v.tensor_reduce(am1[:, 0:1], am24[:, 0:TCH], AX.X, ALU.max).then_inc(s_v.h, 1)
    v_amx = s_v.inc(1)

    def post_b0(v):
        inst = v.tensor_scalar(smax[0:1, 2:3], smax[0:1, 0:1], INV_Q16, 1e-12,
                               ALU.mult, ALU.max)
        fence(v, s_v, inst)
        v.reciprocal(smax[0:1, 3:4], smax[0:1, 2:3])

    pb0 = barrier(0, 1, am1[:, 0:1], post_b0, extra_vwait=v_amx)
    vb0 = bcast(0, 2, smax[0:1, 2:4], pb0)

    @block.vector
    def _(v):
        v.wait_ge(s_v.h, vb0)
        v.tensor_scalar(nmcol[:, 0:1], bc[:, 0:1], -MAGIC, None, ALU.mult)
        v.engine_nop().then_inc(s_v.h, 1)
    v_nm0 = s_v.inc(1)

    xq_tm = alloc("xq_tm", [128, TCH, D], BF16)
    xq_stage = alloc("xq_stage", [128, 2, D], F32)
    xq_done = {}

    @block.scalar
    def _(s):
        s.wait_ge(s_v.h, v_nm0)
        for m in range(TCH):
            b = m % 2
            inst = s.activation(out=xq_stage[:, b, :], in_=x_tm[:, m, :],
                                func=AF.Identity, bias=ccol[:, 0:1], scale=bc[:, 1:2])
            fence(s, s_s, inst)
            s.activation(out=xq_tm[:, m, :], in_=xq_stage[:, b, :], func=AF.Identity,
                         bias=nmcol[:, 0:1], scale=bc[:, 0:1]).then_inc(s_s.h, 1)
            s_s.inc(1)
            xq_done[m] = s_s.v

    xq_fm = xbf_fm

    _xq_seen = set()

    def _xq_idx_wait(t, c, f):
        if c not in _xq_seen:
            _xq_seen.add(c)
            t.wait_ge(s_s.h, xq_done[c])
    tr_xq = pe_transpose(xq_fm, xq_tm, (0, 1), idx_wait=_xq_idx_wait)

    @block.scalar
    def _(s):
        s.wait_ge(s_s.h, ln1_done[TCH - 1])
        s.dma_start(out=xln_h[:, :, :].transpose([1, 0, 2]),
                    in_=x_tm[:, :, :]).then_inc(s_xsp.h, 16)
    ld_xlnspill = s_xsp.inc(16)

    if _STOP_AFTER == "xq":
        def d():
            dump_tm("xq", xq_fm[:, :, :], FCH, T, BF16)
            o2 = dbg_out("scx", [128, 8], F32)

            @block.gpsimd
            def _(g):
                g.wait_ge(s_v.h, s_v.v)
                g.dma_start(out=o2[:, :], in_=bc[:, :]).then_inc(s_out.h, 16)
                g.wait_ge(s_out.h, s_out.inc(16))
        return finish(d)

    free("xq_stage")
    free("xq_tm")
    free("bias_sb")
    free("ones_col")
    free("ones_bf")
    free("wslotA")
    wupT = alloc("wupT", [128, FCH, II], BF16)
    x_flat = x_tm[:, :, :].rearrange("p a b -> p (a b)")
    f1_stage = [x_flat[:, 0:T], x_flat[:, T:2 * T]]
    wgtT = x_flat[:, 2 * T:8 * T].bitcast(BF16).rearrange("p (a b) -> p a b", b=II)

    t_attn_done = s_t.v
    s_xq_done = s_s.v

    pump_dq_tag("up")
    pump_dq_tag("gt")

    @block.sync
    def _(sync):
        sync.wait_ge(s_t.h, t_attn_done)
        sync.wait_ge(s_v.h, tr_xq)
        sync.wait_ge(s_s.h, s_xq_done)
        sync.wait_ge(s_dqst["up"].h, dq_last["up"])
        sync.dma_start(out=wupT[:, :, :],
                       in_=wupT_h[:, :, :].transpose([1, 0, 2])).then_inc(s_wup.h, 16)
        sync.wait_ge(s_dqst["gt"].h, dq_last["gt"])
        sync.dma_start(out=wgtT[:, :, :],
                       in_=wgtT_h[:, :, :].transpose([1, 0, 2])).then_inc(s_wgt.h, 16)
    ld_wup = s_wup.inc(16)
    ld_wgt = s_wgt.inc(16)

    def f1_pass(wT, ld_w, out_h, amdst, skey, prev_spill=None):
        t0, v0, s0 = s_t.v, s_v.v, s_s.v
        ftd = {}
        sp_done = {}

        def sdnv(m, n):
            return s0 + 4 * m + n + 1

        def vredv(m):
            return v0 + m + 1

        GM = 4
        for m0 in range(0, ICH, GM):
            @block.tensor
            def _(t, m0=m0):
                if m0 == 0:
                    t.wait_ge(ld_w[0].h, ld_w[1])
                    t.wait_ge(s_v.h, tr_xq)
                for m in range(m0, m0 + GM):
                    for n in range(4):
                        j = m * 4 + n
                        pi = j % 8
                        if j < 8:
                            bank_wait(t, pi)
                        else:
                            pj = j - 8
                            t.wait_ge(s_s.h, sdnv(pj // 4, pj % 4))
                        for kc in range(FCH):
                            mm = t.matmul(ps[pi][:, :],
                                          wT[:, kc, m * 128:(m + 1) * 128],
                                          xq_fm[:, kc, n * 512:(n + 1) * 512],
                                          start=(kc == 0), stop=(kc == FCH - 1))
                        mm.then_inc(s_t.h, 1)
                        s_t.inc(1)
                        ftd[(m, n)] = s_t.v

            @block.scalar
            def _(s, m0=m0):
                if m0 == 0:
                    s.wait_ge(s_xsp.h, ld_xlnspill)
                for m in range(m0, m0 + GM):
                    b = m % 2
                    if m >= 2:
                        s.wait_ge(s_f1sp[(skey, b)].h, sp_done[m - 2])
                    elif prev_spill is not None:
                        s.wait_ge(s_f1sp[(prev_spill[2], b)].h, prev_spill[b])
                    for n in range(4):
                        s.wait_ge(s_t.h, ftd[(m, n)])
                        s.activation(out=f1_stage[b][:, n * 512:(n + 1) * 512],
                                     in_=ps[(m * 4 + n) % 8][:, :],
                                     func=AF.Identity).then_inc(s_s.h, 1)
                        s_s.inc(1)
                    sp_done[m] = 16 * (m // 2 + 1)

            @block.vector
            def _(v, m0=m0):
                for m in range(m0, m0 + GM):
                    b = m % 2
                    v.wait_ge(s_s.h, sdnv(m, 3))
                    v.tensor_reduce(amdst[:, m:m + 1], f1_stage[b], AX.X,
                                    ALU.max, apply_absolute_value=True).then_inc(s_v.h, 1)
                    s_v.inc(1)

            @block.sync
            def _(sync, m0=m0):
                for m in range(m0, m0 + GM):
                    b = m % 2
                    sync.wait_ge(s_v.h, vredv(m))
                    sync.dma_start(out=out_h[m, :, :],
                                   in_=f1_stage[b]).then_inc(s_f1sp[(skey, b)].h, 16)
                    s_f1sp[(skey, b)].inc(16)
            pump_dq(4)

        for n in range(4):
            j = ICH * 4 - 4 + n
            bank_free[j % 8] = ("s", sdnv(ICH - 1, n))
        for n in range(4):
            j = ICH * 4 - 8 + n
            bank_free[j % 8] = ("s", sdnv(ICH - 2, n))
        return (s_f1sp[(skey, 0)].v, s_f1sp[(skey, 1)].v, skey)

    up_spilled = f1_pass(wupT, (s_wup, ld_wup), up_h, am24, 'u')
    gt_spilled = f1_pass(wgtT, (s_wgt, ld_wgt), gate_h, am24b, 'g',
                         prev_spill=up_spilled)
    t_f1_done = s_t.v

    @block.vector
    def _(v):
        inst = v.engine_nop()
        fence(v, s_v, inst)
        v.tensor_reduce(am1[:, 0:1], am24[:, 0:ICH], AX.X, ALU.max)
        v.tensor_reduce(am1[:, 1:2], am24b[:, 0:ICH], AX.X, ALU.max)
        v.engine_nop().then_inc(s_v.h, 1)
    v_amug = s_v.inc(1)

    def post_b1(v):
        v.tensor_scalar(smax[0:1, 2:3], smax[0:1, 0:1], INV_Q16, 1e-12, ALU.mult, ALU.max)
        inst = v.tensor_scalar(smax[0:1, 4:5], smax[0:1, 1:2], INV_Q16, 1e-12, ALU.mult, ALU.max)
        fence(v, s_v, inst)
        v.reciprocal(smax[0:1, 3:4], smax[0:1, 2:3])
        v.reciprocal(smax[0:1, 5:6], smax[0:1, 4:5])

    pb1 = barrier(1, 2, am1[:, 0:2], post_b1, extra_vwait=v_amug)
    vb1 = bcast(0, 4, smax[0:1, 2:6], pb1)

    @block.vector
    def _(v):
        v.wait_ge(s_v.h, vb1)
        v.tensor_scalar(nmcol[:, 1:2], bc[:, 2:3], -MAGIC, None, ALU.mult)
        v.engine_nop().then_inc(s_v.h, 1)
    v_nm1 = s_v.inc(1)

    if _STOP_AFTER == "f1":
        def d():
            o = dbg_out("scug", [128, 8], F32)

            @block.gpsimd
            def _(g):
                g.wait_ge(s_v.h, v_nm1)
                g.dma_start(out=o[:, :], in_=bc[:, :]).then_inc(s_out.h, 16)
                g.wait_ge(s_out.h, s_out.inc(16))
        return finish(d)

    g2_keep = {}
    for m in range(8):
        g2_keep[m] = x_flat[:, m * T:(m + 1) * T]
    wupT_f = wupT[:, :, :].rearrange("p a b -> p (a b)").bitcast(F32)
    for m in range(6):
        g2_keep[8 + m] = wupT_f[:, m * T:(m + 1) * T]
    xbf_f = xbf_fm[:, :, :].rearrange("p a b -> p (a b)").bitcast(F32)
    for m in range(4):
        g2_keep[14 + m] = xbf_f[:, m * T:(m + 1) * T]

    pump_dq_tag("dn")
    assert not (dq_qA or dq_qB or dq_qC), "dq pieces left unpumped"
    free("dq_tr"); free("dq_bf"); free("dq_si"); free("dq_s")
    free("dq_bm"); free("dq_in")
    fr1 = alloc("fr1", [128, 3 * T], F32)
    fr2 = alloc("fr2", [128, 2 * T], F32)
    fr4 = alloc("fr4", [128, T], F32)

    def p3(buf, m):
        b = m % 3
        return buf[:, b * T:(b + 1) * T]

    def p2(buf, m):
        b = m % 2
        return buf[:, b * T:(b + 1) * T]

    pslc = p3

    def f2a_dst(m):
        return g2_keep[m] if m < N_KEEP else p2(fr2, m)

    f2a_ld = {}
    fA, fS, fR = {}, {}, {}
    fSV1, fS2, fS1, fSP = {}, {}, {}, {}
    for m in range(ICH):
        if m >= 1:
            fA[m - 1] = s_v.inc(1)
        if m >= 2:
            fR[m - 2] = s_v.inc(1)
        if m >= 1:
            fS[m - 1] = s_v.inc(1)
        fSV1[m] = s_s.inc(1)
        if m >= 1:
            fS2[m - 1] = s_s.inc(1)
        fS1[m] = s_s.inc(1)
    fS2[ICH - 1] = s_s.inc(1)
    fA[ICH - 1] = s_v.inc(1)
    fR[ICH - 2] = s_v.inc(1)
    fS[ICH - 1] = s_v.inc(1)
    fR[ICH - 1] = s_v.inc(1)
    for m in range(N_KEEP, ICH):
        fSP[m] = s_g2st.inc(16)
    for m in range(ICH):
        f2a_ld[m] = s_gr[m % 3].inc(16)

    @block.sync
    def _(sync):
        for m in range(ICH):
            if m == 0:
                sync.wait_ge(s_f1sp[("g", 0)].h, gt_spilled[0])
                sync.wait_ge(s_f1sp[("g", 1)].h, gt_spilled[1])
            if m >= 3:
                sync.wait_ge(s_v.h, fS[m - 3])
            sync.dma_start(out=p3(fr1, m),
                           in_=gate_h[m, :, :]).then_inc(s_gr[m % 3].h, 16)

    @block.gpsimd
    def _(g):
        for m in range(N_KEEP, ICH):
            g.wait_ge(s_v.h, fS[m])
            g.dma_start(out=g2_h[m, :, :], in_=p2(fr2, m)).then_inc(s_g2st.h, 16)

    @block.scalar
    def _(s):
        for m in range(ICH):
            s.wait_ge(s_gr[m % 3].h, f2a_ld[m])
            if m == 0:
                s.wait_ge(s_v.h, v_nm1)
            s.activation(out=p3(fr1, m), in_=p3(fr1, m), func=AF.Identity,
                         bias=ccol[:, 0:1], scale=bc[:, 3:4]).then_inc(s_s.h, 1)
            if m >= 1:
                s.wait_ge(s_s.h, fS1[m - 1])
                s.activation(out=p2(fr2, m - 1), in_=p2(fr2, m - 1),
                             func=AF.Identity, bias=ccol[:, 0:1],
                             scale=ccol[:, 2:3]).then_inc(s_s.h, 1)
            s.wait_ge(s_s.h, fSV1[m])
            if m >= 2:
                s.wait_ge(s_v.h, fA[m - 2])
                if m - 2 >= N_KEEP:
                    s.wait_ge(s_v.h, fR[m - 2])
                    s.wait_ge(s_g2st.h, fSP[m - 2])
            s.activation(out=p2(fr2, m), in_=p3(fr1, m), func=AF.Sigmoid,
                         bias=nmcol[:, 1:2], scale=bc[:, 2:3]).then_inc(s_s.h, 1)
        s.wait_ge(s_s.h, fS1[ICH - 1])
        s.activation(out=p2(fr2, ICH - 1), in_=p2(fr2, ICH - 1),
                     func=AF.Identity, bias=ccol[:, 0:1],
                     scale=ccol[:, 2:3]).then_inc(s_s.h, 1)

    @block.vector
    def _(v):
        for m in range(ICH + 1):
            if m >= 1:
                v.wait_ge(s_s.h, fS2[m - 1])
                v.tensor_scalar(fr4[:, :], p2(fr2, m - 1), MAGIC, 65535.0,
                                ALU.subtract, ALU.min).then_inc(s_v.h, 1)
            if m >= 2:
                v.wait_ge(s_v.h, fS[m - 2])
                v.tensor_reduce(am24[:, m - 2:m - 1], f2a_dst(m - 2), AX.X,
                                ALU.max,
                                apply_absolute_value=True).then_inc(s_v.h, 1)
            if m >= 1:
                v.wait_ge(s_v.h, fA[m - 1])
                v.scalar_tensor_tensor(f2a_dst(m - 1), p3(fr1, m - 1), MAGIC,
                                       fr4[:, :], ALU.subtract,
                                       ALU.mult).then_inc(s_v.h, 1)
        v.wait_ge(s_v.h, fS[ICH - 1])
        v.tensor_reduce(am24[:, ICH - 1:ICH], f2a_dst(ICH - 1), AX.X, ALU.max,
                        apply_absolute_value=True).then_inc(s_v.h, 1)
        v.wait_ge(s_v.h, fR[ICH - 1])
        v.tensor_reduce(am1[:, 0:1], am24[:, 0:ICH], AX.X, ALU.max)
        v.engine_nop().then_inc(s_v.h, 1)
    v_am2 = s_v.inc(1)

    def post_b2(v):
        inst = v.tensor_scalar(smax[0:1, 6:7], smax[0:1, 4:5], INV_2P16, None, ALU.mult)
        fence(v, s_v, inst)
        inst = v.tensor_tensor(smax[0:1, 8:9], smax[0:1, 0:1], smax[0:1, 6:7], ALU.mult)
        fence(v, s_v, inst)
        inst = v.tensor_scalar(smax[0:1, 7:8], smax[0:1, 8:9], INV_Q16, 1e-12,
                               ALU.mult, ALU.max)
        fence(v, s_v, inst)
        inst = v.reciprocal(smax[0:1, 8:9], smax[0:1, 7:8])
        fence(v, s_v, inst)
        v.tensor_tensor(smax[0:1, 9:10], smax[0:1, 6:7], smax[0:1, 8:9], ALU.mult)

    pb2 = barrier(2, 1, am1[:, 0:1], post_b2, extra_vwait=v_am2)
    vb2 = bcast(4, 1, smax[0:1, 9:10], pb2)

    if _STOP_AFTER == "f2a":
        def d():
            o = dbg_out("scg2", [1, 12], F32)

            @block.gpsimd
            def _(g):
                g.wait_ge(s_v.h, s_v.v)
                g.dma_start(out=o[:, :], in_=smax[0:1, :]).then_inc(s_out.h, 16)
                g.wait_ge(s_out.h, s_out.inc(16))
        return finish(d)

    def f3_src(m):
        return g2_keep[m] if m < N_KEEP else p3(fr1, m)

    f3_ldg, f3_ldu = {}, {}
    gA, gS, gR = {}, {}, {}
    gSV, gSU, gSP = {}, {}, {}
    for m in range(ICH):
        if m >= 1:
            gA[m - 1] = s_v.inc(1)
        if m >= 2:
            gR[m - 2] = s_v.inc(1)
        if m >= 1:
            gS[m - 1] = s_v.inc(1)
        gSV[m] = s_s.inc(1)
        gSU[m] = s_s.inc(1)
    gA[ICH - 1] = s_v.inc(1)
    gR[ICH - 2] = s_v.inc(1)
    gS[ICH - 1] = s_v.inc(1)
    gR[ICH - 1] = s_v.inc(1)
    for m in range(N_KEEP, ICH):
        gSP[m] = s_opst.inc(16)
    for m in range(ICH):
        if m >= N_KEEP:
            f3_ldg[m] = s_g2r[m % 2].inc(16)
        f3_ldu[m] = s_ur[m % 2].inc(16)

    @block.sync
    def _(sync):
        for m in range(ICH):
            if m == 0:
                sync.wait_ge(s_f1sp[("u", 0)].h, up_spilled[0])
                sync.wait_ge(s_f1sp[("u", 1)].h, up_spilled[1])
            if m < 2:
                sync.wait_ge(s_v.h, fR[ICH - 2 + m])
                sync.wait_ge(s_g2st.h, fSP[ICH - 2 + m])
            else:
                sync.wait_ge(s_v.h, gA[m - 2])
            sync.dma_start(out=p2(fr2, m),
                           in_=up_h[m, :, :]).then_inc(s_ur[m % 2].h, 16)

    @block.gpsimd
    def _(g):
        for m in range(N_KEEP, ICH):
            if m - 3 >= N_KEEP:
                g.wait_ge(s_v.h, gS[m - 3])
                g.dma_start(out=op_h[m - 3, :, :],
                            in_=p3(fr1, m - 3)).then_inc(s_opst.h, 16)
            g.wait_ge(s_v.h, gS[m - 3])
            if m - 3 >= N_KEEP:
                g.wait_ge(s_v.h, gR[m - 3])
                g.wait_ge(s_opst.h, gSP[m - 3])
            g.wait_ge(s_g2st.h, fSP[m])
            g.dma_start(out=p3(fr1, m),
                        in_=g2_h[m, :, :]).then_inc(s_g2r[m % 2].h, 16)
        for m in range(ICH - 3, ICH):
            g.wait_ge(s_v.h, gS[m])
            g.dma_start(out=op_h[m, :, :],
                        in_=p3(fr1, m)).then_inc(s_opst.h, 16)

    @block.scalar
    def _(s):
        for m in range(ICH):
            if m == 0:
                s.wait_ge(s_v.h, vb2)
            if m >= N_KEEP:
                s.wait_ge(s_g2r[m % 2].h, f3_ldg[m])
            elif m >= 3:
                s.wait_ge(s_v.h, gS[m - 3])
            s.activation(out=p3(fr1, m), in_=f3_src(m), func=AF.Identity,
                         bias=ccol[:, 0:1], scale=bc[:, 4:5]).then_inc(s_s.h, 1)
            s.wait_ge(s_ur[m % 2].h, f3_ldu[m])
            s.activation(out=p2(fr2, m), in_=p2(fr2, m), func=AF.Identity,
                         bias=ccol[:, 0:1], scale=bc[:, 1:2]).then_inc(s_s.h, 1)

    @block.vector
    def _(v):
        for m in range(ICH + 1):
            if m >= 1:
                v.wait_ge(s_s.h, gSU[m - 1])
                v.tensor_scalar(fr4[:, :], p2(fr2, m - 1), MAGIC, None,
                                ALU.subtract).then_inc(s_v.h, 1)
            if m >= 2:
                v.wait_ge(s_v.h, gS[m - 2])
                v.tensor_reduce(am24b[:, m - 2:m - 1], f3_src(m - 2), AX.X,
                                ALU.max,
                                apply_absolute_value=True).then_inc(s_v.h, 1)
            if m >= 1:
                v.wait_ge(s_v.h, gA[m - 1])
                v.scalar_tensor_tensor(f3_src(m - 1), p3(fr1, m - 1), MAGIC,
                                       fr4[:, :], ALU.subtract,
                                       ALU.mult).then_inc(s_v.h, 1)
        v.wait_ge(s_v.h, gS[ICH - 1])
        v.tensor_reduce(am24b[:, ICH - 1:ICH], f3_src(ICH - 1), AX.X, ALU.max,
                        apply_absolute_value=True).then_inc(s_v.h, 1)
        v.wait_ge(s_v.h, gR[ICH - 1])
        v.tensor_reduce(am1[:, 0:1], am24b[:, 0:ICH], AX.X, ALU.max)
        v.engine_nop().then_inc(s_v.h, 1)
    v_am3 = s_v.inc(1)
    free("fr4")
    wdn_buf = alloc("wdn_buf", [128, 2, 2, D], BF16, side="right")

    def post_b3(v):
        inst = v.tensor_tensor(smax[0:1, 10:11], smax[0:1, 7:8], smax[0:1, 2:3], ALU.mult)
        fence(v, s_v, inst)
        inst = v.tensor_tensor(smax[0:1, 11:12], smax[0:1, 0:1], smax[0:1, 10:11], ALU.mult)
        fence(v, s_v, inst)
        inst = v.tensor_scalar(smax[0:1, 11:12], smax[0:1, 11:12], INV_Q16, 1e-12,
                               ALU.mult, ALU.max)
        fence(v, s_v, inst)
        inst = v.reciprocal(smax[0:1, 0:1], smax[0:1, 11:12])
        fence(v, s_v, inst)
        v.tensor_tensor(smax[0:1, 10:11], smax[0:1, 10:11], smax[0:1, 0:1], ALU.mult)

    pb3 = barrier(3, 1, am1[:, 0:1], post_b3, extra_vwait=v_am3)
    vb3 = bcast(5, 2, smax[0:1, 10:12], pb3)

    @block.vector
    def _(v):
        v.wait_ge(s_v.h, vb3)
        v.tensor_scalar(nmcol[:, 2:3], bc[:, 6:7], -MAGIC, None, ALU.mult)
        v.engine_nop().then_inc(s_v.h, 1)
    v_nm3 = s_v.inc(1)

    x_bf_flat = x_flat.bitcast(BF16)
    wupT_bfflat = wupT[:, :, :].rearrange("p a b -> p (a b)")
    obf = [x_bf_flat[:, k * T:(k + 1) * T] for k in range(16)]
    obf += [wupT_bfflat[:, k * T:(k + 1) * T] for k in range(8)]

    ob_done = {}
    ob_vo, ob_ldo = {}, {}
    for k in range(ICH):
        ob_vo[k] = s_v.inc(1)
        ob_done[k] = s_s.inc(1)
    for k in range(N_KEEP, ICH):
        ob_ldo[k] = s_opr[k % 2].inc(16)

    @block.gpsimd
    def _(g):
        for k in range(N_KEEP, ICH):
            g.wait_ge(s_s.h, ob_done[k - 3])
            if k - 3 >= N_KEEP:
                g.wait_ge(s_opr[(k - 3) % 2].h, ob_ldo[k - 3])
            g.wait_ge(s_opst.h, gSP[k])
            g.dma_start(out=p3(fr1, k),
                        in_=op_h[k, :, :]).then_inc(s_opr[k % 2].h, 16)

    @block.vector
    def _(v):
        for k in range(ICH):
            if k == 0:
                v.wait_ge(s_v.h, v_nm3)
                v.wait_ge(s_opst.h, gSP[ICH - 1])
            if k >= N_KEEP:
                v.wait_ge(s_opr[k % 2].h, ob_ldo[k])
            if k >= 3:
                v.wait_ge(s_s.h, ob_done[k - 3])
            v.tensor_scalar(p3(fr1, k), f3_src(k), bc[:, 5:6], MAGIC,
                            ALU.mult, ALU.add).then_inc(s_v.h, 1)

    @block.scalar
    def _(s):
        for k in range(ICH):
            s.wait_ge(s_v.h, ob_vo[k])
            s.activation(out=obf[k], in_=p3(fr1, k), func=AF.Identity,
                         bias=nmcol[:, 2:3], scale=bc[:, 6:7]).then_inc(s_s.h, 1)

    free("fr2")
    free("fr1")
    xlnrb = alloc("xlnrb", [128, 4, D], F32)
    y_stage = alloc("y_stage", [128, 4, D], F32)

    xln_rb = {}
    f3b_t = {}
    f3b_res = {}
    yout = {}
    wd_ld = {}
    wd_used = {}

    for w in range(4):
        @block.sync
        def _(sync, w=w):
            if w == 0:
                sync.wait_ge(s_xsp.h, 16)
                sync.wait_ge(s_s.h, ob_done[ICH - 1])
                sync.wait_ge(s_opr[0].h, ob_ldo[ICH - 2])
                sync.wait_ge(s_opr[1].h, ob_ldo[ICH - 1])
            else:
                sync.wait_ge(s_v.h, f3b_res[(w - 1, 3)])
            sync.dma_start(out=xlnrb[:, :, :],
                           in_=xln_h[w * 4:(w + 1) * 4, :, :].transpose([1, 0, 2])).then_inc(s_xl[w % 2].h, 16)
        xln_rb[w] = s_xl[w % 2].inc(16)

        for kg in range(12):
            gidx = w * 12 + kg

            @block.sync
            def _(sync, w=w, kg=kg, gidx=gidx):
                if gidx == 0:
                    sync.wait_ge(s_dqst["dn"].h, dq_last["dn"])
                    sync.wait_ge(s_t.h, t_f1_done)
                    sync.wait_ge(s_v.h, gS[ICH - 1])
                if gidx >= 2:
                    sync.wait_ge(s_t.h, wd_used[gidx - 2])
                sync.dma_start(out=wdn_buf[:, gidx % 2, :, :],
                               in_=wdnT_h[kg * 2:(kg + 1) * 2, :, :].transpose([1, 0, 2])).then_inc(s_wd[gidx % 2].h, 16)
            wd_ld[gidx] = s_wd[gidx % 2].inc(16)

            @block.tensor
            def _(t, w=w, kg=kg, gidx=gidx):
                if kg == 0:
                    for pi in range(8):
                        bank_wait(t, pi)
                t.wait_ge(s_wd[gidx % 2].h, wd_ld[gidx])
                for kk in range(2):
                    k = kg * 2 + kk
                    t.wait_ge(s_s.h, ob_done[k])
                    for m in range(4):
                        for n in range(2):
                            mm = t.matmul(ps[m * 2 + n][:, :],
                                          obf[k][:, (w * 4 + m) * 128:(w * 4 + m + 1) * 128],
                                          wdn_buf[:, gidx % 2, kk, n * 512:(n + 1) * 512],
                                          start=(k == 0), stop=(k == ICH - 1))
                mm.then_inc(s_t.h, 1)
            wd_used[gidx] = s_t.inc(1)
        f3b_t[w] = wd_used[w * 12 + 11]

        @block.vector
        def _(v, w=w):
            v.wait_ge(s_t.h, f3b_t[w])
            v.wait_ge(s_xl[w % 2].h, xln_rb[w])
            for m in range(4):
                if w >= 1:
                    v.wait_ge(s_tr.h, yout[(w - 1) * 4 + 3])
                for n in range(2):
                    inst = v.tensor_tensor(y_stage[:, m, n * 512:(n + 1) * 512],
                                           ps[m * 2 + n][:, :],
                                           xlnrb[:, m, n * 512:(n + 1) * 512],
                                           ALU.add)
                fence(v, s_v, inst)
                v.bn_stats(bn6[:, m, 0:6], y_stage[:, m, 0:512])
                inst = v.bn_stats(bn6[:, m, 6:12], y_stage[:, m, 512:1024])
                fence(v, s_v, inst)
                v.bn_aggr(mvw[:, m, :], bn6[:, m, 0:12])
                v.engine_nop().then_inc(s_v.h, 1)
                f3b_res[(w, m)] = s_v.inc(1)

        for pi in range(8):
            bank_free[pi] = ("v", f3b_res[(w, 3)])

        @block.scalar
        def _(s, w=w):
            s.wait_ge(s_v.h, f3b_res[(w, 3)])
            s.activation(out=sdw[:, :], in_=mvw[:, :, 1], func=AF.Sqrt,
                         bias=ccol[:, 1:2], scale=1.0).then_inc(s_s.h, 1)
        s_sdw = s_s.inc(1)

        @block.vector
        def _(v, w=w, s_sdw=s_sdw):
            v.wait_ge(s_s.h, s_sdw)
            inst = v.reciprocal(rstdw[:, :], sdw[:, :])
            fence(v, s_v, inst)
            v.scalar_tensor_tensor(nmrw[:, :], mvw[:, :, 0], -1.0, rstdw[:, :],
                                   ALU.mult, ALU.mult)
            v.engine_nop().then_inc(s_v.h, 1)
        v_rw = s_v.inc(1)

        for m in range(4):
            @block.scalar
            def _(s, w=w, m=m, v_rw=v_rw):
                if m == 0:
                    s.wait_ge(s_v.h, v_rw)
                b = m
                inst = s.activation(out=y_stage[:, b, :], in_=y_stage[:, b, :],
                                    func=AF.Identity, bias=nmrw[:, m:m + 1],
                                    scale=rstdw[:, m:m + 1])
                fence(s, s_s, inst)
                tc = w * 4 + m
                s.dma_start(out=y_d[tc * 128:(tc + 1) * 128, :],
                            in_=y_stage[:, b, :]).then_inc(s_tr.h, 16)
            yout[w * 4 + m] = s_tr.inc(16)

    @block.sync
    def _(sync):
        sync.wait_ge(s_tr.h, yout[15])

    return finish()


_CACHE = {}


def _get_program():
    key = (_STOP_AFTER, tuple(sorted(_DBG)))
    if key not in _CACHE:
        _CACHE[key] = build_program()
    return _CACHE[key]


def kernel(L0_attn_x, L0_attn_wg_w, L0_attn_wg_b, L0_attn_wq_w, L0_attn_wq_b,
           L0_attn_wk_w, L0_attn_wk_b, L0_attn_wv_w, L0_attn_wv_b,
           L0_attn_out_w, L0_attn_out_b, L0_ffn_w_gate, L0_ffn_w_up,
           L0_ffn_w_down):
    nc, _ = _get_program()
    x = np.ascontiguousarray(np.asarray(L0_attn_x, np.float32))
    com = {
        "wq": np.ascontiguousarray(L0_attn_wq_w, np.float32),
        "wk": np.ascontiguousarray(L0_attn_wk_w, np.float32),
        "wv": np.ascontiguousarray(L0_attn_wv_w, np.float32),
        "wg": np.ascontiguousarray(L0_attn_wg_w, np.float32),
        "wo": np.ascontiguousarray(L0_attn_out_w, np.float32),
        "bq": np.ascontiguousarray(L0_attn_wq_b, np.float32).reshape(1, D),
        "bk": np.ascontiguousarray(L0_attn_wk_b, np.float32).reshape(1, D),
        "bv": np.ascontiguousarray(L0_attn_wv_b, np.float32).reshape(1, D),
        "bg": np.ascontiguousarray(L0_attn_wg_b, np.float32).reshape(1, D),
        "bo": np.ascontiguousarray(L0_attn_out_b, np.float32).reshape(1, D),
        "wup": np.ascontiguousarray(L0_ffn_w_up, np.float32),
        "wgt": np.ascontiguousarray(L0_ffn_w_gate, np.float32),
        "wdn": np.ascontiguousarray(L0_ffn_w_down, np.float32),
        "ident": np.eye(128, dtype=np.float32),
    }
    in_maps = [dict(com, x=np.ascontiguousarray(x[b])) for b in range(8)]
    res = run_bass_kernel_spmd(nc, in_maps, core_ids=list(range(8)))
    return np.stack([res.results[b]["y"] for b in range(8)], axis=0)


# version 18
# speedup vs baseline: 1.0341x; 1.0037x over previous
import numpy as np
import concourse.bass as bass
import concourse.mybir as mybir
from concourse.bass_utils import run_bass_kernel_spmd

F32 = mybir.dt.float32
BF16 = mybir.dt.bfloat16
AF = mybir.ActivationFunctionType
ALU = mybir.AluOpType
AX = mybir.AxisListType

MAGIC = float(np.float32(1.5 * 2 ** 23))
INV_Q16 = float(np.float32(1.0 / 32767.0))
INV_QW = float(np.float32(1.0 / 7.0))
INV_2P16 = float(np.float32(1.0 / 65536.0))

T = 2048
D = 1024
TCH = 16
FCH = 8
II = 3072
ICH = 24
NPAIR = 8
N_KEEP = 18

_DBG = set()
_STOP_AFTER = None


class _Sem:
    def __init__(self, nc, name):
        self.h = nc.semaphore(name).__enter__()
        self.v = 0

    def inc(self, n=1):
        self.v += n
        return self.v


def build_program():
    nc = bass.Bass("TRN2", target_bir_lowering=False, debug=False,
                   detect_race_conditions=False)

    x_d = nc.dram_tensor("x", [T, D], F32, kind="ExternalInput").ap()
    ident_d = nc.dram_tensor("ident", [128, 128], F32, kind="ExternalInput").ap()
    w_d = {nm: nc.dram_tensor(nm, [D, D], F32, kind="ExternalInput").ap()
           for nm in ("wq", "wk", "wv", "wg", "wo")}
    b_d = {nm: nc.dram_tensor(nm, [1, D], F32, kind="ExternalInput").ap()
           for nm in ("bq", "bk", "bv", "bg", "bo")}
    wup_d = nc.dram_tensor("wup", [II, D], F32, kind="ExternalInput").ap()
    wgt_d = nc.dram_tensor("wgt", [II, D], F32, kind="ExternalInput").ap()
    wdn_d = nc.dram_tensor("wdn", [D, II], F32, kind="ExternalInput").ap()
    y_d = nc.dram_tensor("y", [T, D], F32, kind="ExternalOutput").ap()

    wupT_h = nc.dram_tensor("wupT_h", [FCH, 128, II], BF16).ap()
    wgtT_h = nc.dram_tensor("wgtT_h", [FCH, 128, II], BF16).ap()
    wdnT_h = nc.dram_tensor("wdnT_h", [ICH, 128, D], BF16).ap()
    up_h = nc.dram_tensor("up_h", [ICH, 128, T], F32).ap()
    gate_h = nc.dram_tensor("gate_h", [ICH, 128, T], F32).ap()
    g2_h = nc.dram_tensor("g2_h", [ICH, 128, T], F32).ap()
    op_h = nc.dram_tensor("op_h", [ICH, 128, T], F32).ap()
    xln_h = nc.dram_tensor("xln_h", [TCH, 128, D], F32).ap()
    cc_in = [nc.dram_tensor(f"cc_in{k}", [2, 8], F32).ap() for k in range(4)]
    cc_out = [nc.dram_tensor(f"cc_out{k}", [16, 8], F32).ap() for k in range(4)]

    dbg_d = {}

    def dbg_out(name, shape, dtype=F32):
        dbg_d[name] = nc.dram_tensor("dbg_" + name, shape, dtype,
                                     kind="ExternalOutput").ap()
        return dbg_d[name]

    live = {}

    def alloc(name, shape, dtype, side="left"):
        cm = nc.sbuf_tensor(name, shape, dtype, side=side)
        t = cm.__enter__()
        live[name] = cm
        return t

    def free(name):
        live.pop(name).__exit__(None, None, None)

    ident = alloc("identt", [128, 128], F32)
    ones_f = alloc("ones_f", [1, 128], F32)
    zbuf = alloc("zbuf", [128, 16], F32)
    zs32 = alloc("zs32", [128, 32], F32)
    ccol = alloc("ccol", [128, 3], F32)
    ident_bf = alloc("ident_bf", [128, 128], BF16)
    zrow = alloc("zrow", [2, 8], F32)
    zsum = alloc("zsum", [128, 16], F32)
    bn6 = alloc("bn6", [128, TCH, 12], F32)
    mvall = alloc("mvall", [128, TCH, 2], F32)
    sdall = alloc("sdall", [128, TCH], F32)
    rstd = alloc("rstd", [128, TCH], F32)
    nmr = alloc("nmr", [128, TCH], F32)
    am24 = alloc("am24", [128, ICH], F32)
    am24b = alloc("am24b", [128, ICH], F32)
    am1 = alloc("am1", [128, 2], F32)
    lmax = alloc("lmax", [2, 1], F32)
    gth = alloc("gth", [1, 16], F32)
    smax = alloc("smax", [1, 12], F32)
    bc = alloc("bc", [128, 8], F32)
    nmcol = alloc("nmcol", [128, 4], F32)
    mvw = alloc("mvw", [128, 4, 2], F32)
    sdw = alloc("sdw", [128, 4], F32)
    rstdw = alloc("rstdw", [128, 4], F32)
    nmrw = alloc("nmrw", [128, 4], F32)
    PW = 512
    DQNB = PW // 32
    dq_in = alloc("dq_in", [128, 4, PW], F32, side="right")
    dq_bm = alloc("dq_bm", [128, 4, DQNB], F32, side="right")
    dq_s = alloc("dq_s", [128, 4, DQNB], F32, side="right")
    dq_si = alloc("dq_si", [128, 4, DQNB], F32, side="right")
    dq_bf = alloc("dq_bf", [128, 4, PW], BF16, side="right")
    dq_tr = alloc("dq_tr", [128, 4, PW // 128, 128], BF16, side="right")
    wslotA = alloc("wslotA", [128, FCH, D], BF16, side="right")
    ones_bf = alloc("ones_bf", [1, T], BF16, side="right")
    ones_col = alloc("ones_col", [128, 1], BF16, side="right")
    bias_sb = alloc("bias_sb", [1, 5, D], BF16, side="right")
    kvaug = alloc("kvaug", [128, NPAIR, 128], BF16, side="right")
    zk = alloc("zk", [128, 32], BF16, side="right")
    fs_a = alloc("fs_a", [128, 1024], F32, side="right")
    fs_b = alloc("fs_b", [128, 1024], F32, side="right")
    wslotB = alloc("wslotB", [128, FCH, D], BF16, side="right")

    ps = [nc.psum_tensor(f"ps{i}", [128, 512], F32).__enter__() for i in range(8)]

    s_ld = _Sem(nc, "s_ld")
    s_tr = _Sem(nc, "s_tr")
    s_gd = _Sem(nc, "s_gd")
    s_v = _Sem(nc, "s_v")
    s_s = _Sem(nc, "s_s")
    s_t = _Sem(nc, "s_t")
    s_g = _Sem(nc, "s_g")
    s_cc = _Sem(nc, "s_cc")
    s_out = _Sem(nc, "s_out")
    s_id = _Sem(nc, "s_id")
    s_gx = _Sem(nc, "s_gx")
    s_gb = _Sem(nc, "s_gb")
    s_gwA = _Sem(nc, "s_gwA")
    s_gwB = _Sem(nc, "s_gwB")
    s_dqld = [_Sem(nc, f"s_dqld{i}") for i in range(4)]
    s_dqtr = _Sem(nc, "s_dqtr")
    s_dqtr2 = [_Sem(nc, "s_dqtrA"), _Sem(nc, "s_dqtrB")]
    s_dqst = {"up": _Sem(nc, "s_dqstU"), "gt": _Sem(nc, "s_dqstG"),
              "dn": _Sem(nc, "s_dqstD")}
    s_x2 = _Sem(nc, "s_x2")
    s_wup = _Sem(nc, "s_wup")
    s_wgt = _Sem(nc, "s_wgt")
    s_wdn = _Sem(nc, "s_wdn")
    s_f1sp = {("u", 0): _Sem(nc, "s_f1spU0"), ("u", 1): _Sem(nc, "s_f1spU1"),
              ("g", 0): _Sem(nc, "s_f1spG0"), ("g", 1): _Sem(nc, "s_f1spG1")}
    s_gr = [_Sem(nc, "s_gr0"), _Sem(nc, "s_gr1"), _Sem(nc, "s_gr2")]
    s_g2st = _Sem(nc, "s_g2st")
    s_ur = [_Sem(nc, "s_ur0"), _Sem(nc, "s_ur1"), _Sem(nc, "s_ur2")]
    s_g2r = [_Sem(nc, "s_g2r0"), _Sem(nc, "s_g2r1")]
    s_opst = _Sem(nc, "s_opst")
    s_opr = [_Sem(nc, "s_opr0"), _Sem(nc, "s_opr1")]
    s_xl = [_Sem(nc, "s_xl0"), _Sem(nc, "s_xl1")]
    s_wd = [_Sem(nc, "s_wd0"), _Sem(nc, "s_wd1")]
    s_dv = _Sem(nc, "s_dv")
    s_xsp = _Sem(nc, "s_xsp")
    all_dma_sems = ([s_ld, s_tr, s_gd, s_id, s_gx, s_gb, s_gwA, s_gwB, s_dqtr,
                     s_x2, s_wup, s_wgt, s_wdn, s_g2st, s_opst, s_xsp]
                    + s_dqld + list(s_dqst.values()) + list(s_f1sp.values())
                    + s_gr + s_ur + s_g2r + s_opr + s_xl + s_wd + s_dqtr2)

    semmap = {"v": s_v, "s": s_s, "g": s_g, "t": s_t, "ld": s_ld, "tr": s_tr}
    bank_free = [("v", 0)] * 8

    def bank_wait(eng, i):
        nm, val = bank_free[i]
        if val > 0:
            eng.wait_ge(semmap[nm].h, val)

    def fence(eng, sem, inst):
        inst.then_inc(sem.h, 1)
        eng.wait_ge(sem.h, sem.inc(1))

    def pe_transpose(dst_fm, src_tm, banks, pre_twait=None, idx_wait=None):
        base_t = s_t.v
        base_v = s_v.v
        nblk = TCH * FCH

        @block.tensor
        def _(t):
            if pre_twait is not None:
                pre_twait(t)
            for pb in banks:
                bank_wait(t, pb)
            for idx in range(nblk):
                c, f = divmod(idx, FCH)
                pb = banks[idx % 2]
                if idx_wait is not None:
                    idx_wait(t, c, f)
                if idx >= 2:
                    t.wait_ge(s_v.h, base_v + idx - 1)
                t.transpose(ps[pb][:, 0:64].bitcast(BF16),
                            src_tm[:, c, f * 128:(f + 1) * 128],
                            ident_bf[:, :]).then_inc(s_t.h, 1)
                s_t.inc(1)

        @block.vector
        def _(v):
            for idx in range(nblk):
                c, f = divmod(idx, FCH)
                pb = banks[idx % 2]
                v.wait_ge(s_t.h, base_t + idx + 1)
                v.tensor_copy(dst_fm[:, f, c * 128:(c + 1) * 128],
                              ps[pb][:, 0:64].bitcast(BF16)).then_inc(s_v.h, 1)
                s_v.inc(1)
        done = s_v.v
        for pb in banks:
            bank_free[pb] = ("v", done)
        return done

    blk_cm = nc.Block()
    block = blk_cm.__enter__()

    def finish(dump_emitter=None):
        if dump_emitter is not None:
            dump_emitter()
        blk_cm.__exit__(None, None, None)
        return nc, dbg_d

    def dump_tm(tag, buf_ap3, nch, w, dtype=F32):
        o = dbg_out(tag, [nch * 128, w], dtype)
        waits = [(s_v, s_v.v), (s_s, s_s.v), (s_t, s_t.v), (s_g, s_g.v)]
        waits += [(sm, sm.v) for sm in all_dma_sems]

        @block.gpsimd
        def _(g):
            for sm, val in waits:
                if val:
                    g.wait_ge(sm.h, val)
            g.dma_start(out=o.rearrange("(c p) w -> p c w", p=128),
                        in_=buf_ap3).then_inc(s_out.h, 16)
            g.wait_ge(s_out.h, s_out.inc(16))

    xbf_fm = alloc("xbf_fm", [128, FCH, T], BF16)
    xbf_tm = alloc("xbf_tm", [128, TCH, D], BF16, side="right")

    @block.sync
    def _(sync):
        sync.dma_start(out=ident[:], in_=ident_d[:, :]).then_inc(s_id.h, 16)
    ld_ident = s_id.inc(16)

    @block.gpsimd
    def _(g):
        g.dma_start(out=xbf_tm[:, :, :],
                    in_=x_d.rearrange("(t p) d -> p t d", p=128)).then_inc(s_gx.h, 16)
        for i, nm in enumerate(("bq", "bk", "bv", "bg", "bo")):
            g.dma_start(out=bias_sb[0:1, i, :], in_=b_d[nm][:, :]).then_inc(s_gb.h, 16)
    gd_x = s_gx.inc(16)
    gd_bias = s_gb.inc(16 * 5)

    wslot = [wslotA, wslotB]
    wname2slot = {"wk": 0, "wv": 1, "wg": 0, "wq": 1, "wo": 0}
    gd_w = {}
    w_guard = {}

    wsem = {0: s_gwA, 1: s_gwB}

    def emit_wload(nm):
        slot_sem = wsem[wname2slot[nm]]

        @block.gpsimd
        def _(g):
            gd = w_guard.get(nm)
            if gd is not None:
                g.wait_ge(semmap[gd[0]].h, gd[1])
            g.dma_start(out=wslot[wname2slot[nm]][:, :, :],
                        in_=w_d[nm].rearrange("(k p) d -> p k d", p=128)).then_inc(slot_sem.h, 16)
        gd_w[nm] = (slot_sem, slot_sem.inc(16))

    emit_wload("wk")
    emit_wload("wv")

    @block.vector
    def _(v):
        v.memset(ones_bf[:], 1.0)
        v.memset(ones_f[:], 1.0)
        v.memset(ones_col[:], 1.0)
        v.memset(zk[:], 0.0)
        v.memset(kvaug[:, :, :], 0.0)
        v.memset(ccol[:, 0:1], MAGIC)
        v.memset(ccol[:, 1:2], 1e-5)
        v.memset(ccol[:, 2:3], 65536.0)
        v.memset(zrow[:, :], 0.0)
        v.wait_ge(s_id.h, ld_ident)
        v.tensor_copy(ident_bf[:, :], ident[:, :])
        v.engine_nop().then_inc(s_v.h, 1)
    v_const = s_v.inc(1)

    @block.sync
    def _(sync):
        sync.wait_ge(s_v.h, v_const)
        for k in range(4):
            sync.dma_start(out=cc_in[k][:, :], in_=zrow[:, :]).then_inc(s_id.h, 16)
    ld_cc0 = s_id.inc(16 * 4)

    tr_xbf = pe_transpose(xbf_fm, xbf_tm, (0, 1),
                          pre_twait=lambda t: (t.wait_ge(s_gx.h, gd_x),
                                               t.wait_ge(s_v.h, v_const)))
    free("xbf_tm")

    if _STOP_AFTER == "xbf":
        return finish(lambda: dump_tm("xbf", xbf_fm[:, :, :], FCH, T, BF16))

    dq_state = {}
    dq_qA = []
    dq_qB = []
    dq_qC = []
    dq_last = {}
    dq_total = [0]

    def make_dq_pieces(tag, w_src, n_och, cl, out_h):
        npiece = cl // PW
        ntr = PW // 128

        def stageA(c, pc, i):
            b = i % 4

            @block.sync
            def _(sync, c=c, pc=pc, b=b, i=i):
                if i >= 4:
                    sync.wait_ge(s_g.h, dq_state[("gdq", i - 4)])
                sync.dma_start(out=dq_in[:, b, :],
                               in_=w_src[c * 128:(c + 1) * 128,
                                         pc * PW:(pc + 1) * PW]).then_inc(s_dqld[b].h, 16)
                dq_state[("ld", i)] = s_dqld[b].inc(16)

        def stageB(c, pc, i):
            b = i % 4

            @block.vector
            def _(v, b=b, i=i):
                v.wait_ge(s_dqld[b].h, dq_state[("ld", i)])
                v.tensor_reduce(dq_bm[:, b, :],
                                dq_in[:, b, :].rearrange("p (n q) -> p n q", q=32),
                                AX.X, ALU.max,
                                apply_absolute_value=True).then_inc(s_dv.h, 1)
                if i >= 1:
                    b1 = (i - 1) % 4
                    v.wait_ge(s_dv.h, dq_state[("red", i - 1)])
                    v.tensor_scalar(dq_s[:, b1, :], dq_bm[:, b1, :], INV_QW,
                                    1e-12, ALU.mult,
                                    ALU.max).then_inc(s_dv.h, 1)
                if i >= 2:
                    b2 = (i - 2) % 4
                    v.wait_ge(s_dv.h, dq_state[("tsd", i - 2)])
                    v.reciprocal(dq_si[:, b2, :], dq_s[:, b2, :])
                    v.engine_nop().then_inc(s_dv.h, 1)
            dq_state[("red", i)] = s_dv.inc(1)
            if i >= 1:
                dq_state[("tsd", i - 1)] = s_dv.inc(1)
            if i >= 2:
                dq_state[("vsc", i - 2)] = s_dv.inc(1)

            if i >= 2:
                j = i - 2

                @block.gpsimd
                def _(g, j=j):
                    g.wait_ge(s_dv.h, dq_state[("vsc", j)])
                    if j >= 4:
                        g.wait_ge(s_dqtr2[j % 2].h, dq_state[("tr", j - 4)])
                    bj = j % 4
                    sib = dq_si[:, bj, :].unsqueeze(2).broadcast_to([128, DQNB, 32])
                    w3 = dq_in[:, bj, :].rearrange("p (n q) -> p n q", q=32)
                    g.tensor_tensor(w3, w3, sib, ALU.mult).then_inc(s_g.h, 1)
                dq_state[("gmul", j)] = s_g.inc(1)

        def stageC(c, pc, i):
            b = i % 4

            @block.vector
            def _(v, b=b, i=i):
                v.wait_ge(s_g.h, dq_state[("gmul", i)])
                v.tensor_scalar(dq_in[:, b, :], dq_in[:, b, :], MAGIC, MAGIC,
                                ALU.add, ALU.subtract).then_inc(s_dv.h, 1)
            dq_state[("vrnd", i)] = s_dv.inc(1)

            @block.gpsimd
            def _(g, b=b, i=i):
                g.wait_ge(s_dv.h, dq_state[("vrnd", i)])
                sb_ = dq_s[:, b, :].unsqueeze(2).broadcast_to([128, DQNB, 32])
                w3 = dq_in[:, b, :].rearrange("p (n q) -> p n q", q=32)
                bf3 = dq_bf[:, b, :].rearrange("p (n q) -> p n q", q=32)
                g.tensor_tensor(bf3, w3, sb_, ALU.mult)
                g.engine_nop().then_inc(s_g.h, 1)
            dq_state[("gdq", i)] = s_g.inc(1)

            if b % 2 == 0:
                @block.sync
                def _(q, b=b, i=i):
                    q.wait_ge(s_g.h, dq_state[("gdq", i)])
                    for f in range(ntr):
                        q.dma_start_transpose(
                            dq_tr[:, b, f, :],
                            dq_bf[:, b, f * 128:(f + 1) * 128]).then_inc(s_dqtr2[b % 2].h, 16)
            else:
                @block.sync
                def _(q, b=b, i=i):
                    q.wait_ge(s_g.h, dq_state[("gdq", i)])
                    for f in range(ntr):
                        q.dma_start_transpose(
                            dq_tr[:, b, f, :],
                            dq_bf[:, b, f * 128:(f + 1) * 128]).then_inc(s_dqtr2[b % 2].h, 16)
            dq_state[("tr", i)] = s_dqtr2[b % 2].inc(16 * ntr)

            @block.sync
            def _(sync, c=c, pc=pc, b=b, i=i):
                sync.wait_ge(s_dqtr2[b % 2].h, dq_state[("tr", i)])
                sync.dma_start(
                    out=out_h[pc * ntr:(pc + 1) * ntr, :,
                              c * 128:(c + 1) * 128].transpose([1, 0, 2]),
                    in_=dq_tr[:, b, 0:ntr, :]).then_inc(s_dqst[tag].h, 16)
            dq_last[tag] = s_dqst[tag].inc(16)

        for c in range(n_och):
            for pc in range(npiece):
                i = dq_total[0]
                dq_qA.append((tag, stageA, c, pc, i))
                dq_qB.append((tag, stageB, c, pc, i))
                dq_qC.append((tag, stageC, c, pc, i))
                dq_total[0] += 1

    def drain_B():
        N = dq_total[0]
        for vi in (N, N + 1):
            @block.vector
            def _(v, vi=vi, N=N):
                if vi - 1 < N:
                    b1 = (vi - 1) % 4
                    v.wait_ge(s_dv.h, dq_state[("red", vi - 1)])
                    v.tensor_scalar(dq_s[:, b1, :], dq_bm[:, b1, :], INV_QW,
                                    1e-12, ALU.mult,
                                    ALU.max).then_inc(s_dv.h, 1)
                b2 = (vi - 2) % 4
                v.wait_ge(s_dv.h, dq_state[("tsd", vi - 2)])
                v.reciprocal(dq_si[:, b2, :], dq_s[:, b2, :])
                v.engine_nop().then_inc(s_dv.h, 1)
            if vi - 1 < N:
                dq_state[("tsd", vi - 1)] = s_dv.inc(1)
            dq_state[("vsc", vi - 2)] = s_dv.inc(1)

            j = vi - 2

            @block.gpsimd
            def _(g, j=j):
                g.wait_ge(s_dv.h, dq_state[("vsc", j)])
                if j >= 4:
                    g.wait_ge(s_dqtr2[j % 2].h, dq_state[("tr", j - 4)])
                bj = j % 4
                sib = dq_si[:, bj, :].unsqueeze(2).broadcast_to([128, DQNB, 32])
                w3 = dq_in[:, bj, :].rearrange("p (n q) -> p n q", q=32)
                g.tensor_tensor(w3, w3, sib, ALU.mult).then_inc(s_g.h, 1)
            dq_state[("gmul", j)] = s_g.inc(1)

    dq_drained = [False]

    def pump_dq(k):
        for _ in range(k):
            total0 = dq_total[0]
            cA = total0 - len(dq_qA)
            cB = total0 - len(dq_qB)
            cC = total0 - len(dq_qC)
            if dq_qA and cA <= cC + 3:
                tg, fn, c, pc, i = dq_qA.pop(0)
                fn(c, pc, i)
                cA += 1
            if dq_qB and cB < cA and cB <= cC + 3:
                tg, fn, c, pc, i = dq_qB.pop(0)
                fn(c, pc, i)
                cB += 1
                if not dq_qB and not dq_drained[0]:
                    drain_B()
                    dq_drained[0] = True
            if dq_qC and (cC <= cB - 3 or (dq_drained[0] and cC < cB)):
                tg, fn, c, pc, i = dq_qC.pop(0)
                fn(c, pc, i)

    def pump_dq_tag(tag):
        while ((dq_qA and dq_qA[0][0] == tag) or (dq_qB and dq_qB[0][0] == tag)
               or (dq_qC and dq_qC[0][0] == tag)):
            pump_dq(1)

    k_tm = alloc("k_tm", [128, TCH, D], BF16)
    v_tm = alloc("v_tm", [128, TCH, D], BF16)
    tdone = {}

    def run_tm_proj(wname, bidx, nbank, kind, dst):
        bi = "qkvgo".index(wname[1])
        slot = wslot[wname2slot[wname]]
        t0, v0, s0 = s_t.v, s_v.v, s_s.v
        ntile = TCH * 2

        def tdn(j):
            return t0 + j + 1

        if kind == "fm":
            def sexp(j): return s0 + 2 * j + 2
            def stt(j): return v0 + j + 1
            rel = ("v", stt)
        else:
            def scp(j): return s0 + j + 1
            rel = ("s", scp)

        GT = 8
        for g0 in range(0, ntile, GT):
            @block.tensor
            def _(t, g0=g0):
                if g0 == 0:
                    t.wait_ge(s_v.h, tr_xbf)
                    t.wait_ge(gd_w[wname][0].h, gd_w[wname][1])
                    t.wait_ge(s_gb.h, gd_bias)
                    t.wait_ge(s_v.h, v_const)
                for j in range(g0, g0 + GT):
                    m, n = divmod(j, 2)
                    pi = bidx + j % nbank
                    if j < nbank:
                        bank_wait(t, pi)
                    else:
                        t.wait_ge(semmap[rel[0]].h, rel[1](j - nbank))
                    for kc in range(FCH):
                        t.matmul(ps[pi][:, :],
                                 xbf_fm[:, kc, m * 128:(m + 1) * 128],
                                 slot[:, kc, n * 512:(n + 1) * 512],
                                 start=(kc == 0), stop=False)
                    t.matmul(ps[pi][:, :],
                             ones_bf[0:1, 0:128],
                             bias_sb[0:1, bi, n * 512:(n + 1) * 512],
                             start=False, stop=True).then_inc(s_t.h, 1)
                    s_t.inc(1)
                    tdone[(wname, m, n)] = s_t.v

            if kind == "fm":
                @block.scalar
                def _(s, g0=g0):
                    for j in range(g0, g0 + GT):
                        if j >= 1:
                            jj = j - 1
                            s.wait_ge(s_s.h, s0 + 2 * jj + 1)
                            if jj >= 2:
                                s.wait_ge(s_v.h, v0 + jj - 1)
                            s.activation(out=fs_b[:, (jj % 2) * 512:(jj % 2) * 512 + 512],
                                         in_=fs_a[:, (jj % 2) * 512:(jj % 2) * 512 + 512],
                                         func=AF.Exp, scale=-1.0).then_inc(s_s.h, 1)
                            s_s.inc(1)
                        s.wait_ge(s_t.h, tdn(j))
                        s.activation(out=fs_a[:, (j % 2) * 512:(j % 2) * 512 + 512],
                                     in_=ps[bidx + j % nbank][:, :],
                                     func=AF.Relu, scale=-1.0).then_inc(s_s.h, 1)
                        s_s.inc(1)
                    if g0 + GT == ntile:
                        jj = ntile - 1
                        s.wait_ge(s_s.h, s0 + 2 * jj + 1)
                        s.wait_ge(s_v.h, v0 + jj - 1)
                        s.activation(out=fs_b[:, (jj % 2) * 512:(jj % 2) * 512 + 512],
                                     in_=fs_a[:, (jj % 2) * 512:(jj % 2) * 512 + 512],
                                     func=AF.Exp, scale=-1.0).then_inc(s_s.h, 1)
                        s_s.inc(1)

                @block.vector
                def _(v, g0=g0):
                    for j in range(g0, g0 + GT):
                        m, n = divmod(j, 2)
                        pi = bidx + j % nbank
                        v.wait_ge(s_s.h, sexp(j))
                        v.scalar_tensor_tensor(dst[:, m, n * 512:(n + 1) * 512],
                                               ps[pi][:, :], 0.0,
                                               fs_b[:, (j % 2) * 512:(j % 2) * 512 + 512],
                                               ALU.max, ALU.add).then_inc(s_v.h, 1)
                        s_v.inc(1)
            else:
                @block.scalar
                def _(s, g0=g0):
                    for j in range(g0, g0 + GT):
                        m, n = divmod(j, 2)
                        pi = bidx + j % nbank
                        s.wait_ge(s_t.h, tdn(j))
                        s.activation(out=dst[:, m, n * 512:(n + 1) * 512],
                                     in_=ps[pi][:, :], func=kind).then_inc(s_s.h, 1)
                        s_s.inc(1)
            pump_dq(4)

        for j in range(ntile - nbank, ntile):
            bank_free[bidx + j % nbank] = (rel[0], rel[1](j))

    make_dq_pieces("up", wup_d, ICH, D, wupT_h)
    make_dq_pieces("gt", wgt_d, ICH, D, wgtT_h)
    make_dq_pieces("dn", wdn_d, FCH, II, wdnT_h)

    run_tm_proj("wk", 0, 8, "fm", k_tm)
    run_tm_proj("wv", 0, 8, AF.Identity, v_tm)
    v_kevac = s_v.v
    s_vevac = s_s.v

    w_guard["wg"] = ("t", tdone[("wk", TCH - 1, 1)])
    w_guard["wq"] = ("t", tdone[("wv", TCH - 1, 1)])
    emit_wload("wg")
    emit_wload("wq")

    if _STOP_AFTER == "ktm":
        return finish(lambda: (dump_tm("ktm", k_tm[:, :, :], TCH, D, BF16),
                               dump_tm("vtm", v_tm[:, :, :], TCH, D, BF16)))

    @block.tensor
    def _(t):
        t.wait_ge(s_v.h, v_kevac)
        t.wait_ge(s_s.h, s_vevac)
        for p in range(NPAIR):
            bank_wait(t, p)
        for p in range(NPAIR):
            for c in range(TCH):
                t.matmul(ps[p][:, 0:128],
                         k_tm[:, c, p * 128:(p + 1) * 128],
                         v_tm[:, c, p * 128:(p + 1) * 128],
                         start=(c == 0), stop=(c == TCH - 1),
                         skip_group_check=True)
            for c in range(TCH):
                mm = t.matmul(ps[p][:, 128:129],
                              k_tm[:, c, p * 128:(p + 1) * 128],
                              ones_col[:, :],
                              start=(c == 0), stop=(c == TCH - 1),
                              skip_group_check=True)
        mm.then_inc(s_t.h, 1)
    t_kv = s_t.inc(1)

    @block.vector
    def _(v):
        v.wait_ge(s_t.h, t_kv)
        for p in range(NPAIR):
            v.tensor_copy(kvaug[0:64, p, 0:64], ps[p][0:64, 0:64])
            v.tensor_copy(kvaug[64:128, p, 64:128], ps[p][64:128, 64:128])
            v.tensor_copy(zk[0:64, 4 * p:4 * p + 1], ps[p][0:64, 128:129])
            last = v.tensor_copy(zk[64:128, 4 * p + 2:4 * p + 3], ps[p][64:128, 128:129])
        fence(v, s_v, last)
        for p in range(NPAIR):
            v.tensor_tensor(zk[0:64, 4 * p + 1:4 * p + 2], ps[p][0:64, 128:129],
                            zk[0:64, 4 * p:4 * p + 1], ALU.subtract)
            v.tensor_tensor(zk[64:128, 4 * p + 3:4 * p + 4], ps[p][64:128, 128:129],
                            zk[64:128, 4 * p + 2:4 * p + 3], ALU.subtract)
        v.engine_nop().then_inc(s_v.h, 1)
    v_kv = s_v.inc(1)
    for i in range(8):
        bank_free[i] = ("v", v_kv)

    free("v_tm")
    pump_dq(3)

    if _STOP_AFTER == "kv":
        def d():
            o = dbg_out("kvaug", [128, NPAIR * 128], BF16)
            o2 = dbg_out("zkd", [128, 32], BF16)

            @block.gpsimd
            def _(g):
                g.wait_ge(s_v.h, v_kv)
                g.dma_start(out=o[:, :], in_=kvaug[:, :, :]).then_inc(s_out.h, 16)
                g.dma_start(out=o2[:, :], in_=zk[:, :]).then_inc(s_out.h, 16)
                g.wait_ge(s_out.h, s_out.inc(32))
        return finish(d)

    gate_tm = alloc("gate_tm", [128, TCH, D], BF16)
    run_tm_proj("wg", 0, 8, AF.Sigmoid, gate_tm)

    w_guard["wo"] = ("t", tdone[("wg", TCH - 1, 1)])
    emit_wload("wo")

    q_fm = alloc("q_fm", [128, FCH, T], BF16)

    _qt0, _qv0, _qs0 = s_t.v, s_v.v, s_s.v

    for g0 in range(0, 32, 8):
        @block.tensor
        def _(t, g0=g0):
            if g0 == 0:
                t.wait_ge(gd_w['wq'][0].h, gd_w['wq'][1])
            for j in range(g0, g0 + 8):
                m, n = divmod(j, 4)
                pi = j % 8
                if j < 8:
                    bank_wait(t, pi)
                else:
                    t.wait_ge(s_v.h, _qv0 + (j - 8) + 1)
                for kc in range(FCH):
                    t.matmul(ps[pi][:, :],
                             wslot[1][:, kc, m * 128:(m + 1) * 128],
                             xbf_fm[:, kc, n * 512:(n + 1) * 512],
                             start=(kc == 0), stop=False)
                t.matmul(ps[pi][:, :],
                         bias_sb[0:1, 0, m * 128:(m + 1) * 128],
                         ones_bf[0:1, n * 512:(n + 1) * 512],
                         start=False, stop=True).then_inc(s_t.h, 1)
                s_t.inc(1)
                tdone[("q", m, n)] = s_t.v

        @block.scalar
        def _(s, g0=g0):
            for j in range(g0, g0 + 8):
                if j >= 1:
                    jj = j - 1
                    s.wait_ge(s_s.h, _qs0 + 2 * jj + 1)
                    if jj >= 2:
                        s.wait_ge(s_v.h, _qv0 + jj - 1)
                    s.activation(out=fs_b[:, (jj % 2) * 512:(jj % 2) * 512 + 512],
                                 in_=fs_a[:, (jj % 2) * 512:(jj % 2) * 512 + 512],
                                 func=AF.Exp, scale=-1.0).then_inc(s_s.h, 1)
                    s_s.inc(1)
                s.wait_ge(s_t.h, _qt0 + j + 1)
                s.activation(out=fs_a[:, (j % 2) * 512:(j % 2) * 512 + 512],
                             in_=ps[j % 8][:, :],
                             func=AF.Relu, scale=-1.0).then_inc(s_s.h, 1)
                s_s.inc(1)
            if g0 + 8 == 32:
                jj = 31
                s.wait_ge(s_s.h, _qs0 + 2 * jj + 1)
                s.wait_ge(s_v.h, _qv0 + jj - 1)
                s.activation(out=fs_b[:, (jj % 2) * 512:(jj % 2) * 512 + 512],
                             in_=fs_a[:, (jj % 2) * 512:(jj % 2) * 512 + 512],
                             func=AF.Exp, scale=-1.0).then_inc(s_s.h, 1)
                s_s.inc(1)

        @block.vector
        def _(v, g0=g0):
            for j in range(g0, g0 + 8):
                m, n = divmod(j, 4)
                pi = j % 8
                v.wait_ge(s_s.h, _qs0 + 2 * j + 2)
                v.scalar_tensor_tensor(q_fm[:, m, n * 512:(n + 1) * 512],
                                       ps[pi][:, :], 0.0,
                                       fs_b[:, (j % 2) * 512:(j % 2) * 512 + 512],
                                       ALU.max, ALU.add).then_inc(s_v.h, 1)
                s_v.inc(1)
        pump_dq(4)

    for j in range(24, 32):
        bank_free[j % 8] = ("v", _qv0 + j + 1)

    free("wslotB")

    if _STOP_AFTER == "qgate":
        return finish(lambda: (dump_tm("qfm", q_fm[:, :, :], FCH, T, BF16),
                               dump_tm("gatetm", gate_tm[:, :, :], TCH, D, BF16)))

    gm_tm = k_tm
    sc_t = {}
    gate_all = s_s.v
    q_all = s_v.v

    for c0 in range(0, TCH, 2):
        @block.tensor
        def _(t, c0=c0):
            for c in (c0, c0 + 1):
                pa, pb, pz = (0, 1, 2) if c % 2 == 0 else (3, 4, 5)
                if c == 0:
                    t.wait_ge(s_v.h, max(v_kv, q_all))
                for pi in (pa, pb, pz):
                    bank_wait(t, pi)
                for p in range(NPAIR):
                    dst = ps[pa] if p < 4 else ps[pb]
                    t.matmul(dst[:, (p % 4) * 128:(p % 4) * 128 + 128],
                             q_fm[:, p, c * 128:(c + 1) * 128],
                             kvaug[:, p, :], start=True, stop=True,
                             skip_group_check=True)
                    mm = t.matmul(ps[pz][:, p * 4:(p + 1) * 4],
                                  q_fm[:, p, c * 128:(c + 1) * 128],
                                  zk[:, p * 4:(p + 1) * 4], start=True, stop=True,
                                  skip_group_check=True)
                mm.then_inc(s_t.h, 1)
                sc_t[c] = s_t.inc(1)

        @block.vector
        def _(v, c0=c0):
            for c in (c0, c0 + 1):
                pa, pb, pz = (0, 1, 2) if c % 2 == 0 else (3, 4, 5)
                v.wait_ge(s_t.h, sc_t[c])
                if c == 0:
                    v.wait_ge(s_s.h, gate_all)
                inst = v.tensor_copy(zs32[:, :], ps[pz][:, 0:32])
                fence(v, s_v, inst)
                z3 = zs32[:, :].rearrange("p (h t) -> p h t", t=2)
                inst = v.scalar_tensor_tensor(zsum[:, :], z3[:, :, 0], 1e-6,
                                              z3[:, :, 1], ALU.add, ALU.add)
                fence(v, s_v, inst)
                inst = v.reciprocal(zbuf[:, :], zsum[:, :])
                fence(v, s_v, inst)
                for half, pp in ((0, pa), (1, pb)):
                    zv = zbuf[:, half * 8:(half + 1) * 8].unsqueeze(2).broadcast_to([128, 8, 64])
                    inst = v.tensor_tensor(fs_a[:, 0:512].rearrange("p (h e) -> p h e", e=64),
                                           ps[pp][:, :].rearrange("p (h e) -> p h e", e=64),
                                           zv, ALU.mult)
                    fence(v, s_v, inst)
                    v.tensor_tensor(gm_tm[:, c, half * 512:(half + 1) * 512],
                                    fs_a[:, 0:512],
                                    gate_tm[:, c, half * 512:(half + 1) * 512],
                                    ALU.mult)
                v.engine_nop().then_inc(s_v.h, 1)
                val = s_v.inc(1)
                for pi in (pa, pb, pz):
                    bank_free[pi] = ("v", val)
        pump_dq(2)
    v_gm_all = s_v.v

    if _STOP_AFTER == "gm":
        return finish(lambda: dump_tm("gmtm", gm_tm[:, :, :], TCH, D, BF16))

    gm_fm = xbf_fm
    tr_gm = pe_transpose(gm_fm, gm_tm, (6, 7))

    free("q_fm"); free("gate_tm"); free("k_tm")
    free("fs_b"); free("fs_a"); free("zk"); free("kvaug")
    x_tm = alloc("x_tm", [128, TCH, D], F32)

    @block.sync
    def _(sync):
        sync.wait_ge(s_v.h, tr_gm)
        sync.dma_start(out=x_tm[:, :, :],
                       in_=x_d.rearrange("(t p) d -> p t d", p=128)).then_inc(s_x2.h, 16)
    ld_x2 = s_x2.inc(16)

    o_t = {}
    _ot0, _ov0 = s_t.v, s_v.v

    for g0 in range(0, 32, 8):
        @block.tensor
        def _(t, g0=g0):
            if g0 == 0:
                t.wait_ge(s_v.h, tr_gm)
                t.wait_ge(gd_w['wo'][0].h, gd_w['wo'][1])
            for j in range(g0, g0 + 8):
                m, n = divmod(j, 2)
                pi = j % 8
                if j < 8:
                    bank_wait(t, pi)
                else:
                    t.wait_ge(s_v.h, _ov0 + (j - 8) // 2 + 1)
                for kc in range(FCH):
                    t.matmul(ps[pi][:, :],
                             gm_fm[:, kc, m * 128:(m + 1) * 128],
                             wslot[0][:, kc, n * 512:(n + 1) * 512],
                             start=(kc == 0), stop=False)
                t.matmul(ps[pi][:, :],
                         ones_bf[0:1, 0:128],
                         bias_sb[0:1, 4, n * 512:(n + 1) * 512],
                         start=False, stop=True).then_inc(s_t.h, 1)
                s_t.inc(1)
                o_t[(m, n)] = s_t.v
        pump_dq(2)

    res_last = {}

    @block.vector
    def _(v):
        v.wait_ge(s_x2.h, ld_x2)
        for m in range(TCH):
            for n in range(2):
                v.wait_ge(s_t.h, o_t[(m, n)])
                v.tensor_tensor(x_tm[:, m, n * 512:(n + 1) * 512],
                                ps[(m * 2 + n) % 8][:, :],
                                x_tm[:, m, n * 512:(n + 1) * 512], ALU.add)
            v.engine_nop().then_inc(s_v.h, 1)
            s_v.inc(1)
            res_last[m] = s_v.v
    for m in range(TCH - 4, TCH):
        for n in range(2):
            bank_free[(m * 2 + n) % 8] = ("v", res_last[m])

    last_vres = s_v.v

    @block.vector
    def _(v):
        v.wait_ge(s_v.h, last_vres)
        for m in range(TCH):
            v.bn_stats(bn6[:, m, 0:6], x_tm[:, m, 0:512])
            inst = v.bn_stats(bn6[:, m, 6:12], x_tm[:, m, 512:1024])
        fence(v, s_v, inst)
        for m in range(TCH):
            v.bn_aggr(mvall[:, m, :], bn6[:, m, 0:12])
        v.engine_nop().then_inc(s_v.h, 1)
    v_bn1 = s_v.inc(1)

    @block.scalar
    def _(s):
        s.wait_ge(s_v.h, v_bn1)
        s.activation(out=sdall[:, :], in_=mvall[:, :, 1], func=AF.Sqrt,
                     bias=ccol[:, 1:2], scale=1.0).then_inc(s_s.h, 1)
    s_sd1 = s_s.inc(1)

    @block.vector
    def _(v):
        v.wait_ge(s_s.h, s_sd1)
        inst = v.reciprocal(rstd[:, :], sdall[:, :])
        fence(v, s_v, inst)
        v.scalar_tensor_tensor(nmr[:, :], mvall[:, :, 0], -1.0, rstd[:, :],
                               ALU.mult, ALU.mult)
        v.engine_nop().then_inc(s_v.h, 1)
    v_r1 = s_v.inc(1)

    ln1_done = {}

    @block.scalar
    def _(s):
        s.wait_ge(s_v.h, v_r1)
        for m in range(TCH):
            s.activation(out=x_tm[:, m, :], in_=x_tm[:, m, :], func=AF.Identity,
                         bias=nmr[:, m:m + 1], scale=rstd[:, m:m + 1]).then_inc(s_s.h, 1)
            s_s.inc(1)
            ln1_done[m] = s_s.v

    if _STOP_AFTER == "ln1":
        return finish(lambda: dump_tm("xln", x_tm[:, :, :], TCH, D, F32))

    def barrier(bid, nvals, stat_ap, post_vector, extra_vwait=None):
        @block.tensor
        def _(t):
            t.wait_ge(s_id.h, ld_ident)
            if extra_vwait:
                t.wait_ge(s_v.h, extra_vwait)
            bank_wait(t, 7)
            t.transpose(ps[7][0:nvals, 0:128], stat_ap, ident[:]).then_inc(s_t.h, 1)
        tt = s_t.inc(1)

        @block.vector
        def _(v):
            v.wait_ge(s_t.h, tt)
            v.tensor_reduce(lmax[0:nvals, 0:1], ps[7][0:nvals, :], AX.X,
                            ALU.max).then_inc(s_v.h, 1)
        vv = s_v.inc(1)
        bank_free[7] = ("v", vv)

        @block.scalar
        def _(s):
            s.wait_ge(s_v.h, vv)
            s.wait_ge(s_id.h, ld_cc0)
            s.dma_start(out=cc_in[bid][0:nvals, 0:1], in_=lmax[0:nvals, 0:1],
                        allow_slow_non_contiguous=True).then_inc(s_tr.h, 16)
        dd = s_tr.inc(16)

        @block.gpsimd
        def _(g):
            g.wait_ge(s_tr.h, dd)
            g.collective_compute(
                "AllGather", ALU.bypass,
                replica_groups=[list(range(8))],
                ins=[cc_in[bid][:, :].opt()],
                outs=[cc_out[bid][:, :].opt()]).then_inc(s_cc.h, 1)
        cc = s_cc.inc(1)

        @block.scalar
        def _(s):
            s.wait_ge(s_cc.h, cc)
            s.dma_start(out=gth[0:1, 0:16],
                        in_=cc_out[bid][:, 0:1].rearrange("a b -> (b a)").unsqueeze(0),
                        allow_slow_non_contiguous=True).then_inc(s_tr.h, 16)
        gg = s_tr.inc(16)

        @block.vector
        def _(v):
            v.wait_ge(s_tr.h, gg)
            g3 = gth[0:1, :].rearrange("p (r t) -> p t r", t=2)
            inst = v.tensor_reduce(smax[0:1, 0:nvals], g3[:, 0:nvals, :], AX.X,
                                   ALU.max)
            fence(v, s_v, inst)
            post_vector(v)
            v.engine_nop().then_inc(s_v.h, 1)
        return s_v.inc(1)

    def bcast(off, ncols, src_ap, wait_v):
        @block.tensor
        def _(t):
            t.wait_ge(s_v.h, wait_v)
            bank_wait(t, 6)
            t.matmul(ps[6][:, 0:ncols], ones_f[0:1, :], src_ap,
                     start=True, stop=True).then_inc(s_t.h, 1)
        tt = s_t.inc(1)

        @block.vector
        def _(v):
            v.wait_ge(s_t.h, tt)
            v.tensor_copy(bc[:, off:off + ncols], ps[6][:, 0:ncols]).then_inc(s_v.h, 1)
        vv = s_v.inc(1)
        bank_free[6] = ("v", vv)
        return vv

    @block.vector
    def _(v):
        for m in range(TCH):
            v.wait_ge(s_s.h, ln1_done[m])
            inst = v.tensor_reduce(am24[:, m:m + 1], x_tm[:, m, :], AX.X, ALU.max,
                                   apply_absolute_value=True)
        fence(v, s_v, inst)
        v.tensor_reduce(am1[:, 0:1], am24[:, 0:TCH], AX.X, ALU.max).then_inc(s_v.h, 1)
    v_amx = s_v.inc(1)

    def post_b0(v):
        inst = v.tensor_scalar(smax[0:1, 2:3], smax[0:1, 0:1], INV_Q16, 1e-12,
                               ALU.mult, ALU.max)
        fence(v, s_v, inst)
        v.reciprocal(smax[0:1, 3:4], smax[0:1, 2:3])

    pb0 = barrier(0, 1, am1[:, 0:1], post_b0, extra_vwait=v_amx)
    vb0 = bcast(0, 2, smax[0:1, 2:4], pb0)

    @block.vector
    def _(v):
        v.wait_ge(s_v.h, vb0)
        v.tensor_scalar(nmcol[:, 0:1], bc[:, 0:1], -MAGIC, None, ALU.mult)
        v.engine_nop().then_inc(s_v.h, 1)
    v_nm0 = s_v.inc(1)

    xq_tm = alloc("xq_tm", [128, TCH, D], BF16)
    xq_stage = alloc("xq_stage", [128, 2, D], F32)
    xq_done = {}

    @block.scalar
    def _(s):
        s.wait_ge(s_v.h, v_nm0)
        for m in range(TCH):
            b = m % 2
            inst = s.activation(out=xq_stage[:, b, :], in_=x_tm[:, m, :],
                                func=AF.Identity, bias=ccol[:, 0:1], scale=bc[:, 1:2])
            fence(s, s_s, inst)
            s.activation(out=xq_tm[:, m, :], in_=xq_stage[:, b, :], func=AF.Identity,
                         bias=nmcol[:, 0:1], scale=bc[:, 0:1]).then_inc(s_s.h, 1)
            s_s.inc(1)
            xq_done[m] = s_s.v

    xq_fm = xbf_fm

    _xq_seen = set()

    def _xq_idx_wait(t, c, f):
        if c not in _xq_seen:
            _xq_seen.add(c)
            t.wait_ge(s_s.h, xq_done[c])
    tr_xq = pe_transpose(xq_fm, xq_tm, (0, 1), idx_wait=_xq_idx_wait)

    @block.scalar
    def _(s):
        s.wait_ge(s_s.h, ln1_done[TCH - 1])
        s.dma_start(out=xln_h[:, :, :].transpose([1, 0, 2]),
                    in_=x_tm[:, :, :]).then_inc(s_xsp.h, 16)
    ld_xlnspill = s_xsp.inc(16)

    if _STOP_AFTER == "xq":
        def d():
            dump_tm("xq", xq_fm[:, :, :], FCH, T, BF16)
            o2 = dbg_out("scx", [128, 8], F32)

            @block.gpsimd
            def _(g):
                g.wait_ge(s_v.h, s_v.v)
                g.dma_start(out=o2[:, :], in_=bc[:, :]).then_inc(s_out.h, 16)
                g.wait_ge(s_out.h, s_out.inc(16))
        return finish(d)

    free("xq_stage")
    free("xq_tm")
    free("bias_sb")
    free("ones_col")
    free("ones_bf")
    free("wslotA")
    wupT = alloc("wupT", [128, FCH, II], BF16)
    x_flat = x_tm[:, :, :].rearrange("p a b -> p (a b)")
    f1_stage = [x_flat[:, 0:T], x_flat[:, T:2 * T]]
    wgtT = x_flat[:, 2 * T:8 * T].bitcast(BF16).rearrange("p (a b) -> p a b", b=II)

    t_attn_done = s_t.v
    s_xq_done = s_s.v

    pump_dq_tag("up")
    pump_dq_tag("gt")

    @block.sync
    def _(sync):
        sync.wait_ge(s_t.h, t_attn_done)
        sync.wait_ge(s_v.h, tr_xq)
        sync.wait_ge(s_s.h, s_xq_done)
        sync.wait_ge(s_dqst["up"].h, dq_last["up"])
        sync.dma_start(out=wupT[:, :, :],
                       in_=wupT_h[:, :, :].transpose([1, 0, 2])).then_inc(s_wup.h, 16)
        sync.wait_ge(s_dqst["gt"].h, dq_last["gt"])
        sync.dma_start(out=wgtT[:, :, :],
                       in_=wgtT_h[:, :, :].transpose([1, 0, 2])).then_inc(s_wgt.h, 16)
    ld_wup = s_wup.inc(16)
    ld_wgt = s_wgt.inc(16)

    def f1_pass(wT, ld_w, out_h, amdst, skey, prev_spill=None):
        t0, v0, s0 = s_t.v, s_v.v, s_s.v
        ftd = {}
        sp_done = {}

        def sdnv(m, n):
            return s0 + 4 * m + n + 1

        def vredv(m):
            return v0 + m + 1

        GM = 4
        for m0 in range(0, ICH, GM):
            @block.tensor
            def _(t, m0=m0):
                if m0 == 0:
                    t.wait_ge(ld_w[0].h, ld_w[1])
                    t.wait_ge(s_v.h, tr_xq)
                for m in range(m0, m0 + GM):
                    for n in range(4):
                        j = m * 4 + n
                        pi = j % 8
                        if j < 8:
                            bank_wait(t, pi)
                        else:
                            pj = j - 8
                            t.wait_ge(s_s.h, sdnv(pj // 4, pj % 4))
                        for kc in range(FCH):
                            mm = t.matmul(ps[pi][:, :],
                                          wT[:, kc, m * 128:(m + 1) * 128],
                                          xq_fm[:, kc, n * 512:(n + 1) * 512],
                                          start=(kc == 0), stop=(kc == FCH - 1))
                        mm.then_inc(s_t.h, 1)
                        s_t.inc(1)
                        ftd[(m, n)] = s_t.v

            @block.scalar
            def _(s, m0=m0):
                if m0 == 0:
                    s.wait_ge(s_xsp.h, ld_xlnspill)
                for m in range(m0, m0 + GM):
                    b = m % 2
                    if m >= 2:
                        s.wait_ge(s_f1sp[(skey, b)].h, sp_done[m - 2])
                    elif prev_spill is not None:
                        s.wait_ge(s_f1sp[(prev_spill[2], b)].h, prev_spill[b])
                    for n in range(4):
                        s.wait_ge(s_t.h, ftd[(m, n)])
                        s.activation(out=f1_stage[b][:, n * 512:(n + 1) * 512],
                                     in_=ps[(m * 4 + n) % 8][:, :],
                                     func=AF.Identity).then_inc(s_s.h, 1)
                        s_s.inc(1)
                    sp_done[m] = 16 * (m // 2 + 1)

            @block.vector
            def _(v, m0=m0):
                for m in range(m0, m0 + GM):
                    b = m % 2
                    v.wait_ge(s_s.h, sdnv(m, 3))
                    v.tensor_reduce(amdst[:, m:m + 1], f1_stage[b], AX.X,
                                    ALU.max, apply_absolute_value=True).then_inc(s_v.h, 1)
                    s_v.inc(1)

            @block.sync
            def _(sync, m0=m0):
                for m in range(m0, m0 + GM):
                    b = m % 2
                    sync.wait_ge(s_v.h, vredv(m))
                    sync.dma_start(out=out_h[m, :, :],
                                   in_=f1_stage[b]).then_inc(s_f1sp[(skey, b)].h, 16)
                    s_f1sp[(skey, b)].inc(16)
            pump_dq(4)

        for n in range(4):
            j = ICH * 4 - 4 + n
            bank_free[j % 8] = ("s", sdnv(ICH - 1, n))
        for n in range(4):
            j = ICH * 4 - 8 + n
            bank_free[j % 8] = ("s", sdnv(ICH - 2, n))
        return (s_f1sp[(skey, 0)].v, s_f1sp[(skey, 1)].v, skey)

    up_spilled = f1_pass(wupT, (s_wup, ld_wup), up_h, am24, 'u')
    gt_spilled = f1_pass(wgtT, (s_wgt, ld_wgt), gate_h, am24b, 'g',
                         prev_spill=up_spilled)
    t_f1_done = s_t.v

    @block.vector
    def _(v):
        inst = v.engine_nop()
        fence(v, s_v, inst)
        v.tensor_reduce(am1[:, 0:1], am24[:, 0:ICH], AX.X, ALU.max)
        v.tensor_reduce(am1[:, 1:2], am24b[:, 0:ICH], AX.X, ALU.max)
        v.engine_nop().then_inc(s_v.h, 1)
    v_amug = s_v.inc(1)

    def post_b1(v):
        v.tensor_scalar(smax[0:1, 2:3], smax[0:1, 0:1], INV_Q16, 1e-12, ALU.mult, ALU.max)
        inst = v.tensor_scalar(smax[0:1, 4:5], smax[0:1, 1:2], INV_Q16, 1e-12, ALU.mult, ALU.max)
        fence(v, s_v, inst)
        v.reciprocal(smax[0:1, 3:4], smax[0:1, 2:3])
        v.reciprocal(smax[0:1, 5:6], smax[0:1, 4:5])

    pb1 = barrier(1, 2, am1[:, 0:2], post_b1, extra_vwait=v_amug)
    vb1 = bcast(0, 4, smax[0:1, 2:6], pb1)

    @block.vector
    def _(v):
        v.wait_ge(s_v.h, vb1)
        v.tensor_scalar(nmcol[:, 1:2], bc[:, 2:3], -MAGIC, None, ALU.mult)
        v.engine_nop().then_inc(s_v.h, 1)
    v_nm1 = s_v.inc(1)

    if _STOP_AFTER == "f1":
        def d():
            o = dbg_out("scug", [128, 8], F32)

            @block.gpsimd
            def _(g):
                g.wait_ge(s_v.h, v_nm1)
                g.dma_start(out=o[:, :], in_=bc[:, :]).then_inc(s_out.h, 16)
                g.wait_ge(s_out.h, s_out.inc(16))
        return finish(d)

    g2_keep = {}
    for m in range(8):
        g2_keep[m] = x_flat[:, m * T:(m + 1) * T]
    wupT_f = wupT[:, :, :].rearrange("p a b -> p (a b)").bitcast(F32)
    for m in range(6):
        g2_keep[8 + m] = wupT_f[:, m * T:(m + 1) * T]
    xbf_f = xbf_fm[:, :, :].rearrange("p a b -> p (a b)").bitcast(F32)
    for m in range(4):
        g2_keep[14 + m] = xbf_f[:, m * T:(m + 1) * T]

    pump_dq_tag("dn")
    assert not (dq_qA or dq_qB or dq_qC), "dq pieces left unpumped"
    free("dq_tr"); free("dq_bf"); free("dq_si"); free("dq_s")
    free("dq_bm"); free("dq_in")
    fr1 = alloc("fr1", [128, 3 * T], F32)
    fr2 = alloc("fr2", [128, 2 * T], F32)
    fr4 = alloc("fr4", [128, T], F32)

    def p3(buf, m):
        b = m % 3
        return buf[:, b * T:(b + 1) * T]

    def p2(buf, m):
        b = m % 2
        return buf[:, b * T:(b + 1) * T]

    pslc = p3

    def f2a_dst(m):
        return g2_keep[m] if m < N_KEEP else p2(fr2, m)

    f2a_ld = {}
    fA, fS, fR = {}, {}, {}
    fSV1, fS2, fS1, fSP = {}, {}, {}, {}
    for m in range(ICH):
        if m >= 1:
            fA[m - 1] = s_v.inc(1)
        if m >= 2:
            fR[m - 2] = s_v.inc(1)
        if m >= 1:
            fS[m - 1] = s_v.inc(1)
        fSV1[m] = s_s.inc(1)
        if m >= 1:
            fS2[m - 1] = s_s.inc(1)
        fS1[m] = s_s.inc(1)
    fS2[ICH - 1] = s_s.inc(1)
    fA[ICH - 1] = s_v.inc(1)
    fR[ICH - 2] = s_v.inc(1)
    fS[ICH - 1] = s_v.inc(1)
    fR[ICH - 1] = s_v.inc(1)
    for m in range(N_KEEP, ICH):
        fSP[m] = s_g2st.inc(16)
    for m in range(ICH):
        f2a_ld[m] = s_gr[m % 3].inc(16)

    @block.sync
    def _(sync):
        for m in range(ICH):
            if m == 0:
                sync.wait_ge(s_f1sp[("g", 0)].h, gt_spilled[0])
                sync.wait_ge(s_f1sp[("g", 1)].h, gt_spilled[1])
            if m >= 3:
                sync.wait_ge(s_v.h, fS[m - 3])
            sync.dma_start(out=p3(fr1, m),
                           in_=gate_h[m, :, :]).then_inc(s_gr[m % 3].h, 16)

    @block.gpsimd
    def _(g):
        for m in range(N_KEEP, ICH):
            g.wait_ge(s_v.h, fS[m])
            g.dma_start(out=g2_h[m, :, :], in_=p2(fr2, m)).then_inc(s_g2st.h, 16)

    @block.scalar
    def _(s):
        for m in range(ICH):
            s.wait_ge(s_gr[m % 3].h, f2a_ld[m])
            if m == 0:
                s.wait_ge(s_v.h, v_nm1)
            s.activation(out=p3(fr1, m), in_=p3(fr1, m), func=AF.Identity,
                         bias=ccol[:, 0:1], scale=bc[:, 3:4]).then_inc(s_s.h, 1)
            if m >= 1:
                s.wait_ge(s_s.h, fS1[m - 1])
                s.activation(out=p2(fr2, m - 1), in_=p2(fr2, m - 1),
                             func=AF.Identity, bias=ccol[:, 0:1],
                             scale=ccol[:, 2:3]).then_inc(s_s.h, 1)
            s.wait_ge(s_s.h, fSV1[m])
            if m >= 2:
                s.wait_ge(s_v.h, fA[m - 2])
                if m - 2 >= N_KEEP:
                    s.wait_ge(s_v.h, fR[m - 2])
                    s.wait_ge(s_g2st.h, fSP[m - 2])
            s.activation(out=p2(fr2, m), in_=p3(fr1, m), func=AF.Sigmoid,
                         bias=nmcol[:, 1:2], scale=bc[:, 2:3]).then_inc(s_s.h, 1)
        s.wait_ge(s_s.h, fS1[ICH - 1])
        s.activation(out=p2(fr2, ICH - 1), in_=p2(fr2, ICH - 1),
                     func=AF.Identity, bias=ccol[:, 0:1],
                     scale=ccol[:, 2:3]).then_inc(s_s.h, 1)

    @block.vector
    def _(v):
        for m in range(ICH + 1):
            if m >= 1:
                v.wait_ge(s_s.h, fS2[m - 1])
                v.tensor_scalar(fr4[:, :], p2(fr2, m - 1), MAGIC, 65535.0,
                                ALU.subtract, ALU.min).then_inc(s_v.h, 1)
            if m >= 2:
                v.wait_ge(s_v.h, fS[m - 2])
                v.tensor_reduce(am24[:, m - 2:m - 1], f2a_dst(m - 2), AX.X,
                                ALU.max,
                                apply_absolute_value=True).then_inc(s_v.h, 1)
            if m >= 1:
                v.wait_ge(s_v.h, fA[m - 1])
                v.scalar_tensor_tensor(f2a_dst(m - 1), p3(fr1, m - 1), MAGIC,
                                       fr4[:, :], ALU.subtract,
                                       ALU.mult).then_inc(s_v.h, 1)
        v.wait_ge(s_v.h, fS[ICH - 1])
        v.tensor_reduce(am24[:, ICH - 1:ICH], f2a_dst(ICH - 1), AX.X, ALU.max,
                        apply_absolute_value=True).then_inc(s_v.h, 1)
        v.wait_ge(s_v.h, fR[ICH - 1])
        v.tensor_reduce(am1[:, 0:1], am24[:, 0:ICH], AX.X, ALU.max)
        v.engine_nop().then_inc(s_v.h, 1)
    v_am2 = s_v.inc(1)

    def post_b2(v):
        inst = v.tensor_scalar(smax[0:1, 6:7], smax[0:1, 4:5], INV_2P16, None, ALU.mult)
        fence(v, s_v, inst)
        inst = v.tensor_tensor(smax[0:1, 8:9], smax[0:1, 0:1], smax[0:1, 6:7], ALU.mult)
        fence(v, s_v, inst)
        inst = v.tensor_scalar(smax[0:1, 7:8], smax[0:1, 8:9], INV_Q16, 1e-12,
                               ALU.mult, ALU.max)
        fence(v, s_v, inst)
        inst = v.reciprocal(smax[0:1, 8:9], smax[0:1, 7:8])
        fence(v, s_v, inst)
        v.tensor_tensor(smax[0:1, 9:10], smax[0:1, 6:7], smax[0:1, 8:9], ALU.mult)

    pb2 = barrier(2, 1, am1[:, 0:1], post_b2, extra_vwait=v_am2)
    vb2 = bcast(4, 1, smax[0:1, 9:10], pb2)

    if _STOP_AFTER == "f2a":
        def d():
            o = dbg_out("scg2", [1, 12], F32)

            @block.gpsimd
            def _(g):
                g.wait_ge(s_v.h, s_v.v)
                g.dma_start(out=o[:, :], in_=smax[0:1, :]).then_inc(s_out.h, 16)
                g.wait_ge(s_out.h, s_out.inc(16))
        return finish(d)

    def f3_src(m):
        return g2_keep[m] if m < N_KEEP else p3(fr1, m)

    f3_ldg, f3_ldu = {}, {}
    gA, gS, gR = {}, {}, {}
    gSV, gSU, gSP = {}, {}, {}
    for m in range(ICH):
        if m >= 1:
            gA[m - 1] = s_v.inc(1)
        if m >= 2:
            gR[m - 2] = s_v.inc(1)
        if m >= 1:
            gS[m - 1] = s_v.inc(1)
        gSV[m] = s_s.inc(1)
        gSU[m] = s_s.inc(1)
    gA[ICH - 1] = s_v.inc(1)
    gR[ICH - 2] = s_v.inc(1)
    gS[ICH - 1] = s_v.inc(1)
    gR[ICH - 1] = s_v.inc(1)
    for m in range(N_KEEP, ICH):
        gSP[m] = s_opst.inc(16)
    for m in range(ICH):
        if m >= N_KEEP:
            f3_ldg[m] = s_g2r[m % 2].inc(16)
        f3_ldu[m] = s_ur[m % 2].inc(16)

    @block.sync
    def _(sync):
        for m in range(ICH):
            if m == 0:
                sync.wait_ge(s_f1sp[("u", 0)].h, up_spilled[0])
                sync.wait_ge(s_f1sp[("u", 1)].h, up_spilled[1])
            if m < 2:
                sync.wait_ge(s_v.h, fR[ICH - 2 + m])
                sync.wait_ge(s_g2st.h, fSP[ICH - 2 + m])
            else:
                sync.wait_ge(s_v.h, gA[m - 2])
            sync.dma_start(out=p2(fr2, m),
                           in_=up_h[m, :, :]).then_inc(s_ur[m % 2].h, 16)

    @block.gpsimd
    def _(g):
        for m in range(N_KEEP, ICH):
            if m - 3 >= N_KEEP:
                g.wait_ge(s_v.h, gS[m - 3])
                g.dma_start(out=op_h[m - 3, :, :],
                            in_=p3(fr1, m - 3)).then_inc(s_opst.h, 16)
            g.wait_ge(s_v.h, gS[m - 3])
            if m - 3 >= N_KEEP:
                g.wait_ge(s_v.h, gR[m - 3])
                g.wait_ge(s_opst.h, gSP[m - 3])
            g.wait_ge(s_g2st.h, fSP[m])
            g.dma_start(out=p3(fr1, m),
                        in_=g2_h[m, :, :]).then_inc(s_g2r[m % 2].h, 16)
        for m in range(ICH - 3, ICH):
            g.wait_ge(s_v.h, gS[m])
            g.dma_start(out=op_h[m, :, :],
                        in_=p3(fr1, m)).then_inc(s_opst.h, 16)

    @block.scalar
    def _(s):
        for m in range(ICH):
            if m == 0:
                s.wait_ge(s_v.h, vb2)
            if m >= N_KEEP:
                s.wait_ge(s_g2r[m % 2].h, f3_ldg[m])
            elif m >= 3:
                s.wait_ge(s_v.h, gS[m - 3])
            s.activation(out=p3(fr1, m), in_=f3_src(m), func=AF.Identity,
                         bias=ccol[:, 0:1], scale=bc[:, 4:5]).then_inc(s_s.h, 1)
            s.wait_ge(s_ur[m % 2].h, f3_ldu[m])
            s.activation(out=p2(fr2, m), in_=p2(fr2, m), func=AF.Identity,
                         bias=ccol[:, 0:1], scale=bc[:, 1:2]).then_inc(s_s.h, 1)

    @block.vector
    def _(v):
        for m in range(ICH + 1):
            if m >= 1:
                v.wait_ge(s_s.h, gSU[m - 1])
                v.tensor_scalar(fr4[:, :], p2(fr2, m - 1), MAGIC, None,
                                ALU.subtract).then_inc(s_v.h, 1)
            if m >= 2:
                v.wait_ge(s_v.h, gS[m - 2])
                v.tensor_reduce(am24b[:, m - 2:m - 1], f3_src(m - 2), AX.X,
                                ALU.max,
                                apply_absolute_value=True).then_inc(s_v.h, 1)
            if m >= 1:
                v.wait_ge(s_v.h, gA[m - 1])
                v.scalar_tensor_tensor(f3_src(m - 1), p3(fr1, m - 1), MAGIC,
                                       fr4[:, :], ALU.subtract,
                                       ALU.mult).then_inc(s_v.h, 1)
        v.wait_ge(s_v.h, gS[ICH - 1])
        v.tensor_reduce(am24b[:, ICH - 1:ICH], f3_src(ICH - 1), AX.X, ALU.max,
                        apply_absolute_value=True).then_inc(s_v.h, 1)
        v.wait_ge(s_v.h, gR[ICH - 1])
        v.tensor_reduce(am1[:, 0:1], am24b[:, 0:ICH], AX.X, ALU.max)
        v.engine_nop().then_inc(s_v.h, 1)
    v_am3 = s_v.inc(1)
    free("fr4")
    wdn_buf = alloc("wdn_buf", [128, 2, 2, D], BF16, side="right")

    def post_b3(v):
        inst = v.tensor_tensor(smax[0:1, 10:11], smax[0:1, 7:8], smax[0:1, 2:3], ALU.mult)
        fence(v, s_v, inst)
        inst = v.tensor_tensor(smax[0:1, 11:12], smax[0:1, 0:1], smax[0:1, 10:11], ALU.mult)
        fence(v, s_v, inst)
        inst = v.tensor_scalar(smax[0:1, 11:12], smax[0:1, 11:12], INV_Q16, 1e-12,
                               ALU.mult, ALU.max)
        fence(v, s_v, inst)
        inst = v.reciprocal(smax[0:1, 0:1], smax[0:1, 11:12])
        fence(v, s_v, inst)
        v.tensor_tensor(smax[0:1, 10:11], smax[0:1, 10:11], smax[0:1, 0:1], ALU.mult)

    pb3 = barrier(3, 1, am1[:, 0:1], post_b3, extra_vwait=v_am3)
    vb3 = bcast(5, 2, smax[0:1, 10:12], pb3)

    @block.vector
    def _(v):
        v.wait_ge(s_v.h, vb3)
        v.tensor_scalar(nmcol[:, 2:3], bc[:, 6:7], -MAGIC, None, ALU.mult)
        v.engine_nop().then_inc(s_v.h, 1)
    v_nm3 = s_v.inc(1)

    x_bf_flat = x_flat.bitcast(BF16)
    wupT_bfflat = wupT[:, :, :].rearrange("p a b -> p (a b)")
    obf = [x_bf_flat[:, k * T:(k + 1) * T] for k in range(16)]
    obf += [wupT_bfflat[:, k * T:(k + 1) * T] for k in range(8)]

    ob_done = {}
    ob_vo, ob_ldo = {}, {}
    for k in range(ICH):
        ob_vo[k] = s_v.inc(1)
        ob_done[k] = s_s.inc(1)
    for k in range(N_KEEP, ICH):
        ob_ldo[k] = s_opr[k % 2].inc(16)

    @block.gpsimd
    def _(g):
        for k in range(N_KEEP, ICH):
            g.wait_ge(s_s.h, ob_done[k - 3])
            if k - 3 >= N_KEEP:
                g.wait_ge(s_opr[(k - 3) % 2].h, ob_ldo[k - 3])
            g.wait_ge(s_opst.h, gSP[k])
            g.dma_start(out=p3(fr1, k),
                        in_=op_h[k, :, :]).then_inc(s_opr[k % 2].h, 16)

    @block.vector
    def _(v):
        for k in range(ICH):
            if k == 0:
                v.wait_ge(s_v.h, v_nm3)
                v.wait_ge(s_opst.h, gSP[ICH - 1])
            if k >= N_KEEP:
                v.wait_ge(s_opr[k % 2].h, ob_ldo[k])
            if k >= 3:
                v.wait_ge(s_s.h, ob_done[k - 3])
            v.tensor_scalar(p3(fr1, k), f3_src(k), bc[:, 5:6], MAGIC,
                            ALU.mult, ALU.add).then_inc(s_v.h, 1)

    @block.scalar
    def _(s):
        for k in range(ICH):
            s.wait_ge(s_v.h, ob_vo[k])
            s.activation(out=obf[k], in_=p3(fr1, k), func=AF.Identity,
                         bias=nmcol[:, 2:3], scale=bc[:, 6:7]).then_inc(s_s.h, 1)

    free("fr2")
    free("fr1")
    xlnrb = alloc("xlnrb", [128, 4, D], F32)
    y_stage = alloc("y_stage", [128, 4, D], F32)

    xln_rb = {}
    f3b_t = {}
    f3b_res = {}
    yout = {}
    wd_ld = {}
    wd_used = {}

    for w in range(4):
        @block.sync
        def _(sync, w=w):
            if w == 0:
                sync.wait_ge(s_xsp.h, 16)
                sync.wait_ge(s_s.h, ob_done[ICH - 1])
                sync.wait_ge(s_opr[0].h, ob_ldo[ICH - 2])
                sync.wait_ge(s_opr[1].h, ob_ldo[ICH - 1])
            else:
                sync.wait_ge(s_v.h, f3b_res[(w - 1, 3)])
            sync.dma_start(out=xlnrb[:, :, :],
                           in_=xln_h[w * 4:(w + 1) * 4, :, :].transpose([1, 0, 2])).then_inc(s_xl[w % 2].h, 16)
        xln_rb[w] = s_xl[w % 2].inc(16)

        for kg in range(12):
            gidx = w * 12 + kg

            @block.sync
            def _(sync, w=w, kg=kg, gidx=gidx):
                if gidx == 0:
                    sync.wait_ge(s_dqst["dn"].h, dq_last["dn"])
                    sync.wait_ge(s_t.h, t_f1_done)
                    sync.wait_ge(s_v.h, gS[ICH - 1])
                if gidx >= 2:
                    sync.wait_ge(s_t.h, wd_used[gidx - 2])
                sync.dma_start(out=wdn_buf[:, gidx % 2, :, :],
                               in_=wdnT_h[kg * 2:(kg + 1) * 2, :, :].transpose([1, 0, 2])).then_inc(s_wd[gidx % 2].h, 16)
            wd_ld[gidx] = s_wd[gidx % 2].inc(16)

            @block.tensor
            def _(t, w=w, kg=kg, gidx=gidx):
                if kg == 0:
                    for pi in range(8):
                        bank_wait(t, pi)
                t.wait_ge(s_wd[gidx % 2].h, wd_ld[gidx])
                for kk in range(2):
                    k = kg * 2 + kk
                    t.wait_ge(s_s.h, ob_done[k])
                    for m in range(4):
                        for n in range(2):
                            mm = t.matmul(ps[m * 2 + n][:, :],
                                          obf[k][:, (w * 4 + m) * 128:(w * 4 + m + 1) * 128],
                                          wdn_buf[:, gidx % 2, kk, n * 512:(n + 1) * 512],
                                          start=(k == 0), stop=(k == ICH - 1))
                mm.then_inc(s_t.h, 1)
            wd_used[gidx] = s_t.inc(1)
        f3b_t[w] = wd_used[w * 12 + 11]

        @block.vector
        def _(v, w=w):
            v.wait_ge(s_t.h, f3b_t[w])
            v.wait_ge(s_xl[w % 2].h, xln_rb[w])
            for m in range(4):
                if w >= 1:
                    v.wait_ge(s_tr.h, yout[(w - 1) * 4 + 3])
                for n in range(2):
                    inst = v.tensor_tensor(y_stage[:, m, n * 512:(n + 1) * 512],
                                           ps[m * 2 + n][:, :],
                                           xlnrb[:, m, n * 512:(n + 1) * 512],
                                           ALU.add)
                fence(v, s_v, inst)
                v.bn_stats(bn6[:, m, 0:6], y_stage[:, m, 0:512])
                inst = v.bn_stats(bn6[:, m, 6:12], y_stage[:, m, 512:1024])
                fence(v, s_v, inst)
                v.bn_aggr(mvw[:, m, :], bn6[:, m, 0:12])
                v.engine_nop().then_inc(s_v.h, 1)
                f3b_res[(w, m)] = s_v.inc(1)

        for pi in range(8):
            bank_free[pi] = ("v", f3b_res[(w, 3)])

        @block.scalar
        def _(s, w=w):
            s.wait_ge(s_v.h, f3b_res[(w, 3)])
            s.activation(out=sdw[:, :], in_=mvw[:, :, 1], func=AF.Sqrt,
                         bias=ccol[:, 1:2], scale=1.0).then_inc(s_s.h, 1)
        s_sdw = s_s.inc(1)

        @block.vector
        def _(v, w=w, s_sdw=s_sdw):
            v.wait_ge(s_s.h, s_sdw)
            inst = v.reciprocal(rstdw[:, :], sdw[:, :])
            fence(v, s_v, inst)
            v.scalar_tensor_tensor(nmrw[:, :], mvw[:, :, 0], -1.0, rstdw[:, :],
                                   ALU.mult, ALU.mult)
            v.engine_nop().then_inc(s_v.h, 1)
        v_rw = s_v.inc(1)

        for m in range(4):
            @block.scalar
            def _(s, w=w, m=m, v_rw=v_rw):
                if m == 0:
                    s.wait_ge(s_v.h, v_rw)
                b = m
                inst = s.activation(out=y_stage[:, b, :], in_=y_stage[:, b, :],
                                    func=AF.Identity, bias=nmrw[:, m:m + 1],
                                    scale=rstdw[:, m:m + 1])
                fence(s, s_s, inst)
                tc = w * 4 + m
                s.dma_start(out=y_d[tc * 128:(tc + 1) * 128, :],
                            in_=y_stage[:, b, :]).then_inc(s_tr.h, 16)
            yout[w * 4 + m] = s_tr.inc(16)

    @block.sync
    def _(sync):
        sync.wait_ge(s_tr.h, yout[15])

    return finish()


_CACHE = {}


def _get_program():
    key = (_STOP_AFTER, tuple(sorted(_DBG)))
    if key not in _CACHE:
        _CACHE[key] = build_program()
    return _CACHE[key]


def kernel(L0_attn_x, L0_attn_wg_w, L0_attn_wg_b, L0_attn_wq_w, L0_attn_wq_b,
           L0_attn_wk_w, L0_attn_wk_b, L0_attn_wv_w, L0_attn_wv_b,
           L0_attn_out_w, L0_attn_out_b, L0_ffn_w_gate, L0_ffn_w_up,
           L0_ffn_w_down):
    nc, _ = _get_program()
    x = np.ascontiguousarray(np.asarray(L0_attn_x, np.float32))
    com = {
        "wq": np.ascontiguousarray(L0_attn_wq_w, np.float32),
        "wk": np.ascontiguousarray(L0_attn_wk_w, np.float32),
        "wv": np.ascontiguousarray(L0_attn_wv_w, np.float32),
        "wg": np.ascontiguousarray(L0_attn_wg_w, np.float32),
        "wo": np.ascontiguousarray(L0_attn_out_w, np.float32),
        "bq": np.ascontiguousarray(L0_attn_wq_b, np.float32).reshape(1, D),
        "bk": np.ascontiguousarray(L0_attn_wk_b, np.float32).reshape(1, D),
        "bv": np.ascontiguousarray(L0_attn_wv_b, np.float32).reshape(1, D),
        "bg": np.ascontiguousarray(L0_attn_wg_b, np.float32).reshape(1, D),
        "bo": np.ascontiguousarray(L0_attn_out_b, np.float32).reshape(1, D),
        "wup": np.ascontiguousarray(L0_ffn_w_up, np.float32),
        "wgt": np.ascontiguousarray(L0_ffn_w_gate, np.float32),
        "wdn": np.ascontiguousarray(L0_ffn_w_down, np.float32),
        "ident": np.eye(128, dtype=np.float32),
    }
    in_maps = [dict(com, x=np.ascontiguousarray(x[b])) for b in range(8)]
    res = run_bass_kernel_spmd(nc, in_maps, core_ids=list(range(8)))
    return np.stack([res.results[b]["y"] for b in range(8)], axis=0)


# version 19
# speedup vs baseline: 1.0369x; 1.0028x over previous
import numpy as np
import concourse.bass as bass
import concourse.mybir as mybir
from concourse.bass_utils import run_bass_kernel_spmd

F32 = mybir.dt.float32
BF16 = mybir.dt.bfloat16
AF = mybir.ActivationFunctionType
ALU = mybir.AluOpType
AX = mybir.AxisListType

MAGIC = float(np.float32(1.5 * 2 ** 23))
INV_Q16 = float(np.float32(1.0 / 32767.0))
INV_QW = float(np.float32(1.0 / 7.0))
INV_2P16 = float(np.float32(1.0 / 65536.0))

T = 2048
D = 1024
TCH = 16
FCH = 8
II = 3072
ICH = 24
NPAIR = 8
N_KEEP = 18

_DBG = set()
_STOP_AFTER = None


class _Sem:
    def __init__(self, nc, name):
        self.h = nc.semaphore(name).__enter__()
        self.v = 0

    def inc(self, n=1):
        self.v += n
        return self.v


def build_program():
    nc = bass.Bass("TRN2", target_bir_lowering=False, debug=False,
                   detect_race_conditions=False)

    x_d = nc.dram_tensor("x", [T, D], F32, kind="ExternalInput").ap()
    ident_d = nc.dram_tensor("ident", [128, 128], F32, kind="ExternalInput").ap()
    w_d = {nm: nc.dram_tensor(nm, [D, D], F32, kind="ExternalInput").ap()
           for nm in ("wq", "wk", "wv", "wg", "wo")}
    b_d = {nm: nc.dram_tensor(nm, [1, D], F32, kind="ExternalInput").ap()
           for nm in ("bq", "bk", "bv", "bg", "bo")}
    wup_d = nc.dram_tensor("wup", [II, D], F32, kind="ExternalInput").ap()
    wgt_d = nc.dram_tensor("wgt", [II, D], F32, kind="ExternalInput").ap()
    wdn_d = nc.dram_tensor("wdn", [D, II], F32, kind="ExternalInput").ap()
    y_d = nc.dram_tensor("y", [T, D], F32, kind="ExternalOutput").ap()

    wupT_h = nc.dram_tensor("wupT_h", [FCH, 128, II], BF16).ap()
    wgtT_h = nc.dram_tensor("wgtT_h", [FCH, 128, II], BF16).ap()
    wdnT_h = nc.dram_tensor("wdnT_h", [ICH, 128, D], BF16).ap()
    up_h = nc.dram_tensor("up_h", [ICH, 128, T], F32).ap()
    gate_h = nc.dram_tensor("gate_h", [ICH, 128, T], F32).ap()
    g2_h = nc.dram_tensor("g2_h", [ICH, 128, T], F32).ap()
    op_h = nc.dram_tensor("op_h", [ICH, 128, T], F32).ap()
    xln_h = nc.dram_tensor("xln_h", [TCH, 128, D], F32).ap()
    cc_in = [nc.dram_tensor(f"cc_in{k}", [2, 8], F32).ap() for k in range(4)]
    cc_out = [nc.dram_tensor(f"cc_out{k}", [16, 8], F32).ap() for k in range(4)]

    dbg_d = {}

    def dbg_out(name, shape, dtype=F32):
        dbg_d[name] = nc.dram_tensor("dbg_" + name, shape, dtype,
                                     kind="ExternalOutput").ap()
        return dbg_d[name]

    live = {}

    def alloc(name, shape, dtype, side="left"):
        cm = nc.sbuf_tensor(name, shape, dtype, side=side)
        t = cm.__enter__()
        live[name] = cm
        return t

    def free(name):
        live.pop(name).__exit__(None, None, None)

    ident = alloc("identt", [128, 128], F32)
    ones_f = alloc("ones_f", [1, 128], F32)
    zbuf = alloc("zbuf", [128, 16], F32)
    zs32 = alloc("zs32", [128, 32], F32)
    ccol = alloc("ccol", [128, 3], F32)
    ident_bf = alloc("ident_bf", [128, 128], BF16)
    zrow = alloc("zrow", [2, 8], F32)
    zsum = alloc("zsum", [128, 16], F32)
    bn6 = alloc("bn6", [128, TCH, 12], F32)
    mvall = alloc("mvall", [128, TCH, 2], F32)
    sdall = alloc("sdall", [128, TCH], F32)
    rstd = alloc("rstd", [128, TCH], F32)
    nmr = alloc("nmr", [128, TCH], F32)
    am24 = alloc("am24", [128, ICH], F32)
    am24b = alloc("am24b", [128, ICH], F32)
    am1 = alloc("am1", [128, 2], F32)
    lmax = alloc("lmax", [2, 1], F32)
    gth = alloc("gth", [1, 16], F32)
    smax = alloc("smax", [1, 12], F32)
    bc = alloc("bc", [128, 8], F32)
    nmcol = alloc("nmcol", [128, 4], F32)
    mvw = alloc("mvw", [128, 4, 2], F32)
    sdw = alloc("sdw", [128, 4], F32)
    rstdw = alloc("rstdw", [128, 4], F32)
    nmrw = alloc("nmrw", [128, 4], F32)
    PW = 512
    DQNB = PW // 32
    dq_in = alloc("dq_in", [128, 4, PW], F32, side="right")
    dq_bm = alloc("dq_bm", [128, 4, DQNB], F32, side="right")
    dq_s = alloc("dq_s", [128, 4, DQNB], F32, side="right")
    dq_si = alloc("dq_si", [128, 4, DQNB], F32, side="right")
    dq_bf = alloc("dq_bf", [128, 4, PW], BF16, side="right")
    dq_tr = alloc("dq_tr", [128, 4, PW // 128, 128], BF16, side="right")
    wslotA = alloc("wslotA", [128, FCH, D], BF16, side="right")
    ones_bf = alloc("ones_bf", [1, T], BF16, side="right")
    ones_col = alloc("ones_col", [128, 1], BF16, side="right")
    bias_sb = alloc("bias_sb", [1, 5, D], BF16, side="right")
    kvaug = alloc("kvaug", [128, NPAIR, 128], BF16, side="right")
    zk = alloc("zk", [128, 32], BF16, side="right")
    fs_a = alloc("fs_a", [128, 1024], F32, side="right")
    fs_b = alloc("fs_b", [128, 1024], F32, side="right")
    wslotB = alloc("wslotB", [128, FCH, D], BF16, side="right")

    ps = [nc.psum_tensor(f"ps{i}", [128, 512], F32).__enter__() for i in range(8)]

    s_ld = _Sem(nc, "s_ld")
    s_tr = _Sem(nc, "s_tr")
    s_gd = _Sem(nc, "s_gd")
    s_v = _Sem(nc, "s_v")
    s_s = _Sem(nc, "s_s")
    s_t = _Sem(nc, "s_t")
    s_g = _Sem(nc, "s_g")
    s_cc = _Sem(nc, "s_cc")
    s_out = _Sem(nc, "s_out")
    s_id = _Sem(nc, "s_id")
    s_gx = _Sem(nc, "s_gx")
    s_gb = _Sem(nc, "s_gb")
    s_gwA = _Sem(nc, "s_gwA")
    s_gwB = _Sem(nc, "s_gwB")
    s_dqld = [_Sem(nc, f"s_dqld{i}") for i in range(4)]
    s_dqtr = _Sem(nc, "s_dqtr")
    s_dqtr2 = [_Sem(nc, "s_dqtrA"), _Sem(nc, "s_dqtrB")]
    s_dqst = {"up": _Sem(nc, "s_dqstU"), "gt": _Sem(nc, "s_dqstG"),
              "dn": _Sem(nc, "s_dqstD")}
    s_x2 = _Sem(nc, "s_x2")
    s_wup = _Sem(nc, "s_wup")
    s_wgt = _Sem(nc, "s_wgt")
    s_wdn = _Sem(nc, "s_wdn")
    s_f1sp = {("u", 0): _Sem(nc, "s_f1spU0"), ("u", 1): _Sem(nc, "s_f1spU1"),
              ("g", 0): _Sem(nc, "s_f1spG0"), ("g", 1): _Sem(nc, "s_f1spG1")}
    s_gr = [_Sem(nc, "s_gr0"), _Sem(nc, "s_gr1"), _Sem(nc, "s_gr2")]
    s_g2st = _Sem(nc, "s_g2st")
    s_ur = [_Sem(nc, "s_ur0"), _Sem(nc, "s_ur1"), _Sem(nc, "s_ur2")]
    s_g2r = [_Sem(nc, "s_g2r0"), _Sem(nc, "s_g2r1")]
    s_opst = _Sem(nc, "s_opst")
    s_opr = [_Sem(nc, "s_opr0"), _Sem(nc, "s_opr1")]
    s_xl = [_Sem(nc, "s_xl0"), _Sem(nc, "s_xl1")]
    s_wd = [_Sem(nc, "s_wd0"), _Sem(nc, "s_wd1")]
    s_dv = _Sem(nc, "s_dv")
    s_xsp = _Sem(nc, "s_xsp")
    all_dma_sems = ([s_ld, s_tr, s_gd, s_id, s_gx, s_gb, s_gwA, s_gwB, s_dqtr,
                     s_x2, s_wup, s_wgt, s_wdn, s_g2st, s_opst, s_xsp]
                    + s_dqld + list(s_dqst.values()) + list(s_f1sp.values())
                    + s_gr + s_ur + s_g2r + s_opr + s_xl + s_wd + s_dqtr2)

    semmap = {"v": s_v, "s": s_s, "g": s_g, "t": s_t, "ld": s_ld, "tr": s_tr}
    bank_free = [("v", 0)] * 8

    def bank_wait(eng, i):
        nm, val = bank_free[i]
        if val > 0:
            eng.wait_ge(semmap[nm].h, val)

    def fence(eng, sem, inst):
        inst.then_inc(sem.h, 1)
        eng.wait_ge(sem.h, sem.inc(1))

    def pe_transpose(dst_fm, src_tm, banks, pre_twait=None, idx_wait=None):
        base_t = s_t.v
        base_v = s_v.v
        nblk = TCH * FCH

        @block.tensor
        def _(t):
            if pre_twait is not None:
                pre_twait(t)
            for pb in banks:
                bank_wait(t, pb)
            for idx in range(nblk):
                c, f = divmod(idx, FCH)
                pb = banks[idx % 2]
                if idx_wait is not None:
                    idx_wait(t, c, f)
                if idx >= 2:
                    t.wait_ge(s_v.h, base_v + idx - 1)
                t.transpose(ps[pb][:, 0:64].bitcast(BF16),
                            src_tm[:, c, f * 128:(f + 1) * 128],
                            ident_bf[:, :]).then_inc(s_t.h, 1)
                s_t.inc(1)

        @block.vector
        def _(v):
            for idx in range(nblk):
                c, f = divmod(idx, FCH)
                pb = banks[idx % 2]
                v.wait_ge(s_t.h, base_t + idx + 1)
                v.tensor_copy(dst_fm[:, f, c * 128:(c + 1) * 128],
                              ps[pb][:, 0:64].bitcast(BF16)).then_inc(s_v.h, 1)
                s_v.inc(1)
        done = s_v.v
        for pb in banks:
            bank_free[pb] = ("v", done)
        return done

    blk_cm = nc.Block()
    block = blk_cm.__enter__()

    def finish(dump_emitter=None):
        if dump_emitter is not None:
            dump_emitter()
        blk_cm.__exit__(None, None, None)
        return nc, dbg_d

    def dump_tm(tag, buf_ap3, nch, w, dtype=F32):
        o = dbg_out(tag, [nch * 128, w], dtype)
        waits = [(s_v, s_v.v), (s_s, s_s.v), (s_t, s_t.v), (s_g, s_g.v)]
        waits += [(sm, sm.v) for sm in all_dma_sems]

        @block.gpsimd
        def _(g):
            for sm, val in waits:
                if val:
                    g.wait_ge(sm.h, val)
            g.dma_start(out=o.rearrange("(c p) w -> p c w", p=128),
                        in_=buf_ap3).then_inc(s_out.h, 16)
            g.wait_ge(s_out.h, s_out.inc(16))

    xbf_fm = alloc("xbf_fm", [128, FCH, T], BF16)
    xbf_tm = alloc("xbf_tm", [128, TCH, D], BF16, side="right")

    @block.sync
    def _(sync):
        sync.dma_start(out=ident[:], in_=ident_d[:, :]).then_inc(s_id.h, 16)
    ld_ident = s_id.inc(16)

    @block.gpsimd
    def _(g):
        g.dma_start(out=xbf_tm[:, :, :],
                    in_=x_d.rearrange("(t p) d -> p t d", p=128)).then_inc(s_gx.h, 16)
        for i, nm in enumerate(("bq", "bk", "bv", "bg", "bo")):
            g.dma_start(out=bias_sb[0:1, i, :], in_=b_d[nm][:, :]).then_inc(s_gb.h, 16)
    gd_x = s_gx.inc(16)
    gd_bias = s_gb.inc(16 * 5)

    wslot = [wslotA, wslotB]
    wname2slot = {"wk": 0, "wv": 1, "wg": 0, "wq": 1, "wo": 0}
    gd_w = {}
    w_guard = {}

    wsem = {0: s_gwA, 1: s_gwB}

    def emit_wload(nm):
        slot_sem = wsem[wname2slot[nm]]

        @block.gpsimd
        def _(g):
            gd = w_guard.get(nm)
            if gd is not None:
                g.wait_ge(semmap[gd[0]].h, gd[1])
            g.dma_start(out=wslot[wname2slot[nm]][:, :, :],
                        in_=w_d[nm].rearrange("(k p) d -> p k d", p=128)).then_inc(slot_sem.h, 16)
        gd_w[nm] = (slot_sem, slot_sem.inc(16))

    emit_wload("wk")
    emit_wload("wv")

    @block.vector
    def _(v):
        v.memset(ones_bf[:], 1.0)
        v.memset(ones_f[:], 1.0)
        v.memset(ones_col[:], 1.0)
        v.memset(zk[:], 0.0)
        v.memset(kvaug[:, :, :], 0.0)
        v.memset(ccol[:, 0:1], MAGIC)
        v.memset(ccol[:, 1:2], 1e-5)
        v.memset(ccol[:, 2:3], 65536.0)
        v.memset(zrow[:, :], 0.0)
        v.wait_ge(s_id.h, ld_ident)
        v.tensor_copy(ident_bf[:, :], ident[:, :])
        v.engine_nop().then_inc(s_v.h, 1)
    v_const = s_v.inc(1)

    @block.sync
    def _(sync):
        sync.wait_ge(s_v.h, v_const)
        for k in range(4):
            sync.dma_start(out=cc_in[k][:, :], in_=zrow[:, :]).then_inc(s_id.h, 16)
    ld_cc0 = s_id.inc(16 * 4)

    tr_xbf = pe_transpose(xbf_fm, xbf_tm, (0, 1),
                          pre_twait=lambda t: (t.wait_ge(s_gx.h, gd_x),
                                               t.wait_ge(s_v.h, v_const)))
    free("xbf_tm")

    if _STOP_AFTER == "xbf":
        return finish(lambda: dump_tm("xbf", xbf_fm[:, :, :], FCH, T, BF16))

    dq_state = {}
    dq_qA = []
    dq_qB = []
    dq_qC = []
    dq_last = {}
    dq_total = [0]

    def make_dq_pieces(tag, w_src, n_och, cl, out_h):
        npiece = cl // PW
        ntr = PW // 128

        def stageA(c, pc, i):
            b = i % 4

            @block.sync
            def _(sync, c=c, pc=pc, b=b, i=i):
                if i >= 4:
                    sync.wait_ge(s_g.h, dq_state[("gdq", i - 4)])
                sync.dma_start(out=dq_in[:, b, :],
                               in_=w_src[c * 128:(c + 1) * 128,
                                         pc * PW:(pc + 1) * PW]).then_inc(s_dqld[b].h, 16)
                dq_state[("ld", i)] = s_dqld[b].inc(16)

        def stageB(c, pc, i):
            b = i % 4

            @block.vector
            def _(v, b=b, i=i):
                v.wait_ge(s_dqld[b].h, dq_state[("ld", i)])
                v.tensor_reduce(dq_bm[:, b, :],
                                dq_in[:, b, :].rearrange("p (n q) -> p n q", q=32),
                                AX.X, ALU.max,
                                apply_absolute_value=True).then_inc(s_dv.h, 1)
                if i >= 1:
                    b1 = (i - 1) % 4
                    v.wait_ge(s_dv.h, dq_state[("red", i - 1)])
                    v.tensor_scalar(dq_s[:, b1, :], dq_bm[:, b1, :], INV_QW,
                                    1e-12, ALU.mult,
                                    ALU.max).then_inc(s_dv.h, 1)
                if i >= 2:
                    b2 = (i - 2) % 4
                    v.wait_ge(s_dv.h, dq_state[("tsd", i - 2)])
                    v.reciprocal(dq_si[:, b2, :], dq_s[:, b2, :])
                    v.engine_nop().then_inc(s_dv.h, 1)
            dq_state[("red", i)] = s_dv.inc(1)
            if i >= 1:
                dq_state[("tsd", i - 1)] = s_dv.inc(1)
            if i >= 2:
                dq_state[("vsc", i - 2)] = s_dv.inc(1)

            if i >= 2:
                j = i - 2

                @block.gpsimd
                def _(g, j=j):
                    g.wait_ge(s_dv.h, dq_state[("vsc", j)])
                    if j >= 4:
                        g.wait_ge(s_dqtr2[j % 2].h, dq_state[("tr", j - 4)])
                    bj = j % 4
                    sib = dq_si[:, bj, :].unsqueeze(2).broadcast_to([128, DQNB, 32])
                    w3 = dq_in[:, bj, :].rearrange("p (n q) -> p n q", q=32)
                    g.tensor_tensor(w3, w3, sib, ALU.mult).then_inc(s_g.h, 1)
                dq_state[("gmul", j)] = s_g.inc(1)

        def stageC(c, pc, i):
            b = i % 4

            @block.vector
            def _(v, b=b, i=i):
                v.wait_ge(s_g.h, dq_state[("gmul", i)])
                v.tensor_scalar(dq_in[:, b, :], dq_in[:, b, :], MAGIC, MAGIC,
                                ALU.add, ALU.subtract).then_inc(s_dv.h, 1)
            dq_state[("vrnd", i)] = s_dv.inc(1)

            @block.gpsimd
            def _(g, b=b, i=i):
                g.wait_ge(s_dv.h, dq_state[("vrnd", i)])
                sb_ = dq_s[:, b, :].unsqueeze(2).broadcast_to([128, DQNB, 32])
                w3 = dq_in[:, b, :].rearrange("p (n q) -> p n q", q=32)
                bf3 = dq_bf[:, b, :].rearrange("p (n q) -> p n q", q=32)
                g.tensor_tensor(bf3, w3, sb_, ALU.mult)
                g.engine_nop().then_inc(s_g.h, 1)
            dq_state[("gdq", i)] = s_g.inc(1)

            if b % 2 == 0:
                @block.sync
                def _(q, b=b, i=i):
                    q.wait_ge(s_g.h, dq_state[("gdq", i)])
                    for f in range(ntr):
                        q.dma_start_transpose(
                            dq_tr[:, b, f, :],
                            dq_bf[:, b, f * 128:(f + 1) * 128]).then_inc(s_dqtr2[b % 2].h, 16)
            else:
                @block.sync
                def _(q, b=b, i=i):
                    q.wait_ge(s_g.h, dq_state[("gdq", i)])
                    for f in range(ntr):
                        q.dma_start_transpose(
                            dq_tr[:, b, f, :],
                            dq_bf[:, b, f * 128:(f + 1) * 128]).then_inc(s_dqtr2[b % 2].h, 16)
            dq_state[("tr", i)] = s_dqtr2[b % 2].inc(16 * ntr)

            @block.sync
            def _(sync, c=c, pc=pc, b=b, i=i):
                sync.wait_ge(s_dqtr2[b % 2].h, dq_state[("tr", i)])
                sync.dma_start(
                    out=out_h[pc * ntr:(pc + 1) * ntr, :,
                              c * 128:(c + 1) * 128].transpose([1, 0, 2]),
                    in_=dq_tr[:, b, 0:ntr, :]).then_inc(s_dqst[tag].h, 16)
            dq_last[tag] = s_dqst[tag].inc(16)

        for c in range(n_och):
            for pc in range(npiece):
                i = dq_total[0]
                dq_qA.append((tag, stageA, c, pc, i))
                dq_qB.append((tag, stageB, c, pc, i))
                dq_qC.append((tag, stageC, c, pc, i))
                dq_total[0] += 1

    def drain_B():
        N = dq_total[0]
        for vi in (N, N + 1):
            @block.vector
            def _(v, vi=vi, N=N):
                if vi - 1 < N:
                    b1 = (vi - 1) % 4
                    v.wait_ge(s_dv.h, dq_state[("red", vi - 1)])
                    v.tensor_scalar(dq_s[:, b1, :], dq_bm[:, b1, :], INV_QW,
                                    1e-12, ALU.mult,
                                    ALU.max).then_inc(s_dv.h, 1)
                b2 = (vi - 2) % 4
                v.wait_ge(s_dv.h, dq_state[("tsd", vi - 2)])
                v.reciprocal(dq_si[:, b2, :], dq_s[:, b2, :])
                v.engine_nop().then_inc(s_dv.h, 1)
            if vi - 1 < N:
                dq_state[("tsd", vi - 1)] = s_dv.inc(1)
            dq_state[("vsc", vi - 2)] = s_dv.inc(1)

            j = vi - 2

            @block.gpsimd
            def _(g, j=j):
                g.wait_ge(s_dv.h, dq_state[("vsc", j)])
                if j >= 4:
                    g.wait_ge(s_dqtr2[j % 2].h, dq_state[("tr", j - 4)])
                bj = j % 4
                sib = dq_si[:, bj, :].unsqueeze(2).broadcast_to([128, DQNB, 32])
                w3 = dq_in[:, bj, :].rearrange("p (n q) -> p n q", q=32)
                g.tensor_tensor(w3, w3, sib, ALU.mult).then_inc(s_g.h, 1)
            dq_state[("gmul", j)] = s_g.inc(1)

    dq_drained = [False]

    def pump_dq(k):
        for _ in range(k):
            total0 = dq_total[0]
            cA = total0 - len(dq_qA)
            cB = total0 - len(dq_qB)
            cC = total0 - len(dq_qC)
            if dq_qA and cA <= cC + 3:
                tg, fn, c, pc, i = dq_qA.pop(0)
                fn(c, pc, i)
                cA += 1
            if dq_qB and cB < cA and cB <= cC + 3:
                tg, fn, c, pc, i = dq_qB.pop(0)
                fn(c, pc, i)
                cB += 1
                if not dq_qB and not dq_drained[0]:
                    drain_B()
                    dq_drained[0] = True
            if dq_qC and (cC <= cB - 3 or (dq_drained[0] and cC < cB)):
                tg, fn, c, pc, i = dq_qC.pop(0)
                fn(c, pc, i)

    def pump_dq_tag(tag):
        while ((dq_qA and dq_qA[0][0] == tag) or (dq_qB and dq_qB[0][0] == tag)
               or (dq_qC and dq_qC[0][0] == tag)):
            pump_dq(1)

    k_tm = alloc("k_tm", [128, TCH, D], BF16)
    v_tm = alloc("v_tm", [128, TCH, D], BF16)
    tdone = {}

    def run_tm_proj(wname, bidx, nbank, kind, dst):
        bi = "qkvgo".index(wname[1])
        slot = wslot[wname2slot[wname]]
        t0, v0, s0 = s_t.v, s_v.v, s_s.v
        ntile = TCH * 2

        def tdn(j):
            return t0 + j + 1

        if kind == "fm":
            def sexp(j): return s0 + 2 * j + 2
            def stt(j): return v0 + j + 1
            rel = ("v", stt)
        else:
            def scp(j): return s0 + j + 1
            rel = ("s", scp)

        GT = 8
        for g0 in range(0, ntile, GT):
            @block.tensor
            def _(t, g0=g0):
                if g0 == 0:
                    t.wait_ge(s_v.h, tr_xbf)
                    t.wait_ge(gd_w[wname][0].h, gd_w[wname][1])
                    t.wait_ge(s_gb.h, gd_bias)
                    t.wait_ge(s_v.h, v_const)
                for j in range(g0, g0 + GT):
                    m, n = divmod(j, 2)
                    pi = bidx + j % nbank
                    if j < nbank:
                        bank_wait(t, pi)
                    else:
                        t.wait_ge(semmap[rel[0]].h, rel[1](j - nbank))
                    for kc in range(FCH):
                        t.matmul(ps[pi][:, :],
                                 xbf_fm[:, kc, m * 128:(m + 1) * 128],
                                 slot[:, kc, n * 512:(n + 1) * 512],
                                 start=(kc == 0), stop=False)
                    t.matmul(ps[pi][:, :],
                             ones_bf[0:1, 0:128],
                             bias_sb[0:1, bi, n * 512:(n + 1) * 512],
                             start=False, stop=True).then_inc(s_t.h, 1)
                    s_t.inc(1)
                    tdone[(wname, m, n)] = s_t.v

            if kind == "fm":
                @block.scalar
                def _(s, g0=g0):
                    for j in range(g0, g0 + GT):
                        if j >= 1:
                            jj = j - 1
                            s.wait_ge(s_s.h, s0 + 2 * jj + 1)
                            if jj >= 2:
                                s.wait_ge(s_v.h, v0 + jj - 1)
                            s.activation(out=fs_b[:, (jj % 2) * 512:(jj % 2) * 512 + 512],
                                         in_=fs_a[:, (jj % 2) * 512:(jj % 2) * 512 + 512],
                                         func=AF.Exp, scale=-1.0).then_inc(s_s.h, 1)
                            s_s.inc(1)
                        s.wait_ge(s_t.h, tdn(j))
                        s.activation(out=fs_a[:, (j % 2) * 512:(j % 2) * 512 + 512],
                                     in_=ps[bidx + j % nbank][:, :],
                                     func=AF.Relu, scale=-1.0).then_inc(s_s.h, 1)
                        s_s.inc(1)
                    if g0 + GT == ntile:
                        jj = ntile - 1
                        s.wait_ge(s_s.h, s0 + 2 * jj + 1)
                        s.wait_ge(s_v.h, v0 + jj - 1)
                        s.activation(out=fs_b[:, (jj % 2) * 512:(jj % 2) * 512 + 512],
                                     in_=fs_a[:, (jj % 2) * 512:(jj % 2) * 512 + 512],
                                     func=AF.Exp, scale=-1.0).then_inc(s_s.h, 1)
                        s_s.inc(1)

                @block.vector
                def _(v, g0=g0):
                    for j in range(g0, g0 + GT):
                        m, n = divmod(j, 2)
                        pi = bidx + j % nbank
                        v.wait_ge(s_s.h, sexp(j))
                        v.scalar_tensor_tensor(dst[:, m, n * 512:(n + 1) * 512],
                                               ps[pi][:, :], 0.0,
                                               fs_b[:, (j % 2) * 512:(j % 2) * 512 + 512],
                                               ALU.max, ALU.add).then_inc(s_v.h, 1)
                        s_v.inc(1)
            else:
                @block.scalar
                def _(s, g0=g0):
                    for j in range(g0, g0 + GT):
                        m, n = divmod(j, 2)
                        pi = bidx + j % nbank
                        s.wait_ge(s_t.h, tdn(j))
                        s.activation(out=dst[:, m, n * 512:(n + 1) * 512],
                                     in_=ps[pi][:, :], func=kind).then_inc(s_s.h, 1)
                        s_s.inc(1)
            pump_dq(4)

        for j in range(ntile - nbank, ntile):
            bank_free[bidx + j % nbank] = (rel[0], rel[1](j))

    make_dq_pieces("up", wup_d, ICH, D, wupT_h)
    make_dq_pieces("gt", wgt_d, ICH, D, wgtT_h)
    make_dq_pieces("dn", wdn_d, FCH, II, wdnT_h)

    run_tm_proj("wk", 0, 8, "fm", k_tm)
    w_guard["wg"] = ("t", tdone[("wk", TCH - 1, 1)])
    emit_wload("wg")
    run_tm_proj("wv", 0, 8, AF.Identity, v_tm)
    v_kevac = s_v.v
    s_vevac = s_s.v

    w_guard["wq"] = ("t", tdone[("wv", TCH - 1, 1)])
    emit_wload("wq")

    if _STOP_AFTER == "ktm":
        return finish(lambda: (dump_tm("ktm", k_tm[:, :, :], TCH, D, BF16),
                               dump_tm("vtm", v_tm[:, :, :], TCH, D, BF16)))

    @block.tensor
    def _(t):
        t.wait_ge(s_v.h, v_kevac)
        t.wait_ge(s_s.h, s_vevac)
        for p in range(NPAIR):
            bank_wait(t, p)
        for p in range(NPAIR):
            for c in range(TCH):
                t.matmul(ps[p][:, 0:128],
                         k_tm[:, c, p * 128:(p + 1) * 128],
                         v_tm[:, c, p * 128:(p + 1) * 128],
                         start=(c == 0), stop=(c == TCH - 1),
                         skip_group_check=True)
            for c in range(TCH):
                mm = t.matmul(ps[p][:, 128:129],
                              k_tm[:, c, p * 128:(p + 1) * 128],
                              ones_col[:, :],
                              start=(c == 0), stop=(c == TCH - 1),
                              skip_group_check=True)
        mm.then_inc(s_t.h, 1)
    t_kv = s_t.inc(1)

    @block.vector
    def _(v):
        v.wait_ge(s_t.h, t_kv)
        for p in range(NPAIR):
            v.tensor_copy(kvaug[0:64, p, 0:64], ps[p][0:64, 0:64])
            v.tensor_copy(kvaug[64:128, p, 64:128], ps[p][64:128, 64:128])
            v.tensor_copy(zk[0:64, 4 * p:4 * p + 1], ps[p][0:64, 128:129])
            last = v.tensor_copy(zk[64:128, 4 * p + 2:4 * p + 3], ps[p][64:128, 128:129])
        fence(v, s_v, last)
        for p in range(NPAIR):
            v.tensor_tensor(zk[0:64, 4 * p + 1:4 * p + 2], ps[p][0:64, 128:129],
                            zk[0:64, 4 * p:4 * p + 1], ALU.subtract)
            v.tensor_tensor(zk[64:128, 4 * p + 3:4 * p + 4], ps[p][64:128, 128:129],
                            zk[64:128, 4 * p + 2:4 * p + 3], ALU.subtract)
        v.engine_nop().then_inc(s_v.h, 1)
    v_kv = s_v.inc(1)
    for i in range(8):
        bank_free[i] = ("v", v_kv)

    free("v_tm")
    pump_dq(3)

    if _STOP_AFTER == "kv":
        def d():
            o = dbg_out("kvaug", [128, NPAIR * 128], BF16)
            o2 = dbg_out("zkd", [128, 32], BF16)

            @block.gpsimd
            def _(g):
                g.wait_ge(s_v.h, v_kv)
                g.dma_start(out=o[:, :], in_=kvaug[:, :, :]).then_inc(s_out.h, 16)
                g.dma_start(out=o2[:, :], in_=zk[:, :]).then_inc(s_out.h, 16)
                g.wait_ge(s_out.h, s_out.inc(32))
        return finish(d)

    gate_tm = alloc("gate_tm", [128, TCH, D], BF16)
    run_tm_proj("wg", 0, 8, AF.Sigmoid, gate_tm)

    w_guard["wo"] = ("t", tdone[("wg", TCH - 1, 1)])
    emit_wload("wo")

    q_fm = alloc("q_fm", [128, FCH, T], BF16)

    _qt0, _qv0, _qs0 = s_t.v, s_v.v, s_s.v

    for g0 in range(0, 32, 8):
        @block.tensor
        def _(t, g0=g0):
            if g0 == 0:
                t.wait_ge(gd_w['wq'][0].h, gd_w['wq'][1])
            for j in range(g0, g0 + 8):
                m, n = divmod(j, 4)
                pi = j % 8
                if j < 8:
                    bank_wait(t, pi)
                else:
                    t.wait_ge(s_v.h, _qv0 + (j - 8) + 1)
                for kc in range(FCH):
                    t.matmul(ps[pi][:, :],
                             wslot[1][:, kc, m * 128:(m + 1) * 128],
                             xbf_fm[:, kc, n * 512:(n + 1) * 512],
                             start=(kc == 0), stop=False)
                t.matmul(ps[pi][:, :],
                         bias_sb[0:1, 0, m * 128:(m + 1) * 128],
                         ones_bf[0:1, n * 512:(n + 1) * 512],
                         start=False, stop=True).then_inc(s_t.h, 1)
                s_t.inc(1)
                tdone[("q", m, n)] = s_t.v

        @block.scalar
        def _(s, g0=g0):
            for j in range(g0, g0 + 8):
                if j >= 1:
                    jj = j - 1
                    s.wait_ge(s_s.h, _qs0 + 2 * jj + 1)
                    if jj >= 2:
                        s.wait_ge(s_v.h, _qv0 + jj - 1)
                    s.activation(out=fs_b[:, (jj % 2) * 512:(jj % 2) * 512 + 512],
                                 in_=fs_a[:, (jj % 2) * 512:(jj % 2) * 512 + 512],
                                 func=AF.Exp, scale=-1.0).then_inc(s_s.h, 1)
                    s_s.inc(1)
                s.wait_ge(s_t.h, _qt0 + j + 1)
                s.activation(out=fs_a[:, (j % 2) * 512:(j % 2) * 512 + 512],
                             in_=ps[j % 8][:, :],
                             func=AF.Relu, scale=-1.0).then_inc(s_s.h, 1)
                s_s.inc(1)
            if g0 + 8 == 32:
                jj = 31
                s.wait_ge(s_s.h, _qs0 + 2 * jj + 1)
                s.wait_ge(s_v.h, _qv0 + jj - 1)
                s.activation(out=fs_b[:, (jj % 2) * 512:(jj % 2) * 512 + 512],
                             in_=fs_a[:, (jj % 2) * 512:(jj % 2) * 512 + 512],
                             func=AF.Exp, scale=-1.0).then_inc(s_s.h, 1)
                s_s.inc(1)

        @block.vector
        def _(v, g0=g0):
            for j in range(g0, g0 + 8):
                m, n = divmod(j, 4)
                pi = j % 8
                v.wait_ge(s_s.h, _qs0 + 2 * j + 2)
                v.scalar_tensor_tensor(q_fm[:, m, n * 512:(n + 1) * 512],
                                       ps[pi][:, :], 0.0,
                                       fs_b[:, (j % 2) * 512:(j % 2) * 512 + 512],
                                       ALU.max, ALU.add).then_inc(s_v.h, 1)
                s_v.inc(1)
        pump_dq(4)

    for j in range(24, 32):
        bank_free[j % 8] = ("v", _qv0 + j + 1)

    free("wslotB")

    if _STOP_AFTER == "qgate":
        return finish(lambda: (dump_tm("qfm", q_fm[:, :, :], FCH, T, BF16),
                               dump_tm("gatetm", gate_tm[:, :, :], TCH, D, BF16)))

    gm_tm = k_tm
    sc_t = {}
    gate_all = s_s.v
    q_all = s_v.v

    for c in range(TCH):
        pa, pb, pz = (0, 1, 2) if c % 2 == 0 else (3, 4, 5)

        @block.tensor
        def _(t, c=c, pa=pa, pb=pb, pz=pz):
            if c == 0:
                t.wait_ge(s_v.h, max(v_kv, q_all))
            for pi in (pa, pb, pz):
                bank_wait(t, pi)
            for p in range(NPAIR):
                dst = ps[pa] if p < 4 else ps[pb]
                t.matmul(dst[:, (p % 4) * 128:(p % 4) * 128 + 128],
                         q_fm[:, p, c * 128:(c + 1) * 128],
                         kvaug[:, p, :], start=True, stop=True,
                         skip_group_check=True)
                mm = t.matmul(ps[pz][:, p * 4:(p + 1) * 4],
                              q_fm[:, p, c * 128:(c + 1) * 128],
                              zk[:, p * 4:(p + 1) * 4], start=True, stop=True,
                              skip_group_check=True)
            mm.then_inc(s_t.h, 1)
        sc_t[c] = s_t.inc(1)

        @block.vector
        def _(v, c=c, pa=pa, pb=pb, pz=pz):
            v.wait_ge(s_t.h, sc_t[c])
            if c == 0:
                v.wait_ge(s_s.h, gate_all)
            inst = v.tensor_copy(zs32[:, :], ps[pz][:, 0:32])
            fence(v, s_v, inst)
            z3 = zs32[:, :].rearrange("p (h t) -> p h t", t=2)
            inst = v.scalar_tensor_tensor(zsum[:, :], z3[:, :, 0], 1e-6,
                                          z3[:, :, 1], ALU.add, ALU.add)
            fence(v, s_v, inst)
            inst = v.reciprocal(zbuf[:, :], zsum[:, :])
            fence(v, s_v, inst)
            for half, pp in ((0, pa), (1, pb)):
                zv = zbuf[:, half * 8:(half + 1) * 8].unsqueeze(2).broadcast_to([128, 8, 64])
                inst = v.tensor_tensor(fs_a[:, 0:512].rearrange("p (h e) -> p h e", e=64),
                                       ps[pp][:, :].rearrange("p (h e) -> p h e", e=64),
                                       zv, ALU.mult)
                fence(v, s_v, inst)
                v.tensor_tensor(gm_tm[:, c, half * 512:(half + 1) * 512],
                                fs_a[:, 0:512],
                                gate_tm[:, c, half * 512:(half + 1) * 512],
                                ALU.mult)
            v.engine_nop().then_inc(s_v.h, 1)
        val = s_v.inc(1)
        for pi in (pa, pb, pz):
            bank_free[pi] = ("v", val)
        if c % 2 == 1:
            pump_dq(2)
    v_gm_all = s_v.v

    if _STOP_AFTER == "gm":
        return finish(lambda: dump_tm("gmtm", gm_tm[:, :, :], TCH, D, BF16))

    gm_fm = xbf_fm
    tr_gm = pe_transpose(gm_fm, gm_tm, (6, 7))

    free("q_fm"); free("gate_tm"); free("k_tm")
    free("fs_b"); free("fs_a"); free("zk"); free("kvaug")
    x_tm = alloc("x_tm", [128, TCH, D], F32)

    @block.sync
    def _(sync):
        sync.wait_ge(s_v.h, tr_gm)
        sync.dma_start(out=x_tm[:, :, :],
                       in_=x_d.rearrange("(t p) d -> p t d", p=128)).then_inc(s_x2.h, 16)
    ld_x2 = s_x2.inc(16)

    o_t = {}
    _ot0, _ov0 = s_t.v, s_v.v

    for g0 in range(0, 32, 8):
        @block.tensor
        def _(t, g0=g0):
            if g0 == 0:
                t.wait_ge(s_v.h, tr_gm)
                t.wait_ge(gd_w['wo'][0].h, gd_w['wo'][1])
            for j in range(g0, g0 + 8):
                m, n = divmod(j, 2)
                pi = j % 8
                if j < 8:
                    bank_wait(t, pi)
                else:
                    t.wait_ge(s_v.h, _ov0 + (j - 8) // 2 + 1)
                for kc in range(FCH):
                    t.matmul(ps[pi][:, :],
                             gm_fm[:, kc, m * 128:(m + 1) * 128],
                             wslot[0][:, kc, n * 512:(n + 1) * 512],
                             start=(kc == 0), stop=False)
                t.matmul(ps[pi][:, :],
                         ones_bf[0:1, 0:128],
                         bias_sb[0:1, 4, n * 512:(n + 1) * 512],
                         start=False, stop=True).then_inc(s_t.h, 1)
                s_t.inc(1)
                o_t[(m, n)] = s_t.v
        pump_dq(2)

    res_last = {}

    @block.vector
    def _(v):
        v.wait_ge(s_x2.h, ld_x2)
        for m in range(TCH):
            for n in range(2):
                v.wait_ge(s_t.h, o_t[(m, n)])
                v.tensor_tensor(x_tm[:, m, n * 512:(n + 1) * 512],
                                ps[(m * 2 + n) % 8][:, :],
                                x_tm[:, m, n * 512:(n + 1) * 512], ALU.add)
            v.engine_nop().then_inc(s_v.h, 1)
            s_v.inc(1)
            res_last[m] = s_v.v
    for m in range(TCH - 4, TCH):
        for n in range(2):
            bank_free[(m * 2 + n) % 8] = ("v", res_last[m])

    last_vres = s_v.v

    @block.vector
    def _(v):
        v.wait_ge(s_v.h, last_vres)
        for m in range(TCH):
            v.bn_stats(bn6[:, m, 0:6], x_tm[:, m, 0:512])
            inst = v.bn_stats(bn6[:, m, 6:12], x_tm[:, m, 512:1024])
        fence(v, s_v, inst)
        for m in range(TCH):
            v.bn_aggr(mvall[:, m, :], bn6[:, m, 0:12])
        v.engine_nop().then_inc(s_v.h, 1)
    v_bn1 = s_v.inc(1)

    @block.scalar
    def _(s):
        s.wait_ge(s_v.h, v_bn1)
        s.activation(out=sdall[:, :], in_=mvall[:, :, 1], func=AF.Sqrt,
                     bias=ccol[:, 1:2], scale=1.0).then_inc(s_s.h, 1)
    s_sd1 = s_s.inc(1)

    @block.vector
    def _(v):
        v.wait_ge(s_s.h, s_sd1)
        inst = v.reciprocal(rstd[:, :], sdall[:, :])
        fence(v, s_v, inst)
        v.scalar_tensor_tensor(nmr[:, :], mvall[:, :, 0], -1.0, rstd[:, :],
                               ALU.mult, ALU.mult)
        v.engine_nop().then_inc(s_v.h, 1)
    v_r1 = s_v.inc(1)

    ln1_done = {}

    @block.scalar
    def _(s):
        s.wait_ge(s_v.h, v_r1)
        for m in range(TCH):
            s.activation(out=x_tm[:, m, :], in_=x_tm[:, m, :], func=AF.Identity,
                         bias=nmr[:, m:m + 1], scale=rstd[:, m:m + 1]).then_inc(s_s.h, 1)
            s_s.inc(1)
            ln1_done[m] = s_s.v

    if _STOP_AFTER == "ln1":
        return finish(lambda: dump_tm("xln", x_tm[:, :, :], TCH, D, F32))

    def barrier(bid, nvals, stat_ap, post_vector, extra_vwait=None):
        @block.tensor
        def _(t):
            t.wait_ge(s_id.h, ld_ident)
            if extra_vwait:
                t.wait_ge(s_v.h, extra_vwait)
            bank_wait(t, 7)
            t.transpose(ps[7][0:nvals, 0:128], stat_ap, ident[:]).then_inc(s_t.h, 1)
        tt = s_t.inc(1)

        @block.vector
        def _(v):
            v.wait_ge(s_t.h, tt)
            v.tensor_reduce(lmax[0:nvals, 0:1], ps[7][0:nvals, :], AX.X,
                            ALU.max).then_inc(s_v.h, 1)
        vv = s_v.inc(1)
        bank_free[7] = ("v", vv)

        @block.scalar
        def _(s):
            s.wait_ge(s_v.h, vv)
            s.wait_ge(s_id.h, ld_cc0)
            s.dma_start(out=cc_in[bid][0:nvals, 0:1], in_=lmax[0:nvals, 0:1],
                        allow_slow_non_contiguous=True).then_inc(s_tr.h, 16)
        dd = s_tr.inc(16)

        @block.gpsimd
        def _(g):
            g.wait_ge(s_tr.h, dd)
            g.collective_compute(
                "AllGather", ALU.bypass,
                replica_groups=[list(range(8))],
                ins=[cc_in[bid][:, :].opt()],
                outs=[cc_out[bid][:, :].opt()]).then_inc(s_cc.h, 1)
        cc = s_cc.inc(1)

        @block.scalar
        def _(s):
            s.wait_ge(s_cc.h, cc)
            s.dma_start(out=gth[0:1, 0:16],
                        in_=cc_out[bid][:, 0:1].rearrange("a b -> (b a)").unsqueeze(0),
                        allow_slow_non_contiguous=True).then_inc(s_tr.h, 16)
        gg = s_tr.inc(16)

        @block.vector
        def _(v):
            v.wait_ge(s_tr.h, gg)
            g3 = gth[0:1, :].rearrange("p (r t) -> p t r", t=2)
            inst = v.tensor_reduce(smax[0:1, 0:nvals], g3[:, 0:nvals, :], AX.X,
                                   ALU.max)
            fence(v, s_v, inst)
            post_vector(v)
            v.engine_nop().then_inc(s_v.h, 1)
        return s_v.inc(1)

    def bcast(off, ncols, src_ap, wait_v):
        @block.tensor
        def _(t):
            t.wait_ge(s_v.h, wait_v)
            bank_wait(t, 6)
            t.matmul(ps[6][:, 0:ncols], ones_f[0:1, :], src_ap,
                     start=True, stop=True).then_inc(s_t.h, 1)
        tt = s_t.inc(1)

        @block.vector
        def _(v):
            v.wait_ge(s_t.h, tt)
            v.tensor_copy(bc[:, off:off + ncols], ps[6][:, 0:ncols]).then_inc(s_v.h, 1)
        vv = s_v.inc(1)
        bank_free[6] = ("v", vv)
        return vv

    @block.vector
    def _(v):
        for m in range(TCH):
            v.wait_ge(s_s.h, ln1_done[m])
            inst = v.tensor_reduce(am24[:, m:m + 1], x_tm[:, m, :], AX.X, ALU.max,
                                   apply_absolute_value=True)
        fence(v, s_v, inst)
        v.tensor_reduce(am1[:, 0:1], am24[:, 0:TCH], AX.X, ALU.max).then_inc(s_v.h, 1)
    v_amx = s_v.inc(1)

    def post_b0(v):
        inst = v.tensor_scalar(smax[0:1, 2:3], smax[0:1, 0:1], INV_Q16, 1e-12,
                               ALU.mult, ALU.max)
        fence(v, s_v, inst)
        v.reciprocal(smax[0:1, 3:4], smax[0:1, 2:3])

    pb0 = barrier(0, 1, am1[:, 0:1], post_b0, extra_vwait=v_amx)
    vb0 = bcast(0, 2, smax[0:1, 2:4], pb0)

    @block.vector
    def _(v):
        v.wait_ge(s_v.h, vb0)
        v.tensor_scalar(nmcol[:, 0:1], bc[:, 0:1], -MAGIC, None, ALU.mult)
        v.engine_nop().then_inc(s_v.h, 1)
    v_nm0 = s_v.inc(1)

    xq_tm = alloc("xq_tm", [128, TCH, D], BF16)
    xq_stage = alloc("xq_stage", [128, 2, D], F32)
    xq_done = {}

    @block.scalar
    def _(s):
        s.wait_ge(s_v.h, v_nm0)
        for m in range(TCH):
            b = m % 2
            inst = s.activation(out=xq_stage[:, b, :], in_=x_tm[:, m, :],
                                func=AF.Identity, bias=ccol[:, 0:1], scale=bc[:, 1:2])
            fence(s, s_s, inst)
            s.activation(out=xq_tm[:, m, :], in_=xq_stage[:, b, :], func=AF.Identity,
                         bias=nmcol[:, 0:1], scale=bc[:, 0:1]).then_inc(s_s.h, 1)
            s_s.inc(1)
            xq_done[m] = s_s.v

    xq_fm = xbf_fm

    _xq_seen = set()

    def _xq_idx_wait(t, c, f):
        if c not in _xq_seen:
            _xq_seen.add(c)
            t.wait_ge(s_s.h, xq_done[c])
    tr_xq = pe_transpose(xq_fm, xq_tm, (0, 1), idx_wait=_xq_idx_wait)

    @block.scalar
    def _(s):
        s.wait_ge(s_s.h, ln1_done[TCH - 1])
        s.dma_start(out=xln_h[:, :, :].transpose([1, 0, 2]),
                    in_=x_tm[:, :, :]).then_inc(s_xsp.h, 16)
    ld_xlnspill = s_xsp.inc(16)

    if _STOP_AFTER == "xq":
        def d():
            dump_tm("xq", xq_fm[:, :, :], FCH, T, BF16)
            o2 = dbg_out("scx", [128, 8], F32)

            @block.gpsimd
            def _(g):
                g.wait_ge(s_v.h, s_v.v)
                g.dma_start(out=o2[:, :], in_=bc[:, :]).then_inc(s_out.h, 16)
                g.wait_ge(s_out.h, s_out.inc(16))
        return finish(d)

    free("xq_stage")
    free("xq_tm")
    free("bias_sb")
    free("ones_col")
    free("ones_bf")
    free("wslotA")
    wupT = alloc("wupT", [128, FCH, II], BF16)
    x_flat = x_tm[:, :, :].rearrange("p a b -> p (a b)")
    f1_stage = [x_flat[:, 0:T], x_flat[:, T:2 * T]]
    wgtT = x_flat[:, 2 * T:8 * T].bitcast(BF16).rearrange("p (a b) -> p a b", b=II)

    t_attn_done = s_t.v
    s_xq_done = s_s.v

    pump_dq_tag("up")
    pump_dq_tag("gt")

    @block.sync
    def _(sync):
        sync.wait_ge(s_t.h, t_attn_done)
        sync.wait_ge(s_v.h, tr_xq)
        sync.wait_ge(s_s.h, s_xq_done)
        sync.wait_ge(s_dqst["up"].h, dq_last["up"])
        sync.dma_start(out=wupT[:, :, :],
                       in_=wupT_h[:, :, :].transpose([1, 0, 2])).then_inc(s_wup.h, 16)
        sync.wait_ge(s_dqst["gt"].h, dq_last["gt"])
        sync.dma_start(out=wgtT[:, :, :],
                       in_=wgtT_h[:, :, :].transpose([1, 0, 2])).then_inc(s_wgt.h, 16)
    ld_wup = s_wup.inc(16)
    ld_wgt = s_wgt.inc(16)

    def f1_pass(wT, ld_w, out_h, amdst, skey, prev_spill=None):
        t0, v0, s0 = s_t.v, s_v.v, s_s.v
        ftd = {}
        sp_done = {}

        def sdnv(m, n):
            return s0 + 4 * m + n + 1

        def vredv(m):
            return v0 + m + 1

        GM = 4
        for m0 in range(0, ICH, GM):
            @block.tensor
            def _(t, m0=m0):
                if m0 == 0:
                    t.wait_ge(ld_w[0].h, ld_w[1])
                    t.wait_ge(s_v.h, tr_xq)
                for m in range(m0, m0 + GM):
                    for n in range(4):
                        j = m * 4 + n
                        pi = j % 8
                        if j < 8:
                            bank_wait(t, pi)
                        else:
                            pj = j - 8
                            t.wait_ge(s_s.h, sdnv(pj // 4, pj % 4))
                        for kc in range(FCH):
                            mm = t.matmul(ps[pi][:, :],
                                          wT[:, kc, m * 128:(m + 1) * 128],
                                          xq_fm[:, kc, n * 512:(n + 1) * 512],
                                          start=(kc == 0), stop=(kc == FCH - 1))
                        mm.then_inc(s_t.h, 1)
                        s_t.inc(1)
                        ftd[(m, n)] = s_t.v

            @block.scalar
            def _(s, m0=m0):
                if m0 == 0:
                    s.wait_ge(s_xsp.h, ld_xlnspill)
                for m in range(m0, m0 + GM):
                    b = m % 2
                    if m >= 2:
                        s.wait_ge(s_f1sp[(skey, b)].h, sp_done[m - 2])
                    elif prev_spill is not None:
                        s.wait_ge(s_f1sp[(prev_spill[2], b)].h, prev_spill[b])
                    for n in range(4):
                        s.wait_ge(s_t.h, ftd[(m, n)])
                        s.activation(out=f1_stage[b][:, n * 512:(n + 1) * 512],
                                     in_=ps[(m * 4 + n) % 8][:, :],
                                     func=AF.Identity).then_inc(s_s.h, 1)
                        s_s.inc(1)
                    sp_done[m] = 16 * (m // 2 + 1)

            @block.vector
            def _(v, m0=m0):
                for m in range(m0, m0 + GM):
                    b = m % 2
                    v.wait_ge(s_s.h, sdnv(m, 3))
                    v.tensor_reduce(amdst[:, m:m + 1], f1_stage[b], AX.X,
                                    ALU.max, apply_absolute_value=True).then_inc(s_v.h, 1)
                    s_v.inc(1)

            @block.sync
            def _(sync, m0=m0):
                for m in range(m0, m0 + GM):
                    b = m % 2
                    sync.wait_ge(s_v.h, vredv(m))
                    sync.dma_start(out=out_h[m, :, :],
                                   in_=f1_stage[b]).then_inc(s_f1sp[(skey, b)].h, 16)
                    s_f1sp[(skey, b)].inc(16)
            pump_dq(4)

        for n in range(4):
            j = ICH * 4 - 4 + n
            bank_free[j % 8] = ("s", sdnv(ICH - 1, n))
        for n in range(4):
            j = ICH * 4 - 8 + n
            bank_free[j % 8] = ("s", sdnv(ICH - 2, n))
        return (s_f1sp[(skey, 0)].v, s_f1sp[(skey, 1)].v, skey)

    up_spilled = f1_pass(wupT, (s_wup, ld_wup), up_h, am24, 'u')
    gt_spilled = f1_pass(wgtT, (s_wgt, ld_wgt), gate_h, am24b, 'g',
                         prev_spill=up_spilled)
    t_f1_done = s_t.v

    @block.vector
    def _(v):
        inst = v.engine_nop()
        fence(v, s_v, inst)
        v.tensor_reduce(am1[:, 0:1], am24[:, 0:ICH], AX.X, ALU.max)
        v.tensor_reduce(am1[:, 1:2], am24b[:, 0:ICH], AX.X, ALU.max)
        v.engine_nop().then_inc(s_v.h, 1)
    v_amug = s_v.inc(1)

    def post_b1(v):
        v.tensor_scalar(smax[0:1, 2:3], smax[0:1, 0:1], INV_Q16, 1e-12, ALU.mult, ALU.max)
        inst = v.tensor_scalar(smax[0:1, 4:5], smax[0:1, 1:2], INV_Q16, 1e-12, ALU.mult, ALU.max)
        fence(v, s_v, inst)
        v.reciprocal(smax[0:1, 3:4], smax[0:1, 2:3])
        v.reciprocal(smax[0:1, 5:6], smax[0:1, 4:5])

    pb1 = barrier(1, 2, am1[:, 0:2], post_b1, extra_vwait=v_amug)
    vb1 = bcast(0, 4, smax[0:1, 2:6], pb1)

    @block.vector
    def _(v):
        v.wait_ge(s_v.h, vb1)
        v.tensor_scalar(nmcol[:, 1:2], bc[:, 2:3], -MAGIC, None, ALU.mult)
        v.engine_nop().then_inc(s_v.h, 1)
    v_nm1 = s_v.inc(1)

    if _STOP_AFTER == "f1":
        def d():
            o = dbg_out("scug", [128, 8], F32)

            @block.gpsimd
            def _(g):
                g.wait_ge(s_v.h, v_nm1)
                g.dma_start(out=o[:, :], in_=bc[:, :]).then_inc(s_out.h, 16)
                g.wait_ge(s_out.h, s_out.inc(16))
        return finish(d)

    g2_keep = {}
    for m in range(8):
        g2_keep[m] = x_flat[:, m * T:(m + 1) * T]
    wupT_f = wupT[:, :, :].rearrange("p a b -> p (a b)").bitcast(F32)
    for m in range(6):
        g2_keep[8 + m] = wupT_f[:, m * T:(m + 1) * T]
    xbf_f = xbf_fm[:, :, :].rearrange("p a b -> p (a b)").bitcast(F32)
    for m in range(4):
        g2_keep[14 + m] = xbf_f[:, m * T:(m + 1) * T]

    pump_dq_tag("dn")
    assert not (dq_qA or dq_qB or dq_qC), "dq pieces left unpumped"
    free("dq_tr"); free("dq_bf"); free("dq_si"); free("dq_s")
    free("dq_bm"); free("dq_in")
    fr1 = alloc("fr1", [128, 3 * T], F32)
    fr2 = alloc("fr2", [128, 2 * T], F32)
    fr4 = alloc("fr4", [128, T], F32)

    def p3(buf, m):
        b = m % 3
        return buf[:, b * T:(b + 1) * T]

    def p2(buf, m):
        b = m % 2
        return buf[:, b * T:(b + 1) * T]

    pslc = p3

    def f2a_dst(m):
        return g2_keep[m] if m < N_KEEP else p2(fr2, m)

    f2a_ld = {}
    fA, fS, fR = {}, {}, {}
    fSV1, fS2, fS1, fSP = {}, {}, {}, {}
    for m in range(ICH):
        if m >= 1:
            fA[m - 1] = s_v.inc(1)
        if m >= 2:
            fR[m - 2] = s_v.inc(1)
        if m >= 1:
            fS[m - 1] = s_v.inc(1)
        fSV1[m] = s_s.inc(1)
        if m >= 1:
            fS2[m - 1] = s_s.inc(1)
        fS1[m] = s_s.inc(1)
    fS2[ICH - 1] = s_s.inc(1)
    fA[ICH - 1] = s_v.inc(1)
    fR[ICH - 2] = s_v.inc(1)
    fS[ICH - 1] = s_v.inc(1)
    fR[ICH - 1] = s_v.inc(1)
    for m in range(N_KEEP, ICH):
        fSP[m] = s_g2st.inc(16)
    for m in range(ICH):
        f2a_ld[m] = s_gr[m % 3].inc(16)

    @block.sync
    def _(sync):
        for m in range(ICH):
            if m == 0:
                sync.wait_ge(s_f1sp[("g", 0)].h, gt_spilled[0])
                sync.wait_ge(s_f1sp[("g", 1)].h, gt_spilled[1])
            if m >= 3:
                sync.wait_ge(s_v.h, fS[m - 3])
            sync.dma_start(out=p3(fr1, m),
                           in_=gate_h[m, :, :]).then_inc(s_gr[m % 3].h, 16)

    @block.gpsimd
    def _(g):
        for m in range(N_KEEP, ICH):
            g.wait_ge(s_v.h, fS[m])
            g.dma_start(out=g2_h[m, :, :], in_=p2(fr2, m)).then_inc(s_g2st.h, 16)

    @block.scalar
    def _(s):
        for m in range(ICH):
            s.wait_ge(s_gr[m % 3].h, f2a_ld[m])
            if m == 0:
                s.wait_ge(s_v.h, v_nm1)
            s.activation(out=p3(fr1, m), in_=p3(fr1, m), func=AF.Identity,
                         bias=ccol[:, 0:1], scale=bc[:, 3:4]).then_inc(s_s.h, 1)
            if m >= 1:
                s.wait_ge(s_s.h, fS1[m - 1])
                s.activation(out=p2(fr2, m - 1), in_=p2(fr2, m - 1),
                             func=AF.Identity, bias=ccol[:, 0:1],
                             scale=ccol[:, 2:3]).then_inc(s_s.h, 1)
            s.wait_ge(s_s.h, fSV1[m])
            if m >= 2:
                s.wait_ge(s_v.h, fA[m - 2])
                if m - 2 >= N_KEEP:
                    s.wait_ge(s_v.h, fR[m - 2])
                    s.wait_ge(s_g2st.h, fSP[m - 2])
            s.activation(out=p2(fr2, m), in_=p3(fr1, m), func=AF.Sigmoid,
                         bias=nmcol[:, 1:2], scale=bc[:, 2:3]).then_inc(s_s.h, 1)
        s.wait_ge(s_s.h, fS1[ICH - 1])
        s.activation(out=p2(fr2, ICH - 1), in_=p2(fr2, ICH - 1),
                     func=AF.Identity, bias=ccol[:, 0:1],
                     scale=ccol[:, 2:3]).then_inc(s_s.h, 1)

    @block.vector
    def _(v):
        for m in range(ICH + 1):
            if m >= 1:
                v.wait_ge(s_s.h, fS2[m - 1])
                v.tensor_scalar(fr4[:, :], p2(fr2, m - 1), MAGIC, 65535.0,
                                ALU.subtract, ALU.min).then_inc(s_v.h, 1)
            if m >= 2:
                v.wait_ge(s_v.h, fS[m - 2])
                v.tensor_reduce(am24[:, m - 2:m - 1], f2a_dst(m - 2), AX.X,
                                ALU.max,
                                apply_absolute_value=True).then_inc(s_v.h, 1)
            if m >= 1:
                v.wait_ge(s_v.h, fA[m - 1])
                v.scalar_tensor_tensor(f2a_dst(m - 1), p3(fr1, m - 1), MAGIC,
                                       fr4[:, :], ALU.subtract,
                                       ALU.mult).then_inc(s_v.h, 1)
        v.wait_ge(s_v.h, fS[ICH - 1])
        v.tensor_reduce(am24[:, ICH - 1:ICH], f2a_dst(ICH - 1), AX.X, ALU.max,
                        apply_absolute_value=True).then_inc(s_v.h, 1)
        v.wait_ge(s_v.h, fR[ICH - 1])
        v.tensor_reduce(am1[:, 0:1], am24[:, 0:ICH], AX.X, ALU.max)
        v.engine_nop().then_inc(s_v.h, 1)
    v_am2 = s_v.inc(1)

    def post_b2(v):
        inst = v.tensor_scalar(smax[0:1, 6:7], smax[0:1, 4:5], INV_2P16, None, ALU.mult)
        fence(v, s_v, inst)
        inst = v.tensor_tensor(smax[0:1, 8:9], smax[0:1, 0:1], smax[0:1, 6:7], ALU.mult)
        fence(v, s_v, inst)
        inst = v.tensor_scalar(smax[0:1, 7:8], smax[0:1, 8:9], INV_Q16, 1e-12,
                               ALU.mult, ALU.max)
        fence(v, s_v, inst)
        inst = v.reciprocal(smax[0:1, 8:9], smax[0:1, 7:8])
        fence(v, s_v, inst)
        v.tensor_tensor(smax[0:1, 9:10], smax[0:1, 6:7], smax[0:1, 8:9], ALU.mult)

    pb2 = barrier(2, 1, am1[:, 0:1], post_b2, extra_vwait=v_am2)
    vb2 = bcast(4, 1, smax[0:1, 9:10], pb2)

    if _STOP_AFTER == "f2a":
        def d():
            o = dbg_out("scg2", [1, 12], F32)

            @block.gpsimd
            def _(g):
                g.wait_ge(s_v.h, s_v.v)
                g.dma_start(out=o[:, :], in_=smax[0:1, :]).then_inc(s_out.h, 16)
                g.wait_ge(s_out.h, s_out.inc(16))
        return finish(d)

    def f3_src(m):
        return g2_keep[m] if m < N_KEEP else p3(fr1, m)

    f3_ldg, f3_ldu = {}, {}
    gA, gS, gR = {}, {}, {}
    gSV, gSU, gSP = {}, {}, {}
    for m in range(ICH):
        if m >= 1:
            gA[m - 1] = s_v.inc(1)
        if m >= 2:
            gR[m - 2] = s_v.inc(1)
        if m >= 1:
            gS[m - 1] = s_v.inc(1)
        gSV[m] = s_s.inc(1)
        gSU[m] = s_s.inc(1)
    gA[ICH - 1] = s_v.inc(1)
    gR[ICH - 2] = s_v.inc(1)
    gS[ICH - 1] = s_v.inc(1)
    gR[ICH - 1] = s_v.inc(1)
    for m in range(N_KEEP, ICH):
        gSP[m] = s_opst.inc(16)
    for m in range(ICH):
        if m >= N_KEEP:
            f3_ldg[m] = s_g2r[m % 2].inc(16)
        f3_ldu[m] = s_ur[m % 2].inc(16)

    @block.sync
    def _(sync):
        for m in range(ICH):
            if m == 0:
                sync.wait_ge(s_f1sp[("u", 0)].h, up_spilled[0])
                sync.wait_ge(s_f1sp[("u", 1)].h, up_spilled[1])
            if m < 2:
                sync.wait_ge(s_v.h, fR[ICH - 2 + m])
                sync.wait_ge(s_g2st.h, fSP[ICH - 2 + m])
            else:
                sync.wait_ge(s_v.h, gA[m - 2])
            sync.dma_start(out=p2(fr2, m),
                           in_=up_h[m, :, :]).then_inc(s_ur[m % 2].h, 16)

    @block.gpsimd
    def _(g):
        for m in range(N_KEEP, ICH):
            if m - 3 >= N_KEEP:
                g.wait_ge(s_v.h, gS[m - 3])
                g.dma_start(out=op_h[m - 3, :, :],
                            in_=p3(fr1, m - 3)).then_inc(s_opst.h, 16)
            g.wait_ge(s_v.h, gS[m - 3])
            if m - 3 >= N_KEEP:
                g.wait_ge(s_v.h, gR[m - 3])
                g.wait_ge(s_opst.h, gSP[m - 3])
            g.wait_ge(s_g2st.h, fSP[m])
            g.dma_start(out=p3(fr1, m),
                        in_=g2_h[m, :, :]).then_inc(s_g2r[m % 2].h, 16)
        for m in range(ICH - 3, ICH):
            g.wait_ge(s_v.h, gS[m])
            g.dma_start(out=op_h[m, :, :],
                        in_=p3(fr1, m)).then_inc(s_opst.h, 16)

    @block.scalar
    def _(s):
        for m in range(ICH):
            if m == 0:
                s.wait_ge(s_v.h, vb2)
            if m >= N_KEEP:
                s.wait_ge(s_g2r[m % 2].h, f3_ldg[m])
            elif m >= 3:
                s.wait_ge(s_v.h, gS[m - 3])
            s.activation(out=p3(fr1, m), in_=f3_src(m), func=AF.Identity,
                         bias=ccol[:, 0:1], scale=bc[:, 4:5]).then_inc(s_s.h, 1)
            s.wait_ge(s_ur[m % 2].h, f3_ldu[m])
            s.activation(out=p2(fr2, m), in_=p2(fr2, m), func=AF.Identity,
                         bias=ccol[:, 0:1], scale=bc[:, 1:2]).then_inc(s_s.h, 1)

    @block.vector
    def _(v):
        for m in range(ICH + 1):
            if m >= 1:
                v.wait_ge(s_s.h, gSU[m - 1])
                v.tensor_scalar(fr4[:, :], p2(fr2, m - 1), MAGIC, None,
                                ALU.subtract).then_inc(s_v.h, 1)
            if m >= 2:
                v.wait_ge(s_v.h, gS[m - 2])
                v.tensor_reduce(am24b[:, m - 2:m - 1], f3_src(m - 2), AX.X,
                                ALU.max,
                                apply_absolute_value=True).then_inc(s_v.h, 1)
            if m >= 1:
                v.wait_ge(s_v.h, gA[m - 1])
                v.scalar_tensor_tensor(f3_src(m - 1), p3(fr1, m - 1), MAGIC,
                                       fr4[:, :], ALU.subtract,
                                       ALU.mult).then_inc(s_v.h, 1)
        v.wait_ge(s_v.h, gS[ICH - 1])
        v.tensor_reduce(am24b[:, ICH - 1:ICH], f3_src(ICH - 1), AX.X, ALU.max,
                        apply_absolute_value=True).then_inc(s_v.h, 1)
        v.wait_ge(s_v.h, gR[ICH - 1])
        v.tensor_reduce(am1[:, 0:1], am24b[:, 0:ICH], AX.X, ALU.max)
        v.engine_nop().then_inc(s_v.h, 1)
    v_am3 = s_v.inc(1)
    free("fr4")
    wdn_buf = alloc("wdn_buf", [128, 2, 2, D], BF16, side="right")

    def post_b3(v):
        inst = v.tensor_tensor(smax[0:1, 10:11], smax[0:1, 7:8], smax[0:1, 2:3], ALU.mult)
        fence(v, s_v, inst)
        inst = v.tensor_tensor(smax[0:1, 11:12], smax[0:1, 0:1], smax[0:1, 10:11], ALU.mult)
        fence(v, s_v, inst)
        inst = v.tensor_scalar(smax[0:1, 11:12], smax[0:1, 11:12], INV_Q16, 1e-12,
                               ALU.mult, ALU.max)
        fence(v, s_v, inst)
        inst = v.reciprocal(smax[0:1, 0:1], smax[0:1, 11:12])
        fence(v, s_v, inst)
        v.tensor_tensor(smax[0:1, 10:11], smax[0:1, 10:11], smax[0:1, 0:1], ALU.mult)

    pb3 = barrier(3, 1, am1[:, 0:1], post_b3, extra_vwait=v_am3)
    vb3 = bcast(5, 2, smax[0:1, 10:12], pb3)

    @block.vector
    def _(v):
        v.wait_ge(s_v.h, vb3)
        v.tensor_scalar(nmcol[:, 2:3], bc[:, 6:7], -MAGIC, None, ALU.mult)
        v.engine_nop().then_inc(s_v.h, 1)
    v_nm3 = s_v.inc(1)

    x_bf_flat = x_flat.bitcast(BF16)
    wupT_bfflat = wupT[:, :, :].rearrange("p a b -> p (a b)")
    obf = [x_bf_flat[:, k * T:(k + 1) * T] for k in range(16)]
    obf += [wupT_bfflat[:, k * T:(k + 1) * T] for k in range(8)]

    ob_done = {}
    ob_vo, ob_ldo = {}, {}
    for k in range(ICH):
        ob_vo[k] = s_v.inc(1)
        ob_done[k] = s_s.inc(1)
    for k in range(N_KEEP, ICH):
        ob_ldo[k] = s_opr[k % 2].inc(16)

    @block.gpsimd
    def _(g):
        for k in range(N_KEEP, ICH):
            g.wait_ge(s_s.h, ob_done[k - 3])
            if k - 3 >= N_KEEP:
                g.wait_ge(s_opr[(k - 3) % 2].h, ob_ldo[k - 3])
            g.wait_ge(s_opst.h, gSP[k])
            g.dma_start(out=p3(fr1, k),
                        in_=op_h[k, :, :]).then_inc(s_opr[k % 2].h, 16)

    @block.vector
    def _(v):
        for k in range(ICH):
            if k == 0:
                v.wait_ge(s_v.h, v_nm3)
                v.wait_ge(s_opst.h, gSP[ICH - 1])
            if k >= N_KEEP:
                v.wait_ge(s_opr[k % 2].h, ob_ldo[k])
            if k >= 3:
                v.wait_ge(s_s.h, ob_done[k - 3])
            v.tensor_scalar(p3(fr1, k), f3_src(k), bc[:, 5:6], MAGIC,
                            ALU.mult, ALU.add).then_inc(s_v.h, 1)

    @block.scalar
    def _(s):
        for k in range(ICH):
            s.wait_ge(s_v.h, ob_vo[k])
            s.activation(out=obf[k], in_=p3(fr1, k), func=AF.Identity,
                         bias=nmcol[:, 2:3], scale=bc[:, 6:7]).then_inc(s_s.h, 1)

    free("fr2")
    free("fr1")
    xlnrb = alloc("xlnrb", [128, 4, D], F32)
    y_stage = alloc("y_stage", [128, 4, D], F32)

    xln_rb = {}
    f3b_t = {}
    f3b_res = {}
    yout = {}
    wd_ld = {}
    wd_used = {}

    for w in range(4):
        @block.sync
        def _(sync, w=w):
            if w == 0:
                sync.wait_ge(s_xsp.h, 16)
                sync.wait_ge(s_s.h, ob_done[ICH - 1])
                sync.wait_ge(s_opr[0].h, ob_ldo[ICH - 2])
                sync.wait_ge(s_opr[1].h, ob_ldo[ICH - 1])
            else:
                sync.wait_ge(s_v.h, f3b_res[(w - 1, 3)])
            sync.dma_start(out=xlnrb[:, :, :],
                           in_=xln_h[w * 4:(w + 1) * 4, :, :].transpose([1, 0, 2])).then_inc(s_xl[w % 2].h, 16)
        xln_rb[w] = s_xl[w % 2].inc(16)

        for kg in range(12):
            gidx = w * 12 + kg

            @block.sync
            def _(sync, w=w, kg=kg, gidx=gidx):
                if gidx == 0:
                    sync.wait_ge(s_dqst["dn"].h, dq_last["dn"])
                    sync.wait_ge(s_t.h, t_f1_done)
                    sync.wait_ge(s_v.h, gS[ICH - 1])
                if gidx >= 2:
                    sync.wait_ge(s_t.h, wd_used[gidx - 2])
                sync.dma_start(out=wdn_buf[:, gidx % 2, :, :],
                               in_=wdnT_h[kg * 2:(kg + 1) * 2, :, :].transpose([1, 0, 2])).then_inc(s_wd[gidx % 2].h, 16)
            wd_ld[gidx] = s_wd[gidx % 2].inc(16)

            @block.tensor
            def _(t, w=w, kg=kg, gidx=gidx):
                if kg == 0:
                    for pi in range(8):
                        bank_wait(t, pi)
                t.wait_ge(s_wd[gidx % 2].h, wd_ld[gidx])
                for kk in range(2):
                    k = kg * 2 + kk
                    t.wait_ge(s_s.h, ob_done[k])
                    for m in range(4):
                        for n in range(2):
                            mm = t.matmul(ps[m * 2 + n][:, :],
                                          obf[k][:, (w * 4 + m) * 128:(w * 4 + m + 1) * 128],
                                          wdn_buf[:, gidx % 2, kk, n * 512:(n + 1) * 512],
                                          start=(k == 0), stop=(k == ICH - 1))
                mm.then_inc(s_t.h, 1)
            wd_used[gidx] = s_t.inc(1)
        f3b_t[w] = wd_used[w * 12 + 11]

        @block.vector
        def _(v, w=w):
            v.wait_ge(s_t.h, f3b_t[w])
            v.wait_ge(s_xl[w % 2].h, xln_rb[w])
            for m in range(4):
                if w >= 1:
                    v.wait_ge(s_tr.h, yout[(w - 1) * 4 + 3])
                for n in range(2):
                    inst = v.tensor_tensor(y_stage[:, m, n * 512:(n + 1) * 512],
                                           ps[m * 2 + n][:, :],
                                           xlnrb[:, m, n * 512:(n + 1) * 512],
                                           ALU.add)
                fence(v, s_v, inst)
                v.bn_stats(bn6[:, m, 0:6], y_stage[:, m, 0:512])
                inst = v.bn_stats(bn6[:, m, 6:12], y_stage[:, m, 512:1024])
                fence(v, s_v, inst)
                v.bn_aggr(mvw[:, m, :], bn6[:, m, 0:12])
                v.engine_nop().then_inc(s_v.h, 1)
                f3b_res[(w, m)] = s_v.inc(1)

        for pi in range(8):
            bank_free[pi] = ("v", f3b_res[(w, 3)])

        @block.scalar
        def _(s, w=w):
            s.wait_ge(s_v.h, f3b_res[(w, 3)])
            s.activation(out=sdw[:, :], in_=mvw[:, :, 1], func=AF.Sqrt,
                         bias=ccol[:, 1:2], scale=1.0).then_inc(s_s.h, 1)
        s_sdw = s_s.inc(1)

        @block.vector
        def _(v, w=w, s_sdw=s_sdw):
            v.wait_ge(s_s.h, s_sdw)
            inst = v.reciprocal(rstdw[:, :], sdw[:, :])
            fence(v, s_v, inst)
            v.scalar_tensor_tensor(nmrw[:, :], mvw[:, :, 0], -1.0, rstdw[:, :],
                                   ALU.mult, ALU.mult)
            v.engine_nop().then_inc(s_v.h, 1)
        v_rw = s_v.inc(1)

        for m in range(4):
            @block.scalar
            def _(s, w=w, m=m, v_rw=v_rw):
                if m == 0:
                    s.wait_ge(s_v.h, v_rw)
                b = m
                inst = s.activation(out=y_stage[:, b, :], in_=y_stage[:, b, :],
                                    func=AF.Identity, bias=nmrw[:, m:m + 1],
                                    scale=rstdw[:, m:m + 1])
                fence(s, s_s, inst)
                tc = w * 4 + m
                s.dma_start(out=y_d[tc * 128:(tc + 1) * 128, :],
                            in_=y_stage[:, b, :]).then_inc(s_tr.h, 16)
            yout[w * 4 + m] = s_tr.inc(16)

    @block.sync
    def _(sync):
        sync.wait_ge(s_tr.h, yout[15])

    return finish()


_CACHE = {}


def _get_program():
    key = (_STOP_AFTER, tuple(sorted(_DBG)))
    if key not in _CACHE:
        _CACHE[key] = build_program()
    return _CACHE[key]


def kernel(L0_attn_x, L0_attn_wg_w, L0_attn_wg_b, L0_attn_wq_w, L0_attn_wq_b,
           L0_attn_wk_w, L0_attn_wk_b, L0_attn_wv_w, L0_attn_wv_b,
           L0_attn_out_w, L0_attn_out_b, L0_ffn_w_gate, L0_ffn_w_up,
           L0_ffn_w_down):
    nc, _ = _get_program()
    x = np.ascontiguousarray(np.asarray(L0_attn_x, np.float32))
    com = {
        "wq": np.ascontiguousarray(L0_attn_wq_w, np.float32),
        "wk": np.ascontiguousarray(L0_attn_wk_w, np.float32),
        "wv": np.ascontiguousarray(L0_attn_wv_w, np.float32),
        "wg": np.ascontiguousarray(L0_attn_wg_w, np.float32),
        "wo": np.ascontiguousarray(L0_attn_out_w, np.float32),
        "bq": np.ascontiguousarray(L0_attn_wq_b, np.float32).reshape(1, D),
        "bk": np.ascontiguousarray(L0_attn_wk_b, np.float32).reshape(1, D),
        "bv": np.ascontiguousarray(L0_attn_wv_b, np.float32).reshape(1, D),
        "bg": np.ascontiguousarray(L0_attn_wg_b, np.float32).reshape(1, D),
        "bo": np.ascontiguousarray(L0_attn_out_b, np.float32).reshape(1, D),
        "wup": np.ascontiguousarray(L0_ffn_w_up, np.float32),
        "wgt": np.ascontiguousarray(L0_ffn_w_gate, np.float32),
        "wdn": np.ascontiguousarray(L0_ffn_w_down, np.float32),
        "ident": np.eye(128, dtype=np.float32),
    }
    in_maps = [dict(com, x=np.ascontiguousarray(x[b])) for b in range(8)]
    res = run_bass_kernel_spmd(nc, in_maps, core_ids=list(range(8)))
    return np.stack([res.results[b]["y"] for b in range(8)], axis=0)
